# version 01
# baseline (speedup 1.0000x reference)
import numpy as np
import ml_dtypes
import concourse.bass as bass
import concourse.mybir as mybir
from concourse.bass_utils import run_bass_kernel_spmd

F32 = mybir.dt.float32
BF16 = mybir.dt.bfloat16
AF = mybir.ActivationFunctionType
ALU = mybir.AluOpType
AX = mybir.AxisListType

S, D, TT = 8192, 1024, 512
NTL = S // TT
NS = TT // 128
NCK = D // 128
LN_EPS = 1e-5
BF = ml_dtypes.bfloat16

ENGS = ("sync", "gpsimd", "tensor", "vector", "scalar")


class PB:

    def __init__(self, nc, tag="pb"):
        self.nc = nc
        self.tag = tag
        self.items = {e: [] for e in ENGS}
        self.esem = {e: nc.alloc_semaphore(name=f"{tag}_{e}")
                     for e in ("tensor", "vector", "scalar", "gpsimd")}
        self.ecnt = {e: 0 for e in self.esem}
        self.dsem = {}
        self.dcnt = {}
        self.water = {e: {} for e in ENGS}
        self.mem = {e: [] for e in ENGS}
        self.barrier = []

    def _wait(self, eng, tok):
        if tok is None:
            return
        kind, name, val = tok
        if kind == "d":
            val = self.dcnt[name]
        key = (kind, name)
        w = self.water[eng]
        if w.get(key, 0) >= val:
            return
        w[key] = val
        self.items[eng].append(("wait", key, val))

    def _pre(self, eng, waits):
        for t in self.barrier:
            if not (t[0] == "e" and t[1] == eng):
                self._wait(eng, t)
        for t in waits:
            self._wait(eng, t)

    @staticmethod
    def _box(ap):
        try:
            t = ap.tensor.name
            dims = list(ap.ap)
            off = ap.offset
            span = 1
            for (step, count) in dims[1:]:
                span += (count - 1) * step
            return (t, off, off + span)
        except Exception:
            return None

    _WR0 = {"tensor_copy", "tensor_tensor", "tensor_add", "tensor_sub",
            "tensor_mul", "tensor_max", "tensor_scalar", "tensor_single_scalar",
            "tensor_scalar_add", "tensor_scalar_sub", "tensor_scalar_mul",
            "tensor_scalar_max", "tensor_scalar_min", "tensor_relu",
            "tensor_reduce", "reduce_max", "reduce_sum", "scalar_tensor_tensor",
            "tensor_tensor_scan", "memset", "memzero", "activation", "square",
            "sqrt", "mul", "copy", "add", "reciprocal", "select", "max_index",
            "sign"}

    def _self_dep_wait(self, eng, method, args, kw):
        wset, rset = [], []
        known = method in self._WR0
        if known:
            for j, a in enumerate(args):
                b = self._box(a) if hasattr(a, "tensor") else None
                if b is None:
                    continue
                (wset if j == 0 else rset).append(b)
            for k, a in kw.items():
                if hasattr(a, "tensor"):
                    b = self._box(a)
                    if b is not None:
                        (wset if k == "accum_out" else rset).append(b)
        mem = self.mem[eng]
        floor = self.water[eng].get(("e", eng), 0)
        need = 0
        if not known:
            need = self.ecnt[eng]
        else:
            def conflict(entry, mine_w, mine_r):
                t, lo, hi, cnt, is_w = entry
                for (t2, lo2, hi2) in mine_w:
                    if t2 == t and lo < hi2 and lo2 < hi:
                        return True
                if is_w:
                    for (t2, lo2, hi2) in mine_r:
                        if t2 == t and lo < hi2 and lo2 < hi:
                            return True
                return False
            keep = []
            for entry in mem:
                if entry[3] <= floor:
                    continue
                if conflict(entry, wset, rset):
                    need = max(need, entry[3])
                keep.append(entry)
            self.mem[eng] = mem = keep
        if need > 0:
            self._wait(eng, ("e", eng, need))
        return wset, rset, known

    def op(self, eng, method, *args, waits=(), inc=True, **kw):
        assert eng in ("tensor", "vector", "scalar", "gpsimd")
        if eng != "tensor":
            wset, rset, known = self._self_dep_wait(eng, method, args, kw)
        self._pre(eng, waits)
        tok = None
        if inc or eng != "tensor":
            self.ecnt[eng] += 1
            tok = ("e", eng, self.ecnt[eng])
            if eng != "tensor":
                n = self.ecnt[eng]
                if known:
                    for (t, lo, hi) in wset:
                        self.mem[eng].append((t, lo, hi, n, True))
                    for (t, lo, hi) in rset:
                        self.mem[eng].append((t, lo, hi, n, False))
                else:
                    self.mem[eng].append(("*", 0, 1 << 60, n, True))
        self.items[eng].append(("op", method, args, kw, tok))
        return tok if tok is not None else ("e", eng, self.ecnt[eng])

    def dma(self, q, out, in_, waits=(), slot="main"):
        assert q in ("sync", "gpsimd")
        if slot not in self.dsem:
            self.dsem[slot] = self.nc.alloc_semaphore(name=f"{self.tag}_s_{slot}")
            self.dcnt[slot] = 0
        self._pre(q, waits)
        self.dcnt[slot] += 16
        tok = ("d", slot, self.dcnt[slot])
        self.items[q].append(("dma", out, in_, tok))
        return tok

    def wait_only(self, eng, toks):
        for t in toks:
            self._wait(eng, t)

    def tok(self, eng):
        return ("e", eng, self.ecnt[eng])

    def set_barrier(self):
        self.barrier = ([("e", e, c) for e, c in self.ecnt.items() if c > 0]
                        + [("d", s, c) for s, c in self.dcnt.items() if c > 0])

    def _sem(self, key):
        kind, name = key
        return self.esem[name] if kind == "e" else self.dsem[name]

    def replay(self):
        with self.nc.Block() as blk:
            for eng in ENGS:
                items = self.items[eng]

                def make(items):
                    def run(e):
                        for it in items:
                            if it[0] == "wait":
                                e.wait_ge(self._sem(it[1]), it[2])
                            elif it[0] == "op":
                                inst = getattr(e, it[1])(*it[2], **it[3])
                                if it[4] is not None:
                                    inst.then_inc(self.esem[it[4][1]], 1)
                            else:
                                _, out, in_, tok = it
                                e.dma_start(out=out, in_=in_).then_inc(
                                    self.dsem[tok[1]], 16)
                    return run
                getattr(blk, eng)(make(items))


def mm_group(pb, parts, waits=()):
    n = len(parts)
    tok = None
    for k, (out, lh, rh, st_, sp_) in enumerate(parts):
        tok = pb.op("tensor", "matmul", out, lh, rh, start=st_, stop=sp_,
                    waits=(waits if k == 0 else ()), inc=(k == n - 1))
    return tok


class Rot:

    def __init__(self, banks):
        self.banks = banks
        self.evac = [None] * len(banks)
        self.i = 0

    def next(self):
        b = self.i % len(self.banks)
        self.i += 1
        return self.banks[b], self.evac[b], b

    def done(self, b, tok):
        self.evac[b] = tok


class Arena:
    def __init__(self, handle, ncols):
        self.h, self.ncols, self.off = handle, ncols, 0

    def take(self, n):
        assert self.off + n <= self.ncols, f"arena overflow {self.off}+{n}>{self.ncols}"
        o = self.off
        self.off += n
        return o

    def view(self, off, n):
        return self.h[:, off:off + n]


ACT_CONSTS = [1e-5, -0.5, 1e-6, 1.0]


def alloc_common(nc, pb, io, bf_cols=74000, f32_cols=16000):
    ar = {}
    ar["bf"] = Arena(nc.alloc_sbuf_tensor("arena_bf", [128, bf_cols], BF16), bf_cols)
    ar["f32"] = Arena(nc.alloc_sbuf_tensor("arena_f32", [128, f32_cols], F32), f32_cols)
    ar["P"] = [nc.alloc_psum_tensor(f"pr{i}", [128, 512], F32) for i in range(4)]
    ar["PTR"] = nc.alloc_psum_tensor("ptr", [128, 1024], BF16)
    ar["PZ"] = nc.alloc_psum_tensor("pz", [128, 512], F32)
    ar["P67"] = nc.alloc_psum_tensor("p67", [128, 1024], F32)
    f32, bf = ar["f32"], ar["bf"]
    o_c = f32.take(len(ACT_CONSTS))
    for j, v in enumerate(ACT_CONSTS):
        nc.const_aps.register(F32, v, f32.view(o_c + j, 1))
        pb.op("gpsimd", "memset", f32.view(o_c + j, 1), v)
    o_ident = bf.take(128)
    ar["o_ident"] = o_ident
    ar["t_ident"] = pb.dma("gpsimd", bf.view(o_ident, 128), io["ident"][:, :],
                           slot="ident")
    ar["mark"] = (bf.off, f32.off)
    return ar


def arena_reset(ar):
    ar["bf"].off, ar["f32"].off = ar["mark"]


class LNKit:

    def __init__(self, pb, ar):
        self.pb, self.f32 = pb, ar["f32"]
        self.o_st = self.f32.take(32)
        self.o_sq = self.f32.take(D)

    def st1(self, c):
        return self.f32.view(self.o_st + c, 1)

    def st(self, c):
        return self.f32.view(self.o_st + c, 4)

    def m1(self, s):
        return self.f32.view(self.o_st + 8 + s, 1)

    def rstd1(self, s):
        return self.f32.view(self.o_st + 12 + s, 1)

    def ln(self, src_s, dst_s, extra_waits=(), dst2_s=None):
        pb = self.pb
        c_ssum, c_s2, c_m, c_rstd, c_var, c_tmp = 0, 4, 8, 12, 16, 20
        for s in range(NS):
            pb.op("scalar", "activation", self.f32.view(self.o_sq, D), src_s(s),
                  AF.Square, accum_out=self.st1(c_s2 + s),
                  waits=list(extra_waits))
        tsq = pb.tok("scalar")
        for s in range(NS):
            pb.op("vector", "reduce_sum", self.st1(c_ssum + s), src_s(s),
                  axis=AX.X, waits=list(extra_waits))
        pb.op("vector", "tensor_scalar_mul", self.st(c_m), self.st(c_ssum), 1.0 / D)
        pb.op("vector", "tensor_scalar_mul", self.st(c_var), self.st(c_s2), 1.0 / D,
              waits=[tsq])
        pb.op("vector", "tensor_mul", self.st(c_tmp), self.st(c_m), self.st(c_m))
        tvar = pb.op("vector", "tensor_sub", self.st(c_var), self.st(c_var),
                     self.st(c_tmp))
        pb.op("scalar", "activation", self.st(c_rstd), self.st(c_var), AF.Ln,
              bias=LN_EPS, waits=[tvar])
        trstd = pb.op("scalar", "activation", self.st(c_rstd), self.st(c_rstd),
                      AF.Exp, scale=-0.5)
        tok = None
        for s in range(NS):
            tok = pb.op("vector", "tensor_scalar", dst_s(s), src_s(s),
                        self.st1(c_m + s), self.st1(c_rstd + s),
                        ALU.subtract, ALU.mult, waits=[trstd])
        return tok


def transpose_tile(pb, ar, src_tm_off, dst_fm_off, waits, t_tr_last):
    bf, PTR = ar["bf"], ar["PTR"]
    fm3d = bf.view(dst_fm_off, NCK * TT).rearrange("p (c t) -> p c t", c=NCK)
    ptr3d = PTR[:, 0:1024].rearrange("p (c t) -> p c t", c=NCK)
    for s in range(NS):
        w0 = list(waits) + ([t_tr_last] if t_tr_last else []) + [ar["t_ident"]]
        ttr = None
        for ck in range(NCK):
            ttr = pb.op("tensor", "transpose", PTR[:, ck * 128:(ck + 1) * 128],
                        bf.view(src_tm_off + s * D + ck * 128, 128),
                        bf.view(ar["o_ident"], 128),
                        waits=(w0 if ck == 0 else ()), inc=(ck == NCK - 1))
        t_tr_last = pb.op("vector", "tensor_copy",
                          fm3d[:, :, s * 128:(s + 1) * 128], ptr3d, waits=[ttr])
    return t_tr_last


def tile_in(ap_hbm, i):
    return ap_hbm[i * TT:(i + 1) * TT, :].rearrange("(s p) f -> p s f", p=128)



def emit_l0a(pb, io, ar):
    P, PTR, PZ = ar["P"], ar["PTR"], ar["PZ"]
    bf, f32 = ar["bf"], ar["f32"]
    arena_reset(ar)

    o_wq = bf.take(NCK * D)
    o_wk = bf.take(NCK * D)
    o_wvT = bf.take(NCK * D)
    o_proj = bf.take(NCK * 256)
    o_w2q = bf.take(NCK * 256)
    o_w2k = bf.take(NCK * 256)
    o_ones = bf.take(8)
    o_xnfm = bf.take(NCK * TT)
    o_xntm = bf.take(NS * D)
    o_qp = bf.take(2 * S)
    o_kp = bf.take(NS * 256)
    o_vtm = bf.take(NS * D)
    o_kvb = bf.take(2 * D)
    o_eluB = bf.take(512)

    o_xt = f32.take(NS * D)
    o_attn = o_xt
    o_xt2 = f32.take(NS * D)
    o_eluA = f32.take(512)
    o_kvacc = f32.take(2 * D)
    o_xo = f32.take(NS * D)
    o_zi = f32.take(8)
    lnk = LNKit(pb, ar)

    tw = []
    for (off, key, pat) in ((o_wq, "wq", "(c p) d -> p c d"),
                            (o_wk, "wk", "(c p) d -> p c d"),
                            (o_wvT, "wvT", "(c p) d -> p c d"),
                            (o_proj, "proj", "(c p) q -> p c q")):
        n = NCK * (256 if key == "proj" else D)
        tw.append(pb.dma("gpsimd", bf.view(off, n),
                         io[key].rearrange(pat, p=128), slot="l0a_w"))
    t_ones = pb.op("gpsimd", "memset", bf.view(o_ones, 1), 1.0)
    tw.append(t_ones)

    rot = Rot(P)

    def group(parts_fn, width, evac_fn, extra=()):
        bank, banktok, bi = rot.next()
        w = list(extra) + ([banktok] if banktok else [])
        mmtok = mm_group(pb, parts_fn(bank), waits=w)
        evtok = evac_fn(bank, mmtok)
        rot.done(bi, evtok)
        return evtok

    def ev_copy_bf(dst_off, width):
        def f(bank, mmtok):
            return pb.op("vector", "tensor_copy", bf.view(dst_off, width),
                         bank[:, :width], waits=[mmtok])
        return f

    def ev_kv(dst_off, width, first):
        def f(bank, mmtok):
            if first:
                return pb.op("vector", "tensor_copy", f32.view(dst_off, width),
                             bank[:, :width], waits=[mmtok])
            return pb.op("vector", "tensor_add", f32.view(dst_off, width),
                         f32.view(dst_off, width), bank[:, :width], waits=[mmtok])
        return f

    def ev_elu(dst_off, width):
        def f(bank, mmtok):
            tmin = pb.op("vector", "tensor_scalar_min", f32.view(o_eluA, width),
                         bank[:, :width], 0.0, waits=[mmtok])
            texp = pb.op("scalar", "activation", bf.view(o_eluB, width),
                         f32.view(o_eluA, width), AF.Exp, waits=[tmin])
            pb.op("vector", "tensor_scalar_max", bf.view(dst_off, width),
                  bank[:, :width], 0.0)
            return pb.op("vector", "tensor_add", bf.view(dst_off, width),
                         bf.view(dst_off, width), bf.view(o_eluB, width),
                         waits=[texp])
        return f

    for (o_w, o_dst) in ((o_wq, o_w2q), (o_wk, o_w2k)):
        for db in range(NCK):
            group(lambda bank, o_w=o_w, db=db: [
                (bank[:, :256], bf.view(o_w + ck * D + db * 128, 128),
                 bf.view(o_proj + ck * 256, 256), ck == 0, ck == NCK - 1)
                for ck in range(NCK)],
                256, ev_copy_bf(o_dst + db * 256, 256), extra=tw)

    t_groups_done = None
    t_apply_prev = None
    t_tr_last = None
    for i in range(NTL):
        lw = [t_apply_prev] if t_apply_prev else []
        tl = pb.dma("sync", f32.view(o_xt, NS * D), tile_in(io["x_in"], i),
                    waits=lw, slot="l0a_xt")
        t_apply = lnk.ln(lambda s: f32.view(o_xt + s * D, D),
                         lambda s: bf.view(o_xntm + s * D, D),
                         extra_waits=[tl])
        t_apply_prev = t_apply
        w0 = [t_apply] + ([t_groups_done] if t_groups_done else [])
        t_tr_last = transpose_tile(pb, ar, o_xntm, o_xnfm, w0, t_tr_last)
        t_xnfm = t_tr_last
        for half in range(2):
            group(lambda bank, half=half: [
                (bank[:, :TT], bf.view(o_w2q + ck * 256 + half * 128, 128),
                 bf.view(o_xnfm + ck * TT, TT), ck == 0, ck == NCK - 1)
                for ck in range(NCK)],
                TT, ev_elu(o_qp + half * S + i * TT, TT), extra=[t_xnfm])
        for s in range(NS):
            group(lambda bank, s=s: [
                (bank[:, :256], bf.view(o_xnfm + ck * TT + s * 128, 128),
                 bf.view(o_w2k + ck * 256, 256), ck == 0, ck == NCK - 1)
                for ck in range(NCK)],
                256, ev_elu(o_kp + s * 256, 256), extra=[t_xnfm])
        for s in range(NS):
            for h in range(2):
                group(lambda bank, s=s, h=h: [
                    (bank[:, :512], bf.view(o_xnfm + ck * TT + s * 128, 128),
                     bf.view(o_wvT + ck * D + h * 512, 512), ck == 0, ck == NCK - 1)
                    for ck in range(NCK)],
                    512, ev_copy_bf(o_vtm + s * D + h * 512, 512), extra=[t_xnfm])
        t_ready = pb.tok("vector")
        for ph in range(2):
            for dh in range(2):
                group(lambda bank, ph=ph, dh=dh: [
                    (bank[:, :512], bf.view(o_kp + s * 256 + ph * 128, 128),
                     bf.view(o_vtm + s * D + dh * 512, 512), s == 0, s == NS - 1)
                    for s in range(NS)],
                    512, ev_kv(o_kvacc + ph * D + dh * 512, 512, i == 0),
                    extra=[t_ready])
        t_groups_done = pb.tok("tensor")

    t_kvb = None
    for q in range(4):
        t_kvb = pb.op("vector", "tensor_copy", bf.view(o_kvb + q * 512, 512),
                      f32.view(o_kvacc + q * 512, 512))

    t_zinv_prev = None
    t_res_prev = None
    t_store = [None] * NTL
    for i in range(NTL):
        lw = [t_res_prev] if t_res_prev else []
        tl2 = pb.dma("sync", f32.view(o_xt2, NS * D), tile_in(io["x_in"], i),
                     waits=lw, slot="l0a_xt2")
        zparts = []
        for s in range(NS):
            for pc in range(2):
                zparts.append((PZ[:, s:s + 1],
                               bf.view(o_qp + pc * S + i * TT + s * 128, 128),
                               bf.view(o_ones, 1), pc == 0, pc == 1))
        zw = [t_kvb] + ([t_zinv_prev] if t_zinv_prev else [])
        tzmm = mm_group(pb, zparts, waits=zw)
        pb.op("vector", "tensor_scalar_add", f32.view(o_zi, 4),
              PZ[:, 0:4], 1e-8, waits=[tzmm])
        t_zinv_prev = pb.op("vector", "reciprocal", f32.view(o_zi, 4),
                            f32.view(o_zi, 4))
        for s in range(NS):
            for h in range(2):
                def ev(bank, mmtok, s=s, h=h):
                    return pb.op("vector", "tensor_single_scalar",
                                 f32.view(o_attn + s * D + h * 512, 512),
                                 bank[:, :512], f32.view(o_zi + s, 1), ALU.mult,
                                 waits=[mmtok])
                group(lambda bank, s=s, h=h: [
                    (bank[:, :512], bf.view(o_qp + pc * S + i * TT + s * 128, 128),
                     bf.view(o_kvb + pc * D + h * 512, 512), pc == 0, pc == 1)
                    for pc in range(2)],
                    512, ev)
        t_attn = pb.tok("vector")
        if i > 0:
            pb.wait_only("vector", [t_store[i - 1]])
        lnk.ln(lambda s: f32.view(o_attn + s * D, D),
               lambda s: f32.view(o_xo + s * D, D), extra_waits=[t_attn])
        tres = None
        for s in range(NS):
            tres = pb.op("vector", "tensor_add", f32.view(o_xo + s * D, D),
                         f32.view(o_xo + s * D, D), f32.view(o_xt2 + s * D, D),
                         waits=[tl2])
        t_res_prev = tres
        t_store[i] = pb.dma("sync", tile_in(io["x_out"], i),
                            f32.view(o_xo, NS * D), waits=[tres], slot="l0a_xo")
    return t_store[-1]



def emit_l0f(pb, io, ar):
    P, PZ = ar["P"], ar["PZ"]
    bf, f32 = ar["bf"], ar["f32"]
    arena_reset(ar)
    F = 2048
    NFC = F // 128

    o_mask = f32.take(NTL * NS * 8)
    o_iota = f32.take(8)
    lnk = LNKit(pb, ar)
    o_logit = f32.take(NS * 8)
    o_mx = f32.take(4)
    o_mx8 = f32.take(8)
    o_zero8 = f32.take(8)
    o_idx = f32.take(4)
    o_sc = f32.take(4)
    idx_u32 = pb.nc.alloc_sbuf_tensor("l0f_idx", [128, 8], mybir.dt.uint32)
    o_acc = f32.take(NS * D)
    o_xt = f32.take(NS * D)
    o_xo = f32.take(NS * D)

    o_gwf = f32.take(NCK * 8)
    o_xcfm = f32.take(NCK * 128)
    o_identf = f32.take(128)
    mark2 = (bf.off, f32.off)

    o_xntm = bf.take(NS * D)
    o_xstm = bf.take(NS * D)
    o_xnfm = bf.take(NCK * TT)
    o_xsfm = bf.take(NCK * TT)

    t_gw = pb.dma("sync", f32.view(o_gwf, NCK * 8),
                  io["gwT"].rearrange("(c p) e -> p c e", p=128), slot="l0f_gw")
    t_idf = pb.op("vector", "tensor_copy", f32.view(o_identf, 128),
                  bf.view(ar["o_ident"], 128), waits=[ar["t_ident"]])
    t_iota = None
    for e in range(8):
        t_iota = pb.op("gpsimd", "memset", f32.view(o_iota + e, 1), float(e))
    t_iota = pb.op("gpsimd", "memset", f32.view(o_zero8, 8), 0.0)

    rot = Rot(P)

    def group(parts_fn, width, evac_fn, extra=()):
        bank, banktok, bi = rot.next()
        w = list(extra) + ([banktok] if banktok else [])
        mmtok = mm_group(pb, parts_fn(bank), waits=w)
        evtok = evac_fn(bank, mmtok)
        rot.done(bi, evtok)
        return evtok

    xnfm_hbm = io["xnfm_h"].rearrange("c p t -> p c t")
    xsfm_hbm = io["xsfm_h"].rearrange("c p t -> p c t")

    t_apply_prev = None
    t_tr_last = None
    t_pz_last = None
    t_xfst = [None] * NTL
    t_xsst = [None] * NTL
    t_route_pe = None
    for i in range(NTL):
        lw = ([t_apply_prev] if t_apply_prev else []) \
            + ([t_route_pe] if t_route_pe else [])
        tl = pb.dma("sync", f32.view(o_xt, NS * D), tile_in(io["x_in"], i),
                    waits=lw, slot="l0f_xt")
        t_apply = lnk.ln(lambda s: f32.view(o_xt + s * D, D),
                         lambda s: bf.view(o_xntm + s * D, D), extra_waits=[tl])
        w0 = [t_apply] + ([t_xfst[i - 1]] if i > 0 else [])
        t_tr_last = transpose_tile(pb, ar, o_xntm, o_xnfm, w0, t_tr_last)
        t_xnfm = t_tr_last
        for s in range(NS):
            pb.op("vector", "tensor_single_scalar", f32.view(o_xt + s * D, D),
                  f32.view(o_xt + s * D, D), lnk.m1(s), ALU.subtract,
                  waits=[t_apply])
        t_rt = None
        for s in range(NS):
            for half in range(2):
                w0 = [pb.tok("vector")] + ([t_pz_last] if t_pz_last else [])
                ttr = None
                for k in range(4):
                    ck = half * 4 + k
                    ttr = pb.op("tensor", "transpose",
                                PZ[:, k * 128:(k + 1) * 128],
                                f32.view(o_xt + s * D + ck * 128, 128),
                                f32.view(o_identf, 128),
                                waits=(w0 + [t_idf] if k == 0 else ()),
                                inc=(k == 3))
                t_pz_last = pb.op("vector", "tensor_copy",
                                  f32.view(o_xcfm + half * 512, 512),
                                  PZ[:, 0:512], waits=[ttr])
            def ev_logit(bank, mmtok, s=s):
                return pb.op("vector", "tensor_single_scalar",
                             f32.view(o_logit + s * 8, 8), bank[:, :8],
                             lnk.rstd1(s), ALU.mult, waits=[mmtok])
            group(lambda bank, s=s: [
                (bank[:, :8], f32.view(o_xcfm + ck * 128, 128),
                 f32.view(o_gwf + ck * 8, 8), ck == 0, ck == NCK - 1)
                for ck in range(NCK)],
                8, ev_logit, extra=[t_pz_last, t_gw])
            pb.op("vector", "reduce_max", f32.view(o_mx + s, 1),
                  f32.view(o_logit + s * 8, 8), axis=AX.X)
            pb.op("vector", "tensor_single_scalar", f32.view(o_mx8, 8),
                  f32.view(o_zero8, 8), f32.view(o_mx + s, 1), ALU.add,
                  waits=[t_iota])
            pb.op("vector", "max_index", idx_u32[:, 0:8],
                  f32.view(o_mx8, 8), f32.view(o_logit + s * 8, 8))
            pb.op("vector", "tensor_copy", f32.view(o_idx + s, 1),
                  idx_u32[:, 0:1])
            tsg = pb.op("scalar", "activation", f32.view(o_sc + s, 1),
                        f32.view(o_mx + s, 1), AF.Sigmoid,
                        waits=[pb.tok("vector")])
            t_rt = pb.op("vector", "tensor_single_scalar",
                         f32.view(o_mask + (i * NS + s) * 8, 8),
                         f32.view(o_iota, 8),
                         f32.view(o_idx + s, 1), ALU.is_equal)
            pb.op("vector", "tensor_single_scalar", bf.view(o_xstm + s * D, D),
                  bf.view(o_xntm + s * D, D), f32.view(o_sc + s, 1), ALU.mult,
                  waits=[tsg])
        t_xs = pb.tok("vector")
        t_apply_prev = t_xs
        t_route_pe = pb.tok("tensor")
        w1 = [t_xs] + ([t_xsst[i - 1]] if i > 0 else [])
        t_tr_last = transpose_tile(pb, ar, o_xstm, o_xsfm, w1, t_tr_last)
        t_xfst[i] = pb.dma("sync", xnfm_hbm[:, :, i * TT:(i + 1) * TT],
                           bf.view(o_xnfm, NCK * TT).rearrange("p (c t) -> p c t", c=NCK),
                           waits=[t_xnfm], slot="l0f_xnst")
        t_xsst[i] = pb.dma("sync", xsfm_hbm[:, :, i * TT:(i + 1) * TT],
                           bf.view(o_xsfm, NCK * TT).rearrange("p (c t) -> p c t", c=NCK),
                           waits=[t_tr_last], slot="l0f_xsst")
    t_stageA_pe = pb.tok("tensor")
    t_stageA_ve = pb.tok("vector")

    bf.off, f32.off = mark2
    o_wg = bf.take(NCK * F)
    o_wu = bf.take(NCK * F)
    o_wd = bf.take(NFC * D)
    o_xf = bf.take(NCK * TT)
    o_gs = bf.take(4 * TT)
    o_h = bf.take(NFC * TT)

    t_accst = [None] * NTL
    t_pe_pass_end = t_stageA_pe
    t_pe_tile = None
    for e in range(9):
        wld = [t_pe_pass_end, t_stageA_ve, t_xfst[-1], t_xsst[-1]]
        twg = pb.dma("gpsimd", bf.view(o_wg, NCK * F),
                     io["gate_all"][e].rearrange("(c p) f -> p c f", p=128),
                     waits=wld, slot="l0f_w")
        twu = pb.dma("gpsimd", bf.view(o_wu, NCK * F),
                     io["up_all"][e].rearrange("(c p) f -> p c f", p=128),
                     slot="l0f_w")
        twd = pb.dma("gpsimd", bf.view(o_wd, NFC * D),
                     io["down_all"][e].rearrange("(c p) h -> p c h", p=128),
                     slot="l0f_w")
        for i in range(NTL):
            src_h = xnfm_hbm if e == 0 else xsfm_hbm
            lw = [t_xfst[i] if e == 0 else t_xsst[i]]
            if t_pe_tile:
                lw.append(t_pe_tile)
            t_xf = pb.dma("sync", bf.view(o_xf, NCK * TT)
                          .rearrange("p (c t) -> p c t", c=NCK),
                          src_h[:, :, i * TT:(i + 1) * TT], waits=lw, slot="l0f_xf")
            if e > 0:
                t_acc = pb.dma("sync", f32.view(o_acc, NS * D),
                               tile_in(io["acc_h"], i),
                               waits=[t_accst[i], pb.tok("vector")],
                               slot="l0f_acc")
            t_sil = [None] * NFC
            t_uev = [None] * NFC
            for cm in range(NFC):
                def ev_g(bank, mmtok, cm=cm):
                    w = [mmtok] + ([t_uev[cm - 4]] if cm >= 4 else [])
                    t_sil[cm] = pb.op("scalar", "activation",
                                      bf.view(o_gs + (cm % 4) * TT, TT), bank[:, :TT],
                                      AF.Silu, waits=w)
                    return t_sil[cm]
                group(lambda bank, cm=cm: [
                    (bank[:, :TT], bf.view(o_wg + ck * F + cm * 128, 128),
                     bf.view(o_xf + ck * TT, TT), ck == 0, ck == NCK - 1)
                    for ck in range(NCK)],
                    TT, ev_g, extra=[t_xf, twg])
                def ev_u(bank, mmtok, cm=cm):
                    t_uev[cm] = pb.op("vector", "tensor_mul",
                                      bf.view(o_h + cm * TT, TT),
                                      bf.view(o_gs + (cm % 4) * TT, TT),
                                      bank[:, :TT], waits=[mmtok, t_sil[cm]])
                    return t_uev[cm]
                group(lambda bank, cm=cm: [
                    (bank[:, :TT], bf.view(o_wu + ck * F + cm * 128, 128),
                     bf.view(o_xf + ck * TT, TT), ck == 0, ck == NCK - 1)
                    for ck in range(NCK)],
                    TT, ev_u, extra=[t_xf, twu])
            t_h = pb.tok("vector")
            for s in range(NS):
                for hh in range(2):
                    def ev_d(bank, mmtok, s=s, hh=hh):
                        dst = f32.view(o_acc + s * D + hh * 512, 512)
                        if e == 0:
                            w0 = [mmtok] + ([t_accst[i - 1]] if i > 0 else [])
                            return pb.op("vector", "tensor_copy", dst,
                                         bank[:, :512], waits=w0)
                        mcol = f32.view(o_mask + (i * NS + s) * 8 + (e - 1), 1)
                        return pb.op("vector", "scalar_tensor_tensor", dst,
                                     bank[:, :512], mcol, dst,
                                     ALU.mult, ALU.add,
                                     waits=[mmtok] + ([t_acc] if e > 0 else []))
                    group(lambda bank, s=s, hh=hh: [
                        (bank[:, :512], bf.view(o_h + fc * TT + s * 128, 128),
                         bf.view(o_wd + fc * D + hh * 512, 512), fc == 0, fc == NFC - 1)
                        for fc in range(NFC)],
                        512, ev_d, extra=[t_h, twd])
            t_dn = pb.tok("vector")
            t_accst[i] = pb.dma("sync", tile_in(io["acc_h"], i),
                                f32.view(o_acc, NS * D), waits=[t_dn],
                                slot="l0f_accst")
            t_pe_tile = pb.tok("tensor")
        t_pe_pass_end = pb.tok("tensor")

    t_store = [None] * NTL
    t_ep_prev = None
    for i in range(NTL):
        lw = [t_accst[i]] + ([t_ep_prev] if t_ep_prev else [])
        ta = pb.dma("sync", f32.view(o_acc, NS * D), tile_in(io["acc_h"], i),
                    waits=lw, slot="l0f_acc")
        tl = pb.dma("sync", f32.view(o_xt, NS * D), tile_in(io["x_in"], i),
                    waits=lw, slot="l0f_xt")
        if i > 0:
            pb.wait_only("vector", [t_store[i - 1]])
        lnk.ln(lambda s: f32.view(o_acc + s * D, D),
               lambda s: f32.view(o_xo + s * D, D), extra_waits=[ta])
        tres = None
        for s in range(NS):
            tres = pb.op("vector", "tensor_add", f32.view(o_xo + s * D, D),
                         f32.view(o_xo + s * D, D), f32.view(o_xt + s * D, D),
                         waits=[tl])
        t_ep_prev = tres
        t_store[i] = pb.dma("sync", tile_in(io["x_out"], i),
                            f32.view(o_xo, NS * D), waits=[tres], slot="l0f_xo")
    return t_store[-1]



def emit_l1f(pb, io, ar):
    P, PTR, PZ = ar["P"], ar["PTR"], ar["PZ"]
    bf, f32 = ar["bf"], ar["f32"]
    arena_reset(ar)
    SH = TT + 1

    o_wrT = bf.take(NCK * D)
    o_wkT = bf.take(NCK * D)
    o_wvT = bf.take(NCK * D)
    o_xntm = bf.take(NS * D)
    o_xsh = bf.take(NCK * SH)
    o_xr = bf.take(NCK * TT)
    o_xk = bf.take(NCK * TT)
    o_rtm = bf.take(NS * D)
    o_kact = bf.take(NCK * TT)
    o_tmpb = bf.take(512)
    o_tmpc = bf.take(512)

    o_xt = f32.take(NS * D)
    o_xo = f32.take(NS * D)
    o_out = f32.take(NS * D)
    o_mu = f32.take(32)
    o_identf = f32.take(128)
    o_mstg = f32.take(128)
    lnk = LNKit(pb, ar)

    tw = []
    for (off, key) in ((o_wrT, "wrT"), (o_wkT, "wkT"), (o_wvT, "wvT")):
        tw.append(pb.dma("gpsimd", bf.view(off, NCK * D),
                         io[key].rearrange("(c p) d -> p c d", p=128),
                         slot="l1f_w"))
    t_idf = pb.op("vector", "tensor_copy", f32.view(o_identf, 128),
                  bf.view(ar["o_ident"], 128), waits=[ar["t_ident"]])

    t_mu = None
    for j, key in enumerate(("mu_r8", "mu_k8")):
        tm = pb.dma("sync", f32.view(o_mstg, 128)[0:8, :],
                    io[key][:, :], slot="l1f_mu", waits=[t_mu] if t_mu else ())
        ttr = pb.op("tensor", "transpose", PZ[:, 0:8],
                    f32.view(o_mstg, 128)[0:8, :],
                    f32.view(o_identf, 128)[0:8, 0:8], waits=[tm, t_idf], inc=True)
        t_mu = pb.op("vector", "tensor_copy", f32.view(o_mu + j * 8, 8),
                     PZ[:, 0:8], waits=[ttr])
        t_mu = pb.op("vector", "tensor_scalar", f32.view(o_mu + 16 + j * 8, 8),
                     f32.view(o_mu + j * 8, 8), -1.0, 1.0, ALU.mult, ALU.add)

    t_z = None
    for ck in range(NCK):
        t_z = pb.op("vector", "memset", bf.view(o_xsh + ck * SH, 1), 0.0)

    rot = Rot(P)

    def group(parts_fn, width, evac_fn, extra=()):
        bank, banktok, bi = rot.next()
        w = list(extra) + ([banktok] if banktok else [])
        mmtok = mm_group(pb, parts_fn(bank), waits=w)
        evtok = evac_fn(bank, mmtok)
        rot.done(bi, evtok)
        return evtok

    xsh3d = bf.view(o_xsh, NCK * SH).rearrange("p (c t) -> p c t", c=NCK)
    ptr3d = ar["PTR"][:, 0:1024].rearrange("p (c t) -> p c t", c=NCK)

    t_apply_prev = None
    t_tr_last = t_z
    t_pe_prev = None
    t_store = [None] * NTL
    for i in range(NTL):
        lw = ([t_apply_prev] if t_apply_prev else [])
        tl = pb.dma("sync", f32.view(o_xt, NS * D), tile_in(io["x_in"], i),
                    waits=lw, slot="l1f_xt")
        t_apply = lnk.ln(lambda s: f32.view(o_xt + s * D, D),
                         lambda s: bf.view(o_xntm + s * D, D), extra_waits=[tl])
        t_apply_prev = t_apply
        if i > 0:
            tcar = None
            for ck in range(NCK):
                tcar = pb.op("vector", "tensor_copy", bf.view(o_xsh + ck * SH, 1),
                             bf.view(o_xsh + ck * SH + TT, 1),
                             waits=[t_pe_prev] if (ck == 0 and t_pe_prev) else ())
            t_tr_last = tcar
        for s in range(NS):
            w0 = [t_apply] + ([t_tr_last] if t_tr_last else []) + [ar["t_ident"]]
            ttr = None
            for ck in range(NCK):
                ttr = pb.op("tensor", "transpose", PTR[:, ck * 128:(ck + 1) * 128],
                            bf.view(o_xntm + s * D + ck * 128, 128),
                            bf.view(ar["o_ident"], 128),
                            waits=(w0 if ck == 0 else ()), inc=(ck == NCK - 1))
            t_tr_last = pb.op("vector", "tensor_copy",
                              xsh3d[:, :, 1 + s * 128:1 + (s + 1) * 128], ptr3d,
                              waits=[ttr])
        t_mix = None
        for ck in range(NCK):
            for (o_dst, cmu) in ((o_xr, 0), (o_xk, 8)):
                pb.op("gpsimd", "tensor_single_scalar", bf.view(o_tmpc, 512),
                      bf.view(o_xsh + ck * SH, TT), f32.view(o_mu + 16 + cmu + ck, 1),
                      ALU.mult, waits=[t_tr_last, t_mu])
                pb.op("gpsimd", "tensor_single_scalar", bf.view(o_dst + ck * TT, TT),
                      bf.view(o_xsh + ck * SH + 1, TT),
                      f32.view(o_mu + cmu + ck, 1), ALU.mult)
                t_mix = pb.op("gpsimd", "tensor_add", bf.view(o_dst + ck * TT, TT),
                              bf.view(o_dst + ck * TT, TT), bf.view(o_tmpc, 512))
        for s in range(NS):
            for hh in range(2):
                def ev_r(bank, mmtok, s=s, hh=hh):
                    return pb.op("scalar", "activation",
                                 bf.view(o_rtm + s * D + hh * 512, 512),
                                 bank[:, :512], AF.Sigmoid, waits=[mmtok])
                group(lambda bank, s=s, hh=hh: [
                    (bank[:, :512], bf.view(o_xr + ck * TT + s * 128, 128),
                     bf.view(o_wrT + ck * D + hh * 512, 512), ck == 0, ck == NCK - 1)
                    for ck in range(NCK)],
                    512, ev_r, extra=[t_mix, tw[0]])
        for cm in range(NCK):
            def ev_k(bank, mmtok, cm=cm):
                pb.op("vector", "tensor_scalar_max", bf.view(o_tmpb, 512),
                      bank[:, :512], 0.0, waits=[mmtok])
                return pb.op("vector", "tensor_mul", bf.view(o_kact + cm * TT, TT),
                             bf.view(o_tmpb, 512), bf.view(o_tmpb, 512))
            group(lambda bank, cm=cm: [
                (bank[:, :512], bf.view(o_wkT + ck * D + cm * 128, 128),
                 bf.view(o_xk + ck * TT, TT), ck == 0, ck == NCK - 1)
                for ck in range(NCK)],
                512, ev_k, extra=[t_mix, tw[1]])
        t_kact = pb.tok("vector")
        for s in range(NS):
            for hh in range(2):
                def ev_o(bank, mmtok, s=s, hh=hh):
                    return pb.op("vector", "tensor_mul",
                                 f32.view(o_out + s * D + hh * 512, 512),
                                 bf.view(o_rtm + s * D + hh * 512, 512),
                                 bank[:, :512], waits=[mmtok])
                group(lambda bank, s=s, hh=hh: [
                    (bank[:, :512], bf.view(o_kact + ck * TT + s * 128, 128),
                     bf.view(o_wvT + ck * D + hh * 512, 512), ck == 0, ck == NCK - 1)
                    for ck in range(NCK)],
                    512, ev_o, extra=[t_kact, tw[2]])
        t_out = pb.tok("vector")
        t_pe_prev = pb.tok("tensor")
        if i > 0:
            pb.wait_only("vector", [t_store[i - 1]])
        lnk.ln(lambda s: f32.view(o_out + s * D, D),
               lambda s: f32.view(o_xo + s * D, D), extra_waits=[t_out])
        tres = None
        for s in range(NS):
            tres = pb.op("vector", "tensor_add", f32.view(o_xo + s * D, D),
                         f32.view(o_xo + s * D, D), f32.view(o_xt + s * D, D),
                         waits=[tl])
        t_store[i] = pb.dma("sync", tile_in(io["x_out"], i),
                            f32.view(o_xo, NS * D), waits=[tres], slot="l1f_xo")
    return t_store[-1]



def emit_l1a(pb, io, ar):
    P, PTR, PZ = ar["P"], ar["PTR"], ar["PZ"]
    bf, f32 = ar["bf"], ar["f32"]
    arena_reset(ar)
    NH, HTT = 16, 256

    o_wq = bf.take(NCK * D)
    o_wk = bf.take(NCK * D)
    o_wv = bf.take(NCK * D)
    o_wo = bf.take(NCK * D)
    o_wg1 = bf.take(NCK * 16)
    o_wg2 = bf.take(D)
    o_ctab = bf.take(TT)
    o_tri = bf.take(512)
    o_xnfm = bf.take(NCK * TT)
    o_qR = o_xnfm
    o_xntm = bf.take(NS * D)
    o_qsil = bf.take(NCK * TT)
    o_gate = o_qsil
    o_E = bf.take(NCK * TT)
    o_onfm = o_E
    o_t1 = bf.take(HTT)
    o_ELr = bf.take(NH * HTT)
    o_ELrtm = bf.take(2 * 2048)
    o_vtm = bf.take(NS * D)
    o_Ub = bf.take(NH * 64)
    o_Asb = bf.take(NH * 128)
    o_g1 = bf.take(512)
    o_otm = bf.take(NS * D)

    o_xt = f32.take(NS * D)
    o_xo = f32.take(NS * D)
    o_C = f32.take(NCK * TT)
    o_out = o_C
    o_carry = f32.take(8)
    o_U = f32.take(NH * 64)
    o_nwf = f32.take(8)
    o_rms = f32.take(8)
    o_identf = f32.take(128)
    o_stg = f32.take(128)
    lnk = LNKit(pb, ar)

    tw = []
    for (off, key, wdt) in ((o_wq, "wq", D), (o_wk, "wk", D), (o_wv, "wv", D),
                            (o_wo, "wo", D), (o_wg1, "wg1", 16)):
        tw.append(pb.dma("gpsimd", bf.view(off, NCK * wdt),
                         io[key].rearrange("(c p) d -> p c d", p=128),
                         slot="l1a_w"))
    tw.append(pb.dma("gpsimd", bf.h[0:16, o_wg2:o_wg2 + D], io["wg2"][:, :],
                     slot="l1a_w"))
    o_dup = bf.take(128)
    tw.append(pb.dma("gpsimd", bf.view(o_dup, 128), io["dup"][:, :],
                     slot="l1a_w"))
    t_tri = pb.dma("gpsimd", bf.view(o_tri, 512), io["tri4"][:, :],
                   slot="l1a_w")
    t_idf = pb.op("vector", "tensor_copy", f32.view(o_identf, 128),
                  bf.view(ar["o_ident"], 128), waits=[ar["t_ident"]])
    tnw0 = pb.dma("sync", f32.view(o_stg, 128)[0:8, :], io["nw8"][:, :],
                  slot="l1a_nw")
    tnw1 = pb.op("tensor", "transpose", PZ[:, 0:8], f32.view(o_stg, 128)[0:8, :],
                 f32.view(o_identf, 128)[0:8, 0:8], waits=[tnw0, t_idf], inc=True)
    t_nw = pb.op("vector", "tensor_copy", f32.view(o_nwf, 8), PZ[:, 0:8],
                 waits=[tnw1])
    t_u = None
    for g in range(2):
        pb.op("vector", "memset", f32.view(o_U + g * 512, 512), 0.0)
        t_u = pb.op("vector", "memset", bf.view(o_Ub + g * 512, 512), 0.0)

    rot = Rot(P)

    def group(parts_fn, width, evac_fn, extra=()):
        bank, banktok, bi = rot.next()
        w = list(extra) + ([banktok] if banktok else [])
        mmtok = mm_group(pb, parts_fn(bank), waits=w)
        evtok = evac_fn(bank, mmtok)
        rot.done(bi, evtok)
        return evtok

    t_apply_prev = None
    t_tr_last = None
    t_store = [None] * NTL
    t_lrpe_prev = []
    t_Ubf = t_u
    for i in range(NTL):
        lw = ([t_apply_prev] if t_apply_prev else [])
        tl = pb.dma("sync", f32.view(o_xt, NS * D), tile_in(io["x_in"], i),
                    waits=lw, slot="l1a_xt")
        tct = pb.dma("gpsimd", bf.view(o_ctab, TT),
                     io["costab"][:, i * TT:(i + 1) * TT],
                     waits=list(t_lrpe_prev), slot="l1a_ct")
        t_apply = lnk.ln(lambda s: f32.view(o_xt + s * D, D),
                         lambda s: bf.view(o_xntm + s * D, D), extra_waits=[tl])
        t_apply_prev = t_apply
        t_tr_last = transpose_tile(pb, ar, o_xntm, o_xnfm, [t_apply], t_tr_last)
        t_xnfm = t_tr_last
        tE = [None] * NCK
        for cm in range(NCK):
            def ev_q(bank, mmtok, cm=cm):
                return pb.op("scalar", "activation", bf.view(o_qsil + cm * TT, TT),
                             bank[:, :TT], AF.Silu, waits=[mmtok])
            group(lambda bank, cm=cm: [
                (bank[:, :TT], bf.view(o_wq + ck * D + cm * 128, 128),
                 bf.view(o_xnfm + ck * TT, TT), ck == 0, ck == NCK - 1)
                for ck in range(NCK)],
                TT, ev_q, extra=[t_xnfm, tw[0]])
            def ev_e(bank, mmtok, cm=cm):
                tE[cm] = pb.op("scalar", "activation", bf.view(o_E + cm * TT, TT),
                               bank[:, :TT], AF.Exp, waits=[mmtok])
                return tE[cm]
            group(lambda bank, cm=cm: [
                (bank[:, :TT], bf.view(o_wk + ck * D + cm * 128, 128),
                 bf.view(o_xnfm + ck * TT, TT), ck == 0, ck == NCK - 1)
                for ck in range(NCK)],
                TT, ev_e, extra=[t_xnfm, tw[1]])
        for s in range(NS):
            for hh in range(2):
                def ev_v(bank, mmtok, s=s, hh=hh):
                    return pb.op("vector", "tensor_copy",
                                 bf.view(o_vtm + s * D + hh * 512, 512),
                                 bank[:, :512], waits=[mmtok])
                group(lambda bank, s=s, hh=hh: [
                    (bank[:, :512], bf.view(o_xnfm + ck * TT + s * 128, 128),
                     bf.view(o_wv + ck * D + hh * 512, 512), ck == 0, ck == NCK - 1)
                    for ck in range(NCK)],
                    512, ev_v, extra=[t_xnfm, tw[2]])
        def ev_g1(bank, mmtok):
            return pb.op("vector", "tensor_copy", bf.h[0:16, o_g1:o_g1 + 512],
                         bank[0:16, :512], waits=[mmtok])
        group(lambda bank: [
            (bank[0:16, :512], bf.view(o_wg1 + ck * 16, 16),
             bf.view(o_xnfm + ck * TT, TT), ck == 0, ck == NCK - 1)
            for ck in range(NCK)],
            512, ev_g1, extra=[t_xnfm, tw[4]])
        t_qkv_pe = pb.tok("tensor")
        t_scan = None
        for ck in range(NCK):
            ini = 1.0 if i == 0 else f32.view(o_carry + ck, 1)
            pb.op("vector", "tensor_tensor_scan", f32.view(o_C + ck * TT, TT),
                  bf.view(o_E + ck * TT, TT), bf.view(o_E + ck * TT, TT), ini,
                  ALU.add, ALU.bypass, waits=[tE[ck]])
            t_scan = pb.op("vector", "tensor_copy", f32.view(o_carry + ck, 1),
                           f32.view(o_C + ck * TT + TT - 1, 1))
        tR = [None] * NCK
        for ck in range(NCK):
            pb.op("scalar", "activation", f32.view(o_C + ck * TT, TT),
                  f32.view(o_C + ck * TT, TT), AF.Ln, waits=[t_scan])
            tR[ck] = pb.op("scalar", "activation", f32.view(o_C + ck * TT, TT),
                           f32.view(o_C + ck * TT, TT), AF.Exp, scale=-1.0)

        for hf in range(2):
            t_q = None
            t_el = None
            ctab_v = bf.view(o_ctab + hf * HTT, HTT)
            for ck in range(NCK):
                tt1 = pb.op("vector", "tensor_mul", bf.view(o_t1, HTT),
                            bf.view(o_qsil + ck * TT + hf * HTT, HTT),
                            f32.view(o_C + ck * TT + hf * HTT, HTT),
                            waits=[tR[ck], t_qkv_pe, tct]
                            + ([t_q] if t_q else []))
                for hh in range(2):
                    h = 2 * ck + hh
                    b0 = hh * 64
                    def ev_qd(bank, mmtok, h=h):
                        return pb.op("vector", "tensor_mul",
                                     bf.view(o_qR + h * HTT, HTT),
                                     bank[:, :HTT], ctab_v,
                                     waits=[mmtok, tct])
                    t_q = group(lambda bank, b0=b0: [
                        (bank[:, :HTT], bf.h[b0:b0 + 64, o_dup:o_dup + 128],
                         bf.h[b0:b0 + 64, o_t1:o_t1 + HTT], True, True)],
                        HTT, ev_qd, extra=[tt1, tw[6]])
                    def ev_ed(bank, mmtok, h=h):
                        return pb.op("vector", "tensor_mul",
                                     bf.view(o_ELr + h * HTT, HTT),
                                     bank[:, :HTT], ctab_v,
                                     waits=[mmtok, tct])
                    t_el = group(lambda bank, b0=b0, ck=ck: [
                        (bank[:, :HTT], bf.h[b0:b0 + 64, o_dup:o_dup + 128],
                         bf.h[b0:b0 + 64, o_E + ck * TT + hf * HTT:o_E + ck * TT + hf * HTT + HTT],
                         True, True)],
                        HTT, ev_ed, extra=[tE[ck], tw[6]])
            t_lrpe_prev = [t_el, t_q]
            for sl in range(2):
                for bb in range(2):
                    w0 = [t_el, t_q] + ([t_tr_last] if t_tr_last else [])
                    ttr = None
                    for k in range(8):
                        h = bb * 8 + k
                        ttr = pb.op("tensor", "transpose",
                                    PTR[:, k * 128:(k + 1) * 128],
                                    bf.view(o_ELr + h * HTT + sl * 128, 128),
                                    bf.view(ar["o_ident"], 128),
                                    waits=(w0 if k == 0 else ()), inc=(k == 7))
                    t_tr_last = pb.op("vector", "tensor_copy",
                                      bf.view(o_ELrtm + sl * 2048 + bb * 1024, 1024),
                                      PTR[:, 0:1024], waits=[ttr])
            t_eltm = t_tr_last
            for c in range(2):
                sg = hf * 2 + c
                tA = [None] * 4
                for g4 in range(4):
                    def ev_a(bank, mmtok, g4=g4):
                        tA[g4] = pb.op("vector", "tensor_mul",
                                       bf.view(o_Asb + g4 * 512, 512),
                                       bf.view(o_tri, 512),
                                       bank[:, :512], waits=[mmtok, t_tri])
                        return tA[g4]
                    group(lambda bank, g4=g4: [
                        (bank[:, q * 128:(q + 1) * 128],
                         bf.view(o_ELr + (g4 * 4 + q) * HTT + c * 128, 128),
                         bf.view(o_qR + (g4 * 4 + q) * HTT + c * 128, 128),
                         True, True)
                        for q in range(4)],
                        512, ev_a, extra=[t_el, t_q])
                for hg in range(2):
                    def ev_o(bank, mmtok, hg=hg):
                        return pb.op("vector", "tensor_copy",
                                     bf.view(o_otm + sg * D + hg * 512, 512),
                                     bank[:, :512], waits=[mmtok])
                    def mk_parts(bank, hg=hg):
                        out = []
                        for h8 in range(8):
                            h = hg * 8 + h8
                            out.append((bank[:, h8 * 64:(h8 + 1) * 64],
                                        bf.view(o_Asb + h * 128, 128),
                                        bf.view(o_vtm + sg * D + h * 64, 64),
                                        True, False))
                            out.append((bank[:, h8 * 64:(h8 + 1) * 64],
                                        bf.view(o_qR + h * HTT + c * 128, 128),
                                        bf.view(o_Ub + h * 64, 64),
                                        False, True))
                        return out
                    group(mk_parts, 512, ev_o,
                          extra=[tA[hg * 2], tA[hg * 2 + 1], t_Ubf])
                t_o_pe = pb.tok("tensor")
                for hg in range(2):
                    def ev_ds(bank, mmtok, hg=hg):
                        pb.op("vector", "tensor_add",
                              f32.view(o_U + hg * 512, 512),
                              f32.view(o_U + hg * 512, 512), bank[:, :512],
                              waits=[mmtok])
                        return pb.op("vector", "tensor_copy",
                                     bf.view(o_Ub + hg * 512, 512),
                                     f32.view(o_U + hg * 512, 512),
                                     waits=[t_o_pe])
                    t_Ubf = group(lambda bank, hg=hg: [
                        (bank[:, h8 * 64:(h8 + 1) * 64],
                         bf.view(o_ELrtm + c * 2048 + (hg * 8 + h8) * 128, 128),
                         bf.view(o_vtm + sg * D + (hg * 8 + h8) * 64, 64),
                         True, True)
                        for h8 in range(8)],
                        512, ev_ds, extra=[t_eltm])

        t_half2 = pb.tok("vector")
        tgate = None
        for s in range(NS):
            for hh in range(2):
                def ev_gt(bank, mmtok, s=s, hh=hh):
                    return pb.op("scalar", "activation",
                                 bf.view(o_gate + s * D + hh * 512, 512),
                                 bank[:, :512], AF.Sigmoid, waits=[mmtok])
                tgate = group(lambda bank, s=s, hh=hh: [
                    (bank[:, :512], bf.h[0:16, o_g1 + s * 128:o_g1 + s * 128 + 128],
                     bf.h[0:16, o_wg2 + hh * 512:o_wg2 + hh * 512 + 512],
                     True, True)],
                    512, ev_gt, extra=[t_half2, tw[5]])
        tog = None
        for s in range(NS):
            for hh in range(2):
                tog = pb.op("vector", "tensor_mul",
                            bf.view(o_otm + s * D + hh * 512, 512),
                            bf.view(o_otm + s * D + hh * 512, 512),
                            bf.view(o_gate + s * D + hh * 512, 512),
                            waits=[tgate])
        tsq = None
        for s in range(NS):
            tsq = pb.op("scalar", "activation", f32.view(lnk.o_sq, D),
                        bf.view(o_otm + s * D, D), AF.Square,
                        accum_out=f32.view(o_rms + s, 1), waits=[tog])
        pb.op("vector", "tensor_scalar_mul", f32.view(o_rms, 4),
              f32.view(o_rms, 4), 1.0 / D, waits=[tsq])
        tms = pb.tok("vector")
        pb.op("scalar", "activation", f32.view(o_rms + 4, 4),
              f32.view(o_rms, 4), AF.Ln, bias=1e-6, waits=[tms])
        trst = pb.op("scalar", "activation", f32.view(o_rms + 4, 4),
                     f32.view(o_rms + 4, 4), AF.Exp, scale=-0.5)
        ton = None
        for s in range(NS):
            ton = pb.op("vector", "tensor_single_scalar",
                        bf.view(o_otm + s * D, D), bf.view(o_otm + s * D, D),
                        f32.view(o_rms + 4 + s, 1), ALU.mult, waits=[trst])
        t_tr_last = transpose_tile(pb, ar, o_otm, o_onfm, [ton], t_tr_last)
        tnw_ap = None
        for ck in range(NCK):
            tnw_ap = pb.op("gpsimd", "tensor_single_scalar",
                           bf.view(o_onfm + ck * TT, TT),
                           bf.view(o_onfm + ck * TT, TT),
                           f32.view(o_nwf + ck, 1), ALU.mult,
                           waits=[t_tr_last, t_nw])
        tout = None
        for s in range(NS):
            for hh in range(2):
                def ev_out(bank, mmtok, s=s, hh=hh):
                    return pb.op("vector", "tensor_copy",
                                 f32.view(o_out + s * D + hh * 512, 512),
                                 bank[:, :512], waits=[mmtok])
                tout = group(lambda bank, s=s, hh=hh: [
                    (bank[:, :512], bf.view(o_onfm + ck * TT + s * 128, 128),
                     bf.view(o_wo + ck * D + hh * 512, 512), ck == 0, ck == NCK - 1)
                    for ck in range(NCK)],
                    512, ev_out, extra=[tnw_ap, tw[3]])
        if i > 0:
            pb.wait_only("vector", [t_store[i - 1]])
        lnk.ln(lambda s: f32.view(o_out + s * D, D),
               lambda s: f32.view(o_xo + s * D, D), extra_waits=[tout])
        tres = None
        for s in range(NS):
            tres = pb.op("vector", "tensor_add", f32.view(o_xo + s * D, D),
                         f32.view(o_xo + s * D, D), f32.view(o_xt + s * D, D),
                         waits=[tl])
        t_store[i] = pb.dma("sync", tile_in(io["x_out"], i),
                            f32.view(o_xo, NS * D), waits=[tres], slot="l1a_xo")
    return t_store[-1]



PHASES = ("l0a", "l0f", "l1a", "l1f")


def build_program(phases=PHASES):
    nc = bass.Bass("TRN2", target_bir_lowering=False)
    dt = nc.dram_tensor
    io = {"ident": dt("ident", [128, 128], BF16, kind="ExternalInput").ap()}
    io["x_in"] = dt("x_in", [S, D], F32, kind="ExternalInput").ap()
    io["x_out"] = dt("x_out", [S, D], F32, kind="ExternalOutput").ap()
    n_mid = len(phases) - 1
    mids = [dt(f"mid{k}", [S, D], F32, kind="Internal").ap() for k in range(n_mid)]

    if "l0a" in phases:
        io["l0a"] = {
            "wq": dt("l0a_wq", [D, D], F32, kind="ExternalInput").ap(),
            "wk": dt("l0a_wk", [D, D], F32, kind="ExternalInput").ap(),
            "wvT": dt("l0a_wvT", [D, D], F32, kind="ExternalInput").ap(),
            "proj": dt("l0a_proj", [D, 256], F32, kind="ExternalInput").ap(),
        }
    if "l1a" in phases:
        io["l1a"] = {
            "wq": dt("l1a_wq", [D, D], F32, kind="ExternalInput").ap(),
            "wk": dt("l1a_wk", [D, D], F32, kind="ExternalInput").ap(),
            "wv": dt("l1a_wv", [D, D], F32, kind="ExternalInput").ap(),
            "wo": dt("l1a_wo", [D, D], F32, kind="ExternalInput").ap(),
            "wg1": dt("l1a_wg1", [D, 16], F32, kind="ExternalInput").ap(),
            "wg2": dt("l1a_wg2", [16, D], F32, kind="ExternalInput").ap(),
            "nw8": dt("l1a_nw8", [8, 128], F32, kind="ExternalInput").ap(),
            "costab": dt("l1a_costab", [128, S], F32, kind="ExternalInput").ap(),
            "tri4": dt("l1a_tri4", [128, 512], BF16, kind="ExternalInput").ap(),
            "dup": dt("l1a_dup", [128, 128], BF16, kind="ExternalInput").ap(),
        }
    if "l1f" in phases:
        io["l1f"] = {
            "wrT": dt("l1f_wrT", [D, D], F32, kind="ExternalInput").ap(),
            "wkT": dt("l1f_wkT", [D, D], F32, kind="ExternalInput").ap(),
            "wvT": dt("l1f_wvT", [D, D], F32, kind="ExternalInput").ap(),
            "mu_r8": dt("l1f_mur", [8, 128], F32, kind="ExternalInput").ap(),
            "mu_k8": dt("l1f_muk", [8, 128], F32, kind="ExternalInput").ap(),
        }
    if "l0f" in phases:
        io["l0f"] = {
            "gate_all": dt("l0f_gate", [9, D, 2048], F32, kind="ExternalInput").ap(),
            "up_all": dt("l0f_up", [9, D, 2048], F32, kind="ExternalInput").ap(),
            "down_all": dt("l0f_down", [9, 2048, D], F32, kind="ExternalInput").ap(),
            "gwT": dt("l0f_gwT", [D, 8], F32, kind="ExternalInput").ap(),
            "xnfm_h": dt("l0f_xnfm", [NCK, 128, S], BF16, kind="Internal").ap(),
            "xsfm_h": dt("l0f_xsfm", [NCK, 128, S], BF16, kind="Internal").ap(),
            "acc_h": dt("l0f_acc", [S, D], F32, kind="Internal").ap(),
        }

    pb = PB(nc)
    ar = alloc_common(nc, pb, io)
    emits = {"l0a": emit_l0a, "l0f": emit_l0f, "l1a": emit_l1a,
             "l1f": emit_l1f}
    t_last = None
    for k, ph in enumerate(phases):
        pio = dict(io[ph])
        pio["x_in"] = io["x_in"] if k == 0 else mids[k - 1]
        pio["x_out"] = io["x_out"] if k == len(phases) - 1 else mids[k]
        t_last = emits[ph](pb, pio, ar)
        pb.set_barrier()
    pb.wait_only("sync", [t_last])
    pb.replay()
    return nc


def prep_inputs(inputs, c, phases=PHASES):
    C = np.ascontiguousarray
    m = {"ident": np.eye(128, dtype=BF),
         "x_in": C(np.asarray(inputs["L0_attn_x"][c]))}
    if "l0a" in phases:
        m.update({
            "l0a_wq": C(np.asarray(inputs["L0_attn_Wq"])),
            "l0a_wk": C(np.asarray(inputs["L0_attn_Wk"])),
            "l0a_wvT": C(np.asarray(inputs["L0_attn_Wv"]).T),
            "l0a_proj": C(np.asarray(inputs["L0_attn_proj"])),
        })
    if "l1a" in phases:
        HD = 64
        theta = (np.arange(S, dtype=np.float64)[:, None]
                 * (10000.0 ** (-2.0 * np.arange(HD, dtype=np.float64) / HD))[None])
        ctab = np.empty((128, S), np.float32)
        ctab[0:64] = np.cos(theta).T
        ctab[64:128] = np.sin(theta).T
        tri = np.triu(np.ones((128, 128), np.float32))
        tri4 = np.tile(tri, (1, 4)).astype(BF)
        m.update({
            "l1a_wq": C(np.asarray(inputs["L1_attn_Wq"])),
            "l1a_wk": C(np.asarray(inputs["L1_attn_Wk"])),
            "l1a_wv": C(np.asarray(inputs["L1_attn_Wv"])),
            "l1a_wo": C(np.asarray(inputs["L1_attn_Wo"])),
            "l1a_wg1": C(np.asarray(inputs["L1_attn_Wg1"])),
            "l1a_wg2": C(np.asarray(inputs["L1_attn_Wg2"])),
            "l1a_nw8": C(np.asarray(inputs["L1_attn_norm_w"]).reshape(8, 128)),
            "l1a_costab": ctab,
            "l1a_tri4": C(tri4),
            "l1a_dup": C(np.equal(np.arange(128)[:, None] % 64,
                                  np.arange(128)[None, :] % 64).astype(BF)),
        })
    if "l1f" in phases:
        m.update({
            "l1f_wrT": C(np.asarray(inputs["L1_ffn_W_r"]).T),
            "l1f_wkT": C(np.asarray(inputs["L1_ffn_W_k"]).T),
            "l1f_wvT": C(np.asarray(inputs["L1_ffn_W_v"]).T),
            "l1f_mur": C(np.asarray(inputs["L1_ffn_mu_r"]).reshape(8, 128)),
            "l1f_muk": C(np.asarray(inputs["L1_ffn_mu_k"]).reshape(8, 128)),
        })
    if "l0f" in phases:
        sg = np.asarray(inputs["L0_ffn_shared_gate"])
        su = np.asarray(inputs["L0_ffn_shared_up"])
        sd = np.asarray(inputs["L0_ffn_shared_down"])
        rg = np.asarray(inputs["L0_ffn_r_gate"])
        ru = np.asarray(inputs["L0_ffn_r_up"])
        rd = np.asarray(inputs["L0_ffn_r_down"])
        m.update({
            "l0f_gate": C(np.concatenate([sg.T[None], rg.transpose(0, 2, 1)])),
            "l0f_up": C(np.concatenate([su.T[None], ru.transpose(0, 2, 1)])),
            "l0f_down": C(np.concatenate([sd.T[None], rd.transpose(0, 2, 1)])),
            "l0f_gwT": C(np.asarray(inputs["L0_ffn_gate_w"]).T),
        })
    return m


def kernel(**inputs):
    phases = PHASES
    nc = build_program(phases)
    in_maps = [prep_inputs(inputs, c, phases) for c in range(8)]
    res = run_bass_kernel_spmd(nc, in_maps, core_ids=list(range(8)))
    return np.stack([np.asarray(r["x_out"], dtype=np.float32) for r in res.results])


# version 02
# speedup vs baseline: 1.0174x; 1.0174x over previous
import numpy as np
import ml_dtypes
import concourse.bass as bass
import concourse.mybir as mybir
from concourse.bass_utils import run_bass_kernel_spmd

F32 = mybir.dt.float32
BF16 = mybir.dt.bfloat16
AF = mybir.ActivationFunctionType
ALU = mybir.AluOpType
AX = mybir.AxisListType

S, D, TT = 8192, 1024, 512
NTL = S // TT
NS = TT // 128
NCK = D // 128
LN_EPS = 1e-5
BF = ml_dtypes.bfloat16

ENGS = ("sync", "gpsimd", "tensor", "vector", "scalar")


class PB:

    def __init__(self, nc, tag="pb"):
        self.nc = nc
        self.tag = tag
        self.items = {e: [] for e in ENGS}
        self.esem = {e: nc.alloc_semaphore(name=f"{tag}_{e}")
                     for e in ("tensor", "vector", "scalar", "gpsimd")}
        self.ecnt = {e: 0 for e in self.esem}
        self.dsem = {}
        self.dcnt = {}
        self.water = {e: {} for e in ENGS}
        self.mem = {e: [] for e in ENGS}
        self.barrier = []

    def _wait(self, eng, tok):
        if tok is None:
            return
        kind, name, val = tok
        if kind == "d":
            val = self.dcnt[name]
        key = (kind, name)
        w = self.water[eng]
        if w.get(key, 0) >= val:
            return
        w[key] = val
        self.items[eng].append(("wait", key, val))

    def _pre(self, eng, waits):
        for t in self.barrier:
            if not (t[0] == "e" and t[1] == eng):
                self._wait(eng, t)
        for t in waits:
            self._wait(eng, t)

    @staticmethod
    def _box(ap):
        try:
            t = ap.tensor.name
            dims = list(ap.ap)
            off = ap.offset
            span = 1
            for (step, count) in dims[1:]:
                span += (count - 1) * step
            return (t, off, off + span)
        except Exception:
            return None

    _WR0 = {"tensor_copy", "tensor_tensor", "tensor_add", "tensor_sub",
            "tensor_mul", "tensor_max", "tensor_scalar", "tensor_single_scalar",
            "tensor_scalar_add", "tensor_scalar_sub", "tensor_scalar_mul",
            "tensor_scalar_max", "tensor_scalar_min", "tensor_relu",
            "tensor_reduce", "reduce_max", "reduce_sum", "scalar_tensor_tensor",
            "tensor_tensor_scan", "memset", "memzero", "activation", "square",
            "sqrt", "mul", "copy", "add", "reciprocal", "select", "max_index",
            "sign"}

    def _self_dep_wait(self, eng, method, args, kw):
        wset, rset = [], []
        known = method in self._WR0
        if known:
            for j, a in enumerate(args):
                b = self._box(a) if hasattr(a, "tensor") else None
                if b is None:
                    continue
                (wset if j == 0 else rset).append(b)
            for k, a in kw.items():
                if hasattr(a, "tensor"):
                    b = self._box(a)
                    if b is not None:
                        (wset if k == "accum_out" else rset).append(b)
        mem = self.mem[eng]
        floor = self.water[eng].get(("e", eng), 0)
        need = 0
        if not known:
            need = self.ecnt[eng]
        else:
            def conflict(entry, mine_w, mine_r):
                t, lo, hi, cnt, is_w = entry
                for (t2, lo2, hi2) in mine_w:
                    if t2 == t and lo < hi2 and lo2 < hi:
                        return True
                if is_w:
                    for (t2, lo2, hi2) in mine_r:
                        if t2 == t and lo < hi2 and lo2 < hi:
                            return True
                return False
            keep = []
            for entry in mem:
                if entry[3] <= floor:
                    continue
                if conflict(entry, wset, rset):
                    need = max(need, entry[3])
                keep.append(entry)
            self.mem[eng] = mem = keep
        if need > 0:
            self._wait(eng, ("e", eng, need))
        return wset, rset, known

    def op(self, eng, method, *args, waits=(), inc=True, **kw):
        assert eng in ("tensor", "vector", "scalar", "gpsimd")
        if eng != "tensor":
            wset, rset, known = self._self_dep_wait(eng, method, args, kw)
        self._pre(eng, waits)
        tok = None
        if inc or eng != "tensor":
            self.ecnt[eng] += 1
            tok = ("e", eng, self.ecnt[eng])
            if eng != "tensor":
                n = self.ecnt[eng]
                if known:
                    for (t, lo, hi) in wset:
                        self.mem[eng].append((t, lo, hi, n, True))
                    for (t, lo, hi) in rset:
                        self.mem[eng].append((t, lo, hi, n, False))
                else:
                    self.mem[eng].append(("*", 0, 1 << 60, n, True))
        self.items[eng].append(("op", method, args, kw, tok))
        return tok if tok is not None else ("e", eng, self.ecnt[eng])

    def dma(self, q, out, in_, waits=(), slot="main"):
        assert q in ("sync", "gpsimd")
        if slot not in self.dsem:
            self.dsem[slot] = self.nc.alloc_semaphore(name=f"{self.tag}_s_{slot}")
            self.dcnt[slot] = 0
        self._pre(q, waits)
        self.dcnt[slot] += 16
        tok = ("d", slot, self.dcnt[slot])
        self.items[q].append(("dma", out, in_, tok))
        return tok

    def wait_only(self, eng, toks):
        for t in toks:
            self._wait(eng, t)

    def tok(self, eng):
        return ("e", eng, self.ecnt[eng])

    def set_barrier(self):
        self.barrier = ([("e", e, c) for e, c in self.ecnt.items() if c > 0]
                        + [("d", s, c) for s, c in self.dcnt.items() if c > 0])

    def _sem(self, key):
        kind, name = key
        return self.esem[name] if kind == "e" else self.dsem[name]

    def replay(self):
        with self.nc.Block() as blk:
            for eng in ENGS:
                items = self.items[eng]

                def make(items):
                    def run(e):
                        for it in items:
                            if it[0] == "wait":
                                e.wait_ge(self._sem(it[1]), it[2])
                            elif it[0] == "op":
                                inst = getattr(e, it[1])(*it[2], **it[3])
                                if it[4] is not None:
                                    inst.then_inc(self.esem[it[4][1]], 1)
                            else:
                                _, out, in_, tok = it
                                e.dma_start(out=out, in_=in_).then_inc(
                                    self.dsem[tok[1]], 16)
                    return run
                getattr(blk, eng)(make(items))


def mm_group(pb, parts, waits=()):
    n = len(parts)
    tok = None
    for k, (out, lh, rh, st_, sp_) in enumerate(parts):
        tok = pb.op("tensor", "matmul", out, lh, rh, start=st_, stop=sp_,
                    waits=(waits if k == 0 else ()), inc=(k == n - 1))
    return tok


class Rot:

    def __init__(self, banks):
        self.banks = banks
        self.evac = [None] * len(banks)
        self.i = 0

    def next(self):
        b = self.i % len(self.banks)
        self.i += 1
        return self.banks[b], self.evac[b], b

    def done(self, b, tok):
        self.evac[b] = tok


class Arena:
    def __init__(self, handle, ncols):
        self.h, self.ncols, self.off = handle, ncols, 0

    def take(self, n):
        assert self.off + n <= self.ncols, f"arena overflow {self.off}+{n}>{self.ncols}"
        o = self.off
        self.off += n
        return o

    def view(self, off, n):
        return self.h[:, off:off + n]


ACT_CONSTS = [1e-5, -0.5, 1e-6, 1.0]


def alloc_common(nc, pb, io, bf_cols=74000, f32_cols=16000):
    ar = {}
    ar["bf"] = Arena(nc.alloc_sbuf_tensor("arena_bf", [128, bf_cols], BF16), bf_cols)
    ar["f32"] = Arena(nc.alloc_sbuf_tensor("arena_f32", [128, f32_cols], F32), f32_cols)
    ar["P"] = [nc.alloc_psum_tensor(f"pr{i}", [128, 512], F32) for i in range(4)]
    ar["PTR"] = nc.alloc_psum_tensor("ptr", [128, 1024], BF16)
    ar["PZ"] = nc.alloc_psum_tensor("pz", [128, 512], F32)
    ar["P67"] = nc.alloc_psum_tensor("p67", [128, 1024], F32)
    f32, bf = ar["f32"], ar["bf"]
    o_c = f32.take(len(ACT_CONSTS))
    for j, v in enumerate(ACT_CONSTS):
        nc.const_aps.register(F32, v, f32.view(o_c + j, 1))
        pb.op("gpsimd", "memset", f32.view(o_c + j, 1), v)
    o_ident = bf.take(128)
    ar["o_ident"] = o_ident
    ar["t_ident"] = pb.dma("gpsimd", bf.view(o_ident, 128), io["ident"][:, :],
                           slot="ident")
    ar["mark"] = (bf.off, f32.off)
    return ar


def arena_reset(ar):
    ar["bf"].off, ar["f32"].off = ar["mark"]


class LNKit:

    def __init__(self, pb, ar):
        self.pb, self.f32 = pb, ar["f32"]
        self.o_st = self.f32.take(32)
        self.o_sq = self.f32.take(D)

    def st1(self, c):
        return self.f32.view(self.o_st + c, 1)

    def st(self, c):
        return self.f32.view(self.o_st + c, 4)

    def m1(self, s):
        return self.f32.view(self.o_st + 8 + s, 1)

    def rstd1(self, s):
        return self.f32.view(self.o_st + 12 + s, 1)

    def ln(self, src_s, dst_s, extra_waits=(), dst2_s=None):
        pb = self.pb
        c_ssum, c_s2, c_m, c_rstd, c_var, c_tmp = 0, 4, 8, 12, 16, 20
        for s in range(NS):
            pb.op("scalar", "activation", self.f32.view(self.o_sq, D), src_s(s),
                  AF.Square, accum_out=self.st1(c_s2 + s),
                  waits=list(extra_waits))
        tsq = pb.tok("scalar")
        for s in range(NS):
            pb.op("vector", "reduce_sum", self.st1(c_ssum + s), src_s(s),
                  axis=AX.X, waits=list(extra_waits))
        pb.op("vector", "tensor_scalar_mul", self.st(c_m), self.st(c_ssum), 1.0 / D)
        pb.op("vector", "tensor_scalar_mul", self.st(c_var), self.st(c_s2), 1.0 / D,
              waits=[tsq])
        pb.op("vector", "tensor_mul", self.st(c_tmp), self.st(c_m), self.st(c_m))
        tvar = pb.op("vector", "tensor_sub", self.st(c_var), self.st(c_var),
                     self.st(c_tmp))
        pb.op("scalar", "activation", self.st(c_rstd), self.st(c_var), AF.Ln,
              bias=LN_EPS, waits=[tvar])
        trstd = pb.op("scalar", "activation", self.st(c_rstd), self.st(c_rstd),
                      AF.Exp, scale=-0.5)
        tok = None
        for s in range(NS):
            tok = pb.op("vector", "tensor_scalar", dst_s(s), src_s(s),
                        self.st1(c_m + s), self.st1(c_rstd + s),
                        ALU.subtract, ALU.mult, waits=[trstd])
        return tok


def transpose_tile(pb, ar, src_tm_off, dst_fm_off, waits, t_tr_last):
    bf, PTR = ar["bf"], ar["PTR"]
    fm3d = bf.view(dst_fm_off, NCK * TT).rearrange("p (c t) -> p c t", c=NCK)
    ptr3d = PTR[:, 0:1024].rearrange("p (c t) -> p c t", c=NCK)
    for s in range(NS):
        w0 = list(waits) + ([t_tr_last] if t_tr_last else []) + [ar["t_ident"]]
        ttr = None
        for ck in range(NCK):
            ttr = pb.op("tensor", "transpose", PTR[:, ck * 128:(ck + 1) * 128],
                        bf.view(src_tm_off + s * D + ck * 128, 128),
                        bf.view(ar["o_ident"], 128),
                        waits=(w0 if ck == 0 else ()), inc=(ck == NCK - 1))
        t_tr_last = pb.op("vector", "tensor_copy",
                          fm3d[:, :, s * 128:(s + 1) * 128], ptr3d, waits=[ttr])
    return t_tr_last


def tile_in(ap_hbm, i):
    return ap_hbm[i * TT:(i + 1) * TT, :].rearrange("(s p) f -> p s f", p=128)



def emit_l0a(pb, io, ar):
    P, PTR, PZ = ar["P"], ar["PTR"], ar["PZ"]
    bf, f32 = ar["bf"], ar["f32"]
    arena_reset(ar)

    o_wq = bf.take(NCK * D)
    o_wk = bf.take(NCK * D)
    o_wvT = bf.take(NCK * D)
    o_proj = bf.take(NCK * 256)
    o_w2q = bf.take(NCK * 256)
    o_w2k = bf.take(NCK * 256)
    o_ones = bf.take(8)
    o_xnfm = bf.take(NCK * TT)
    o_xntm = bf.take(NS * D)
    o_qp = bf.take(2 * S)
    o_kp = bf.take(NS * 256)
    o_vtm = bf.take(NS * D)
    o_kvb = bf.take(2 * D)
    o_eluB = bf.take(512)

    o_xt = f32.take(NS * D)
    o_attn = o_xt
    o_xt2 = f32.take(NS * D)
    o_eluA = f32.take(512)
    o_kvacc = f32.take(2 * D)
    o_xo = f32.take(NS * D)
    o_zi = f32.take(8)
    lnk = LNKit(pb, ar)

    tw = []
    for (off, key, pat) in ((o_wq, "wq", "(c p) d -> p c d"),
                            (o_wk, "wk", "(c p) d -> p c d"),
                            (o_wvT, "wvT", "(c p) d -> p c d"),
                            (o_proj, "proj", "(c p) q -> p c q")):
        n = NCK * (256 if key == "proj" else D)
        tw.append(pb.dma("gpsimd", bf.view(off, n),
                         io[key].rearrange(pat, p=128), slot="l0a_w"))
    t_ones = pb.op("gpsimd", "memset", bf.view(o_ones, 1), 1.0)
    tw.append(t_ones)

    rot = Rot(P)

    def group(parts_fn, width, evac_fn, extra=()):
        bank, banktok, bi = rot.next()
        w = list(extra) + ([banktok] if banktok else [])
        mmtok = mm_group(pb, parts_fn(bank), waits=w)
        evtok = evac_fn(bank, mmtok)
        rot.done(bi, evtok)
        return evtok

    def ev_copy_bf(dst_off, width):
        def f(bank, mmtok):
            return pb.op("vector", "tensor_copy", bf.view(dst_off, width),
                         bank[:, :width], waits=[mmtok])
        return f

    def ev_kv(dst_off, width, first):
        def f(bank, mmtok):
            if first:
                return pb.op("vector", "tensor_copy", f32.view(dst_off, width),
                             bank[:, :width], waits=[mmtok])
            return pb.op("vector", "tensor_add", f32.view(dst_off, width),
                         f32.view(dst_off, width), bank[:, :width], waits=[mmtok])
        return f

    def ev_elu(dst_off, width):
        def f(bank, mmtok):
            tmin = pb.op("vector", "tensor_scalar_min", f32.view(o_eluA, width),
                         bank[:, :width], 0.0, waits=[mmtok])
            texp = pb.op("scalar", "activation", bf.view(o_eluB, width),
                         f32.view(o_eluA, width), AF.Exp, waits=[tmin])
            pb.op("vector", "tensor_scalar_max", bf.view(dst_off, width),
                  bank[:, :width], 0.0)
            return pb.op("vector", "tensor_add", bf.view(dst_off, width),
                         bf.view(dst_off, width), bf.view(o_eluB, width),
                         waits=[texp])
        return f

    for (o_w, o_dst) in ((o_wq, o_w2q), (o_wk, o_w2k)):
        for db in range(NCK):
            group(lambda bank, o_w=o_w, db=db: [
                (bank[:, :256], bf.view(o_w + ck * D + db * 128, 128),
                 bf.view(o_proj + ck * 256, 256), ck == 0, ck == NCK - 1)
                for ck in range(NCK)],
                256, ev_copy_bf(o_dst + db * 256, 256), extra=tw)

    t_groups_done = None
    t_apply_prev = None
    t_tr_last = None
    for i in range(NTL):
        lw = [t_apply_prev] if t_apply_prev else []
        tl = pb.dma("sync", f32.view(o_xt, NS * D), tile_in(io["x_in"], i),
                    waits=lw, slot="l0a_xt")
        t_apply = lnk.ln(lambda s: f32.view(o_xt + s * D, D),
                         lambda s: bf.view(o_xntm + s * D, D),
                         extra_waits=[tl])
        t_apply_prev = t_apply
        w0 = [t_apply] + ([t_groups_done] if t_groups_done else [])
        t_tr_last = transpose_tile(pb, ar, o_xntm, o_xnfm, w0, t_tr_last)
        t_xnfm = t_tr_last
        for half in range(2):
            group(lambda bank, half=half: [
                (bank[:, :TT], bf.view(o_w2q + ck * 256 + half * 128, 128),
                 bf.view(o_xnfm + ck * TT, TT), ck == 0, ck == NCK - 1)
                for ck in range(NCK)],
                TT, ev_elu(o_qp + half * S + i * TT, TT), extra=[t_xnfm])
        for s in range(NS):
            group(lambda bank, s=s: [
                (bank[:, :256], bf.view(o_xnfm + ck * TT + s * 128, 128),
                 bf.view(o_w2k + ck * 256, 256), ck == 0, ck == NCK - 1)
                for ck in range(NCK)],
                256, ev_elu(o_kp + s * 256, 256), extra=[t_xnfm])
        for s in range(NS):
            for h in range(2):
                group(lambda bank, s=s, h=h: [
                    (bank[:, :512], bf.view(o_xnfm + ck * TT + s * 128, 128),
                     bf.view(o_wvT + ck * D + h * 512, 512), ck == 0, ck == NCK - 1)
                    for ck in range(NCK)],
                    512, ev_copy_bf(o_vtm + s * D + h * 512, 512), extra=[t_xnfm])
        t_ready = pb.tok("vector")
        for ph in range(2):
            for dh in range(2):
                group(lambda bank, ph=ph, dh=dh: [
                    (bank[:, :512], bf.view(o_kp + s * 256 + ph * 128, 128),
                     bf.view(o_vtm + s * D + dh * 512, 512), s == 0, s == NS - 1)
                    for s in range(NS)],
                    512, ev_kv(o_kvacc + ph * D + dh * 512, 512, i == 0),
                    extra=[t_ready])
        t_groups_done = pb.tok("tensor")

    t_kvb = None
    for q in range(4):
        t_kvb = pb.op("vector", "tensor_copy", bf.view(o_kvb + q * 512, 512),
                      f32.view(o_kvacc + q * 512, 512))

    t_zinv_prev = None
    t_res_prev = None
    t_store = [None] * NTL
    for i in range(NTL):
        lw = [t_res_prev] if t_res_prev else []
        tl2 = pb.dma("sync", f32.view(o_xt2, NS * D), tile_in(io["x_in"], i),
                     waits=lw, slot="l0a_xt2")
        zparts = []
        for s in range(NS):
            for pc in range(2):
                zparts.append((PZ[:, s:s + 1],
                               bf.view(o_qp + pc * S + i * TT + s * 128, 128),
                               bf.view(o_ones, 1), pc == 0, pc == 1))
        zw = [t_kvb] + ([t_zinv_prev] if t_zinv_prev else [])
        tzmm = mm_group(pb, zparts, waits=zw)
        pb.op("vector", "tensor_scalar_add", f32.view(o_zi, 4),
              PZ[:, 0:4], 1e-8, waits=[tzmm])
        t_zinv_prev = pb.op("vector", "reciprocal", f32.view(o_zi, 4),
                            f32.view(o_zi, 4))
        for s in range(NS):
            for h in range(2):
                def ev(bank, mmtok, s=s, h=h):
                    return pb.op("vector", "tensor_single_scalar",
                                 f32.view(o_attn + s * D + h * 512, 512),
                                 bank[:, :512], f32.view(o_zi + s, 1), ALU.mult,
                                 waits=[mmtok])
                group(lambda bank, s=s, h=h: [
                    (bank[:, :512], bf.view(o_qp + pc * S + i * TT + s * 128, 128),
                     bf.view(o_kvb + pc * D + h * 512, 512), pc == 0, pc == 1)
                    for pc in range(2)],
                    512, ev)
        t_attn = pb.tok("vector")
        if i > 0:
            pb.wait_only("vector", [t_store[i - 1]])
        lnk.ln(lambda s: f32.view(o_attn + s * D, D),
               lambda s: f32.view(o_xo + s * D, D), extra_waits=[t_attn])
        tres = None
        for s in range(NS):
            tres = pb.op("vector", "tensor_add", f32.view(o_xo + s * D, D),
                         f32.view(o_xo + s * D, D), f32.view(o_xt2 + s * D, D),
                         waits=[tl2])
        t_res_prev = tres
        t_store[i] = pb.dma("sync", tile_in(io["x_out"], i),
                            f32.view(o_xo, NS * D), waits=[tres], slot="l0a_xo")
    return t_store[-1]



def emit_l0f(pb, io, ar):
    P, PZ = ar["P"], ar["PZ"]
    bf, f32 = ar["bf"], ar["f32"]
    arena_reset(ar)
    F = 2048
    NFC = F // 128

    o_mask = f32.take(NTL * NS * 8)
    o_iota = f32.take(8)
    lnk = LNKit(pb, ar)
    o_logit = f32.take(NS * 8)
    o_mx = f32.take(4)
    o_mx8 = f32.take(8)
    o_zero8 = f32.take(8)
    o_idx = f32.take(4)
    o_sc = f32.take(4)
    idx_u32 = pb.nc.alloc_sbuf_tensor("l0f_idx", [128, 8], mybir.dt.uint32)
    o_acc = f32.take(NS * D)
    o_xt = f32.take(NS * D)
    o_xo = f32.take(NS * D)

    o_gwf = f32.take(NCK * 8)
    o_xcfm = f32.take(NCK * 128)
    o_identf = f32.take(128)
    mark2 = (bf.off, f32.off)

    o_xntm = bf.take(NS * D)
    o_xstm = bf.take(NS * D)
    o_xnfm = bf.take(NCK * TT)
    o_xsfm = bf.take(NCK * TT)

    t_gw = pb.dma("sync", f32.view(o_gwf, NCK * 8),
                  io["gwT"].rearrange("(c p) e -> p c e", p=128), slot="l0f_gw")
    t_idf = pb.op("vector", "tensor_copy", f32.view(o_identf, 128),
                  bf.view(ar["o_ident"], 128), waits=[ar["t_ident"]])
    t_iota = None
    for e in range(8):
        t_iota = pb.op("gpsimd", "memset", f32.view(o_iota + e, 1), float(e))
    t_iota = pb.op("gpsimd", "memset", f32.view(o_zero8, 8), 0.0)

    rot = Rot(P)

    def group(parts_fn, width, evac_fn, extra=()):
        bank, banktok, bi = rot.next()
        w = list(extra) + ([banktok] if banktok else [])
        mmtok = mm_group(pb, parts_fn(bank), waits=w)
        evtok = evac_fn(bank, mmtok)
        rot.done(bi, evtok)
        return evtok

    xnfm_hbm = io["xnfm_h"].rearrange("c p t -> p c t")
    xsfm_hbm = io["xsfm_h"].rearrange("c p t -> p c t")

    t_apply_prev = None
    t_tr_last = None
    t_pz_last = None
    t_xfst = [None] * NTL
    t_xsst = [None] * NTL
    t_route_pe = None
    for i in range(NTL):
        lw = ([t_apply_prev] if t_apply_prev else []) \
            + ([t_route_pe] if t_route_pe else [])
        tl = pb.dma("sync", f32.view(o_xt, NS * D), tile_in(io["x_in"], i),
                    waits=lw, slot="l0f_xt")
        t_apply = lnk.ln(lambda s: f32.view(o_xt + s * D, D),
                         lambda s: bf.view(o_xntm + s * D, D), extra_waits=[tl])
        w0 = [t_apply] + ([t_xfst[i - 1]] if i > 0 else [])
        t_tr_last = transpose_tile(pb, ar, o_xntm, o_xnfm, w0, t_tr_last)
        t_xnfm = t_tr_last
        for s in range(NS):
            pb.op("vector", "tensor_single_scalar", f32.view(o_xt + s * D, D),
                  f32.view(o_xt + s * D, D), lnk.m1(s), ALU.subtract,
                  waits=[t_apply])
        t_rt = None
        for s in range(NS):
            for half in range(2):
                w0 = [pb.tok("vector")] + ([t_pz_last] if t_pz_last else [])
                ttr = None
                for k in range(4):
                    ck = half * 4 + k
                    ttr = pb.op("tensor", "transpose",
                                PZ[:, k * 128:(k + 1) * 128],
                                f32.view(o_xt + s * D + ck * 128, 128),
                                f32.view(o_identf, 128),
                                waits=(w0 + [t_idf] if k == 0 else ()),
                                inc=(k == 3))
                t_pz_last = pb.op("vector", "tensor_copy",
                                  f32.view(o_xcfm + half * 512, 512),
                                  PZ[:, 0:512], waits=[ttr])
            def ev_logit(bank, mmtok, s=s):
                return pb.op("vector", "tensor_single_scalar",
                             f32.view(o_logit + s * 8, 8), bank[:, :8],
                             lnk.rstd1(s), ALU.mult, waits=[mmtok])
            group(lambda bank, s=s: [
                (bank[:, :8], f32.view(o_xcfm + ck * 128, 128),
                 f32.view(o_gwf + ck * 8, 8), ck == 0, ck == NCK - 1)
                for ck in range(NCK)],
                8, ev_logit, extra=[t_pz_last, t_gw])
            pb.op("vector", "reduce_max", f32.view(o_mx + s, 1),
                  f32.view(o_logit + s * 8, 8), axis=AX.X)
            pb.op("vector", "tensor_single_scalar", f32.view(o_mx8, 8),
                  f32.view(o_zero8, 8), f32.view(o_mx + s, 1), ALU.add,
                  waits=[t_iota])
            pb.op("vector", "max_index", idx_u32[:, 0:8],
                  f32.view(o_mx8, 8), f32.view(o_logit + s * 8, 8))
            pb.op("vector", "tensor_copy", f32.view(o_idx + s, 1),
                  idx_u32[:, 0:1])
            tsg = pb.op("scalar", "activation", f32.view(o_sc + s, 1),
                        f32.view(o_mx + s, 1), AF.Sigmoid,
                        waits=[pb.tok("vector")])
            t_rt = pb.op("vector", "tensor_single_scalar",
                         f32.view(o_mask + (i * NS + s) * 8, 8),
                         f32.view(o_iota, 8),
                         f32.view(o_idx + s, 1), ALU.is_equal)
            pb.op("vector", "tensor_single_scalar", bf.view(o_xstm + s * D, D),
                  bf.view(o_xntm + s * D, D), f32.view(o_sc + s, 1), ALU.mult,
                  waits=[tsg])
        t_xs = pb.tok("vector")
        t_apply_prev = t_xs
        t_route_pe = pb.tok("tensor")
        w1 = [t_xs] + ([t_xsst[i - 1]] if i > 0 else [])
        t_tr_last = transpose_tile(pb, ar, o_xstm, o_xsfm, w1, t_tr_last)
        t_xfst[i] = pb.dma("sync", xnfm_hbm[:, :, i * TT:(i + 1) * TT],
                           bf.view(o_xnfm, NCK * TT).rearrange("p (c t) -> p c t", c=NCK),
                           waits=[t_xnfm], slot="l0f_xnst")
        t_xsst[i] = pb.dma("sync", xsfm_hbm[:, :, i * TT:(i + 1) * TT],
                           bf.view(o_xsfm, NCK * TT).rearrange("p (c t) -> p c t", c=NCK),
                           waits=[t_tr_last], slot="l0f_xsst")
    t_stageA_pe = pb.tok("tensor")
    t_stageA_ve = pb.tok("vector")

    bf.off, f32.off = mark2
    o_wg = bf.take(NCK * F)
    o_wu = bf.take(NCK * F)
    o_wd = bf.take(NFC * D)
    o_xf = bf.take(NCK * TT)
    o_gs = bf.take(4 * TT)
    o_h = bf.take(NFC * TT)

    t_accst = [None] * NTL
    t_pe_pass_end = t_stageA_pe
    o_xf2 = bf.take(NCK * TT)
    o_acc2 = o_xt
    xf_off = {0: o_xf, 1: o_xf2}
    acc_off = {0: o_acc, 1: o_acc2}
    t_pe_par = {0: None, 1: None}
    t_ve_par = {0: None, 1: None}
    t_st_par = {0: None, 1: None}

    def issue_loads(e, i):
        p = i % 2
        src_h = xnfm_hbm if e == 0 else xsfm_hbm
        lw = [t_xfst[i] if e == 0 else t_xsst[i]]
        if t_pe_par[p]:
            lw.append(t_pe_par[p])
        txf = pb.dma("sync", bf.view(xf_off[p], NCK * TT)
                     .rearrange("p (c t) -> p c t", c=NCK),
                     src_h[:, :, i * TT:(i + 1) * TT], waits=lw, slot="l0f_xf")
        tacc = None
        if e > 0:
            aw = [t_accst[i]]
            for t in (t_ve_par[p], t_st_par[p]):
                if t:
                    aw.append(t)
            tacc = pb.dma("sync", f32.view(acc_off[p], NS * D),
                          tile_in(io["acc_h"], i), waits=aw, slot="l0f_acc")
        return txf, tacc

    for e in range(9):
        wld = [t_pe_pass_end, t_stageA_ve, t_xfst[-1], t_xsst[-1]]
        twg = pb.dma("gpsimd", bf.view(o_wg, NCK * F),
                     io["gate_all"][e].rearrange("(c p) f -> p c f", p=128),
                     waits=wld, slot="l0f_w")
        twu = pb.dma("gpsimd", bf.view(o_wu, NCK * F),
                     io["up_all"][e].rearrange("(c p) f -> p c f", p=128),
                     slot="l0f_w")
        twd = pb.dma("gpsimd", bf.view(o_wd, NFC * D),
                     io["down_all"][e].rearrange("(c p) h -> p c h", p=128),
                     slot="l0f_w")
        pend = issue_loads(e, 0)
        for i in range(NTL):
            p = i % 2
            t_xf, t_acc = pend
            if i + 1 < NTL:
                pend = issue_loads(e, i + 1)
            o_xfc = xf_off[p]
            o_accc = acc_off[p]
            t_sil = [None] * NFC
            t_uev = [None] * NFC
            for cm in range(NFC):
                def ev_g(bank, mmtok, cm=cm):
                    w = [mmtok] + ([t_uev[cm - 4]] if cm >= 4 else [])
                    t_sil[cm] = pb.op("scalar", "activation",
                                      bf.view(o_gs + (cm % 4) * TT, TT), bank[:, :TT],
                                      AF.Silu, waits=w)
                    return t_sil[cm]
                group(lambda bank, cm=cm: [
                    (bank[:, :TT], bf.view(o_wg + ck * F + cm * 128, 128),
                     bf.view(o_xfc + ck * TT, TT), ck == 0, ck == NCK - 1)
                    for ck in range(NCK)],
                    TT, ev_g, extra=[t_xf, twg])
                def ev_u(bank, mmtok, cm=cm):
                    t_uev[cm] = pb.op("vector", "tensor_mul",
                                      bf.view(o_h + cm * TT, TT),
                                      bf.view(o_gs + (cm % 4) * TT, TT),
                                      bank[:, :TT], waits=[mmtok, t_sil[cm]])
                    return t_uev[cm]
                group(lambda bank, cm=cm: [
                    (bank[:, :TT], bf.view(o_wu + ck * F + cm * 128, 128),
                     bf.view(o_xfc + ck * TT, TT), ck == 0, ck == NCK - 1)
                    for ck in range(NCK)],
                    TT, ev_u, extra=[t_xf, twu])
            t_h = pb.tok("vector")
            for s in range(NS):
                for hh in range(2):
                    def ev_d(bank, mmtok, s=s, hh=hh):
                        dst = f32.view(o_accc + s * D + hh * 512, 512)
                        if e == 0:
                            w0 = [mmtok] + ([t_st_par[p]] if t_st_par[p] else [])
                            return pb.op("vector", "tensor_copy", dst,
                                         bank[:, :512], waits=w0)
                        mcol = f32.view(o_mask + (i * NS + s) * 8 + (e - 1), 1)
                        return pb.op("vector", "scalar_tensor_tensor", dst,
                                     bank[:, :512], mcol, dst,
                                     ALU.mult, ALU.add, waits=[mmtok, t_acc])
                    group(lambda bank, s=s, hh=hh: [
                        (bank[:, :512], bf.view(o_h + fc * TT + s * 128, 128),
                         bf.view(o_wd + fc * D + hh * 512, 512), fc == 0, fc == NFC - 1)
                        for fc in range(NFC)],
                        512, ev_d, extra=[t_h, twd])
            t_dn = pb.tok("vector")
            t_accst[i] = pb.dma("gpsimd", tile_in(io["acc_h"], i),
                                f32.view(o_accc, NS * D), waits=[t_dn],
                                slot="l0f_accst")
            t_st_par[p] = t_accst[i]
            t_ve_par[p] = t_dn
            t_pe_par[p] = pb.tok("tensor")
        t_pe_pass_end = pb.tok("tensor")

    t_store = [None] * NTL
    t_ep_prev = None
    for i in range(NTL):
        lw = [t_accst[i]] + ([t_ep_prev] if t_ep_prev else [])
        ta = pb.dma("sync", f32.view(o_acc, NS * D), tile_in(io["acc_h"], i),
                    waits=lw, slot="l0f_acc")
        tl = pb.dma("sync", f32.view(o_xt, NS * D), tile_in(io["x_in"], i),
                    waits=lw, slot="l0f_xt")
        if i > 0:
            pb.wait_only("vector", [t_store[i - 1]])
        lnk.ln(lambda s: f32.view(o_acc + s * D, D),
               lambda s: f32.view(o_xo + s * D, D), extra_waits=[ta])
        tres = None
        for s in range(NS):
            tres = pb.op("vector", "tensor_add", f32.view(o_xo + s * D, D),
                         f32.view(o_xo + s * D, D), f32.view(o_xt + s * D, D),
                         waits=[tl])
        t_ep_prev = tres
        t_store[i] = pb.dma("sync", tile_in(io["x_out"], i),
                            f32.view(o_xo, NS * D), waits=[tres], slot="l0f_xo")
    return t_store[-1]



def emit_l1f(pb, io, ar):
    P, PTR, PZ = ar["P"], ar["PTR"], ar["PZ"]
    bf, f32 = ar["bf"], ar["f32"]
    arena_reset(ar)
    SH = TT + 1

    o_wrT = bf.take(NCK * D)
    o_wkT = bf.take(NCK * D)
    o_wvT = bf.take(NCK * D)
    o_xntm = bf.take(NS * D)
    o_xsh = bf.take(NCK * SH)
    o_xr = bf.take(NCK * TT)
    o_xk = bf.take(NCK * TT)
    o_rtm = bf.take(NS * D)
    o_kact = bf.take(NCK * TT)
    o_tmpb = bf.take(512)
    o_tmpc = bf.take(512)

    o_xt = f32.take(NS * D)
    o_xo = f32.take(NS * D)
    o_out = f32.take(NS * D)
    o_mu = f32.take(32)
    o_identf = f32.take(128)
    o_mstg = f32.take(128)
    lnk = LNKit(pb, ar)

    tw = []
    for (off, key) in ((o_wrT, "wrT"), (o_wkT, "wkT"), (o_wvT, "wvT")):
        tw.append(pb.dma("gpsimd", bf.view(off, NCK * D),
                         io[key].rearrange("(c p) d -> p c d", p=128),
                         slot="l1f_w"))
    t_idf = pb.op("vector", "tensor_copy", f32.view(o_identf, 128),
                  bf.view(ar["o_ident"], 128), waits=[ar["t_ident"]])

    t_mu = None
    for j, key in enumerate(("mu_r8", "mu_k8")):
        tm = pb.dma("sync", f32.view(o_mstg, 128)[0:8, :],
                    io[key][:, :], slot="l1f_mu", waits=[t_mu] if t_mu else ())
        ttr = pb.op("tensor", "transpose", PZ[:, 0:8],
                    f32.view(o_mstg, 128)[0:8, :],
                    f32.view(o_identf, 128)[0:8, 0:8], waits=[tm, t_idf], inc=True)
        t_mu = pb.op("vector", "tensor_copy", f32.view(o_mu + j * 8, 8),
                     PZ[:, 0:8], waits=[ttr])
        t_mu = pb.op("vector", "tensor_scalar", f32.view(o_mu + 16 + j * 8, 8),
                     f32.view(o_mu + j * 8, 8), -1.0, 1.0, ALU.mult, ALU.add)

    t_z = None
    for ck in range(NCK):
        t_z = pb.op("vector", "memset", bf.view(o_xsh + ck * SH, 1), 0.0)

    rot = Rot(P)

    def group(parts_fn, width, evac_fn, extra=()):
        bank, banktok, bi = rot.next()
        w = list(extra) + ([banktok] if banktok else [])
        mmtok = mm_group(pb, parts_fn(bank), waits=w)
        evtok = evac_fn(bank, mmtok)
        rot.done(bi, evtok)
        return evtok

    xsh3d = bf.view(o_xsh, NCK * SH).rearrange("p (c t) -> p c t", c=NCK)
    ptr3d = ar["PTR"][:, 0:1024].rearrange("p (c t) -> p c t", c=NCK)

    t_apply_prev = None
    t_tr_last = t_z
    t_pe_prev = None
    t_store = [None] * NTL
    for i in range(NTL):
        lw = ([t_apply_prev] if t_apply_prev else [])
        tl = pb.dma("sync", f32.view(o_xt, NS * D), tile_in(io["x_in"], i),
                    waits=lw, slot="l1f_xt")
        t_apply = lnk.ln(lambda s: f32.view(o_xt + s * D, D),
                         lambda s: bf.view(o_xntm + s * D, D), extra_waits=[tl])
        t_apply_prev = t_apply
        if i > 0:
            tcar = None
            for ck in range(NCK):
                tcar = pb.op("vector", "tensor_copy", bf.view(o_xsh + ck * SH, 1),
                             bf.view(o_xsh + ck * SH + TT, 1),
                             waits=[t_pe_prev] if (ck == 0 and t_pe_prev) else ())
            t_tr_last = tcar
        for s in range(NS):
            w0 = [t_apply] + ([t_tr_last] if t_tr_last else []) + [ar["t_ident"]]
            ttr = None
            for ck in range(NCK):
                ttr = pb.op("tensor", "transpose", PTR[:, ck * 128:(ck + 1) * 128],
                            bf.view(o_xntm + s * D + ck * 128, 128),
                            bf.view(ar["o_ident"], 128),
                            waits=(w0 if ck == 0 else ()), inc=(ck == NCK - 1))
            t_tr_last = pb.op("vector", "tensor_copy",
                              xsh3d[:, :, 1 + s * 128:1 + (s + 1) * 128], ptr3d,
                              waits=[ttr])
        t_mix = None
        for ck in range(NCK):
            for (o_dst, cmu) in ((o_xr, 0), (o_xk, 8)):
                pb.op("gpsimd", "tensor_single_scalar", bf.view(o_tmpc, 512),
                      bf.view(o_xsh + ck * SH, TT), f32.view(o_mu + 16 + cmu + ck, 1),
                      ALU.mult, waits=[t_tr_last, t_mu])
                pb.op("gpsimd", "tensor_single_scalar", bf.view(o_dst + ck * TT, TT),
                      bf.view(o_xsh + ck * SH + 1, TT),
                      f32.view(o_mu + cmu + ck, 1), ALU.mult)
                t_mix = pb.op("gpsimd", "tensor_add", bf.view(o_dst + ck * TT, TT),
                              bf.view(o_dst + ck * TT, TT), bf.view(o_tmpc, 512))
        for s in range(NS):
            for hh in range(2):
                def ev_r(bank, mmtok, s=s, hh=hh):
                    return pb.op("scalar", "activation",
                                 bf.view(o_rtm + s * D + hh * 512, 512),
                                 bank[:, :512], AF.Sigmoid, waits=[mmtok])
                group(lambda bank, s=s, hh=hh: [
                    (bank[:, :512], bf.view(o_xr + ck * TT + s * 128, 128),
                     bf.view(o_wrT + ck * D + hh * 512, 512), ck == 0, ck == NCK - 1)
                    for ck in range(NCK)],
                    512, ev_r, extra=[t_mix, tw[0]])
        for cm in range(NCK):
            def ev_k(bank, mmtok, cm=cm):
                pb.op("vector", "tensor_scalar_max", bf.view(o_tmpb, 512),
                      bank[:, :512], 0.0, waits=[mmtok])
                return pb.op("vector", "tensor_mul", bf.view(o_kact + cm * TT, TT),
                             bf.view(o_tmpb, 512), bf.view(o_tmpb, 512))
            group(lambda bank, cm=cm: [
                (bank[:, :512], bf.view(o_wkT + ck * D + cm * 128, 128),
                 bf.view(o_xk + ck * TT, TT), ck == 0, ck == NCK - 1)
                for ck in range(NCK)],
                512, ev_k, extra=[t_mix, tw[1]])
        t_kact = pb.tok("vector")
        for s in range(NS):
            for hh in range(2):
                def ev_o(bank, mmtok, s=s, hh=hh):
                    return pb.op("vector", "tensor_mul",
                                 f32.view(o_out + s * D + hh * 512, 512),
                                 bf.view(o_rtm + s * D + hh * 512, 512),
                                 bank[:, :512], waits=[mmtok])
                group(lambda bank, s=s, hh=hh: [
                    (bank[:, :512], bf.view(o_kact + ck * TT + s * 128, 128),
                     bf.view(o_wvT + ck * D + hh * 512, 512), ck == 0, ck == NCK - 1)
                    for ck in range(NCK)],
                    512, ev_o, extra=[t_kact, tw[2]])
        t_out = pb.tok("vector")
        t_pe_prev = pb.tok("tensor")
        if i > 0:
            pb.wait_only("vector", [t_store[i - 1]])
        lnk.ln(lambda s: f32.view(o_out + s * D, D),
               lambda s: f32.view(o_xo + s * D, D), extra_waits=[t_out])
        tres = None
        for s in range(NS):
            tres = pb.op("vector", "tensor_add", f32.view(o_xo + s * D, D),
                         f32.view(o_xo + s * D, D), f32.view(o_xt + s * D, D),
                         waits=[tl])
        t_store[i] = pb.dma("sync", tile_in(io["x_out"], i),
                            f32.view(o_xo, NS * D), waits=[tres], slot="l1f_xo")
    return t_store[-1]



def emit_l1a(pb, io, ar):
    P, PTR, PZ = ar["P"], ar["PTR"], ar["PZ"]
    bf, f32 = ar["bf"], ar["f32"]
    arena_reset(ar)
    NH, HTT = 16, 256

    o_wq = bf.take(NCK * D)
    o_wk = bf.take(NCK * D)
    o_wv = bf.take(NCK * D)
    o_wo = bf.take(NCK * D)
    o_wg1 = bf.take(NCK * 16)
    o_wg2 = bf.take(D)
    o_ctab = bf.take(TT)
    o_tri = bf.take(512)
    o_xnfm = bf.take(NCK * TT)
    o_qR = o_xnfm
    o_xntm = bf.take(NS * D)
    o_qsil = bf.take(NCK * TT)
    o_gate = o_qsil
    o_E = bf.take(NCK * TT)
    o_onfm = o_E
    o_t1 = bf.take(HTT)
    o_ELr = bf.take(NH * HTT)
    o_ELrtm = bf.take(2 * 2048)
    o_vtm = bf.take(NS * D)
    o_Ub = bf.take(NH * 64)
    o_Asb = bf.take(NH * 128)
    o_g1 = bf.take(512)
    o_otm = bf.take(NS * D)

    o_xt = f32.take(NS * D)
    o_xo = f32.take(NS * D)
    o_C = f32.take(NCK * TT)
    o_out = o_C
    o_carry = f32.take(8)
    o_U = f32.take(NH * 64)
    o_nwf = f32.take(8)
    o_rms = f32.take(8)
    o_identf = f32.take(128)
    o_stg = f32.take(128)
    lnk = LNKit(pb, ar)

    tw = []
    for (off, key, wdt) in ((o_wq, "wq", D), (o_wk, "wk", D), (o_wv, "wv", D),
                            (o_wo, "wo", D), (o_wg1, "wg1", 16)):
        tw.append(pb.dma("gpsimd", bf.view(off, NCK * wdt),
                         io[key].rearrange("(c p) d -> p c d", p=128),
                         slot="l1a_w"))
    tw.append(pb.dma("gpsimd", bf.h[0:16, o_wg2:o_wg2 + D], io["wg2"][:, :],
                     slot="l1a_w"))
    o_dup = bf.take(128)
    tw.append(pb.dma("gpsimd", bf.view(o_dup, 128), io["dup"][:, :],
                     slot="l1a_w"))
    t_tri = pb.dma("gpsimd", bf.view(o_tri, 512), io["tri4"][:, :],
                   slot="l1a_w")
    t_idf = pb.op("vector", "tensor_copy", f32.view(o_identf, 128),
                  bf.view(ar["o_ident"], 128), waits=[ar["t_ident"]])
    tnw0 = pb.dma("sync", f32.view(o_stg, 128)[0:8, :], io["nw8"][:, :],
                  slot="l1a_nw")
    tnw1 = pb.op("tensor", "transpose", PZ[:, 0:8], f32.view(o_stg, 128)[0:8, :],
                 f32.view(o_identf, 128)[0:8, 0:8], waits=[tnw0, t_idf], inc=True)
    t_nw = pb.op("vector", "tensor_copy", f32.view(o_nwf, 8), PZ[:, 0:8],
                 waits=[tnw1])
    t_u = None
    for g in range(2):
        pb.op("vector", "memset", f32.view(o_U + g * 512, 512), 0.0)
        t_u = pb.op("vector", "memset", bf.view(o_Ub + g * 512, 512), 0.0)

    rot = Rot(P)

    def group(parts_fn, width, evac_fn, extra=()):
        bank, banktok, bi = rot.next()
        w = list(extra) + ([banktok] if banktok else [])
        mmtok = mm_group(pb, parts_fn(bank), waits=w)
        evtok = evac_fn(bank, mmtok)
        rot.done(bi, evtok)
        return evtok

    t_apply_prev = None
    t_tr_last = None
    t_store = [None] * NTL
    t_lrpe_prev = []
    t_Ubf = t_u
    for i in range(NTL):
        lw = ([t_apply_prev] if t_apply_prev else [])
        tl = pb.dma("sync", f32.view(o_xt, NS * D), tile_in(io["x_in"], i),
                    waits=lw, slot="l1a_xt")
        tct = pb.dma("gpsimd", bf.view(o_ctab, TT),
                     io["costab"][:, i * TT:(i + 1) * TT],
                     waits=list(t_lrpe_prev), slot="l1a_ct")
        t_apply = lnk.ln(lambda s: f32.view(o_xt + s * D, D),
                         lambda s: bf.view(o_xntm + s * D, D), extra_waits=[tl])
        t_apply_prev = t_apply
        t_tr_last = transpose_tile(pb, ar, o_xntm, o_xnfm, [t_apply], t_tr_last)
        t_xnfm = t_tr_last
        tE = [None] * NCK
        for cm in range(NCK):
            def ev_q(bank, mmtok, cm=cm):
                return pb.op("scalar", "activation", bf.view(o_qsil + cm * TT, TT),
                             bank[:, :TT], AF.Silu, waits=[mmtok])
            group(lambda bank, cm=cm: [
                (bank[:, :TT], bf.view(o_wq + ck * D + cm * 128, 128),
                 bf.view(o_xnfm + ck * TT, TT), ck == 0, ck == NCK - 1)
                for ck in range(NCK)],
                TT, ev_q, extra=[t_xnfm, tw[0]])
            def ev_e(bank, mmtok, cm=cm):
                tE[cm] = pb.op("scalar", "activation", bf.view(o_E + cm * TT, TT),
                               bank[:, :TT], AF.Exp, waits=[mmtok])
                return tE[cm]
            group(lambda bank, cm=cm: [
                (bank[:, :TT], bf.view(o_wk + ck * D + cm * 128, 128),
                 bf.view(o_xnfm + ck * TT, TT), ck == 0, ck == NCK - 1)
                for ck in range(NCK)],
                TT, ev_e, extra=[t_xnfm, tw[1]])
        for s in range(NS):
            for hh in range(2):
                def ev_v(bank, mmtok, s=s, hh=hh):
                    return pb.op("vector", "tensor_copy",
                                 bf.view(o_vtm + s * D + hh * 512, 512),
                                 bank[:, :512], waits=[mmtok])
                group(lambda bank, s=s, hh=hh: [
                    (bank[:, :512], bf.view(o_xnfm + ck * TT + s * 128, 128),
                     bf.view(o_wv + ck * D + hh * 512, 512), ck == 0, ck == NCK - 1)
                    for ck in range(NCK)],
                    512, ev_v, extra=[t_xnfm, tw[2]])
        def ev_g1(bank, mmtok):
            return pb.op("vector", "tensor_copy", bf.h[0:16, o_g1:o_g1 + 512],
                         bank[0:16, :512], waits=[mmtok])
        group(lambda bank: [
            (bank[0:16, :512], bf.view(o_wg1 + ck * 16, 16),
             bf.view(o_xnfm + ck * TT, TT), ck == 0, ck == NCK - 1)
            for ck in range(NCK)],
            512, ev_g1, extra=[t_xnfm, tw[4]])
        t_qkv_pe = pb.tok("tensor")
        t_scan = None
        for ck in range(NCK):
            ini = 1.0 if i == 0 else f32.view(o_carry + ck, 1)
            pb.op("vector", "tensor_tensor_scan", f32.view(o_C + ck * TT, TT),
                  bf.view(o_E + ck * TT, TT), bf.view(o_E + ck * TT, TT), ini,
                  ALU.add, ALU.bypass, waits=[tE[ck]])
            t_scan = pb.op("vector", "tensor_copy", f32.view(o_carry + ck, 1),
                           f32.view(o_C + ck * TT + TT - 1, 1))
        tR = [None] * NCK
        for ck in range(NCK):
            pb.op("scalar", "activation", f32.view(o_C + ck * TT, TT),
                  f32.view(o_C + ck * TT, TT), AF.Ln, waits=[t_scan])
            tR[ck] = pb.op("scalar", "activation", f32.view(o_C + ck * TT, TT),
                           f32.view(o_C + ck * TT, TT), AF.Exp, scale=-1.0)

        for hf in range(2):
            t_q = None
            t_el = None
            ctab_v = bf.view(o_ctab + hf * HTT, HTT)
            for ck in range(NCK):
                tt1 = pb.op("vector", "tensor_mul", bf.view(o_t1, HTT),
                            bf.view(o_qsil + ck * TT + hf * HTT, HTT),
                            f32.view(o_C + ck * TT + hf * HTT, HTT),
                            waits=[tR[ck], t_qkv_pe, tct]
                            + ([t_q] if t_q else []))
                for hh in range(2):
                    h = 2 * ck + hh
                    b0 = hh * 64
                    def ev_qd(bank, mmtok, h=h):
                        return pb.op("vector", "tensor_mul",
                                     bf.view(o_qR + h * HTT, HTT),
                                     bank[:, :HTT], ctab_v,
                                     waits=[mmtok, tct])
                    t_q = group(lambda bank, b0=b0: [
                        (bank[:, :HTT], bf.h[b0:b0 + 64, o_dup:o_dup + 128],
                         bf.h[b0:b0 + 64, o_t1:o_t1 + HTT], True, True)],
                        HTT, ev_qd, extra=[tt1, tw[6]])
                    def ev_ed(bank, mmtok, h=h):
                        return pb.op("vector", "tensor_mul",
                                     bf.view(o_ELr + h * HTT, HTT),
                                     bank[:, :HTT], ctab_v,
                                     waits=[mmtok, tct])
                    t_el = group(lambda bank, b0=b0, ck=ck: [
                        (bank[:, :HTT], bf.h[b0:b0 + 64, o_dup:o_dup + 128],
                         bf.h[b0:b0 + 64, o_E + ck * TT + hf * HTT:o_E + ck * TT + hf * HTT + HTT],
                         True, True)],
                        HTT, ev_ed, extra=[tE[ck], tw[6]])
            t_lrpe_prev = [t_el, t_q]
            for sl in range(2):
                for bb in range(2):
                    w0 = [t_el, t_q] + ([t_tr_last] if t_tr_last else [])
                    ttr = None
                    for k in range(8):
                        h = bb * 8 + k
                        ttr = pb.op("tensor", "transpose",
                                    PTR[:, k * 128:(k + 1) * 128],
                                    bf.view(o_ELr + h * HTT + sl * 128, 128),
                                    bf.view(ar["o_ident"], 128),
                                    waits=(w0 if k == 0 else ()), inc=(k == 7))
                    t_tr_last = pb.op("vector", "tensor_copy",
                                      bf.view(o_ELrtm + sl * 2048 + bb * 1024, 1024),
                                      PTR[:, 0:1024], waits=[ttr])
            t_eltm = t_tr_last
            for c in range(2):
                sg = hf * 2 + c
                tA = [None] * 4
                for g4 in range(4):
                    def ev_a(bank, mmtok, g4=g4):
                        tA[g4] = pb.op("vector", "tensor_mul",
                                       bf.view(o_Asb + g4 * 512, 512),
                                       bf.view(o_tri, 512),
                                       bank[:, :512], waits=[mmtok, t_tri])
                        return tA[g4]
                    group(lambda bank, g4=g4: [
                        (bank[:, q * 128:(q + 1) * 128],
                         bf.view(o_ELr + (g4 * 4 + q) * HTT + c * 128, 128),
                         bf.view(o_qR + (g4 * 4 + q) * HTT + c * 128, 128),
                         True, True)
                        for q in range(4)],
                        512, ev_a, extra=[t_el, t_q])
                for hg in range(2):
                    def ev_o(bank, mmtok, hg=hg):
                        return pb.op("vector", "tensor_copy",
                                     bf.view(o_otm + sg * D + hg * 512, 512),
                                     bank[:, :512], waits=[mmtok])
                    def mk_parts(bank, hg=hg):
                        out = []
                        for h8 in range(8):
                            h = hg * 8 + h8
                            out.append((bank[:, h8 * 64:(h8 + 1) * 64],
                                        bf.view(o_Asb + h * 128, 128),
                                        bf.view(o_vtm + sg * D + h * 64, 64),
                                        True, False))
                            out.append((bank[:, h8 * 64:(h8 + 1) * 64],
                                        bf.view(o_qR + h * HTT + c * 128, 128),
                                        bf.view(o_Ub + h * 64, 64),
                                        False, True))
                        return out
                    group(mk_parts, 512, ev_o,
                          extra=[tA[hg * 2], tA[hg * 2 + 1], t_Ubf])
                t_o_pe = pb.tok("tensor")
                for hg in range(2):
                    def ev_ds(bank, mmtok, hg=hg):
                        pb.op("vector", "tensor_add",
                              f32.view(o_U + hg * 512, 512),
                              f32.view(o_U + hg * 512, 512), bank[:, :512],
                              waits=[mmtok])
                        return pb.op("vector", "tensor_copy",
                                     bf.view(o_Ub + hg * 512, 512),
                                     f32.view(o_U + hg * 512, 512),
                                     waits=[t_o_pe])
                    t_Ubf = group(lambda bank, hg=hg: [
                        (bank[:, h8 * 64:(h8 + 1) * 64],
                         bf.view(o_ELrtm + c * 2048 + (hg * 8 + h8) * 128, 128),
                         bf.view(o_vtm + sg * D + (hg * 8 + h8) * 64, 64),
                         True, True)
                        for h8 in range(8)],
                        512, ev_ds, extra=[t_eltm])

        t_half2 = pb.tok("vector")
        tgate = None
        for s in range(NS):
            for hh in range(2):
                def ev_gt(bank, mmtok, s=s, hh=hh):
                    return pb.op("scalar", "activation",
                                 bf.view(o_gate + s * D + hh * 512, 512),
                                 bank[:, :512], AF.Sigmoid, waits=[mmtok])
                tgate = group(lambda bank, s=s, hh=hh: [
                    (bank[:, :512], bf.h[0:16, o_g1 + s * 128:o_g1 + s * 128 + 128],
                     bf.h[0:16, o_wg2 + hh * 512:o_wg2 + hh * 512 + 512],
                     True, True)],
                    512, ev_gt, extra=[t_half2, tw[5]])
        tog = None
        for s in range(NS):
            for hh in range(2):
                tog = pb.op("vector", "tensor_mul",
                            bf.view(o_otm + s * D + hh * 512, 512),
                            bf.view(o_otm + s * D + hh * 512, 512),
                            bf.view(o_gate + s * D + hh * 512, 512),
                            waits=[tgate])
        tsq = None
        for s in range(NS):
            tsq = pb.op("scalar", "activation", f32.view(lnk.o_sq, D),
                        bf.view(o_otm + s * D, D), AF.Square,
                        accum_out=f32.view(o_rms + s, 1), waits=[tog])
        pb.op("vector", "tensor_scalar_mul", f32.view(o_rms, 4),
              f32.view(o_rms, 4), 1.0 / D, waits=[tsq])
        tms = pb.tok("vector")
        pb.op("scalar", "activation", f32.view(o_rms + 4, 4),
              f32.view(o_rms, 4), AF.Ln, bias=1e-6, waits=[tms])
        trst = pb.op("scalar", "activation", f32.view(o_rms + 4, 4),
                     f32.view(o_rms + 4, 4), AF.Exp, scale=-0.5)
        ton = None
        for s in range(NS):
            ton = pb.op("vector", "tensor_single_scalar",
                        bf.view(o_otm + s * D, D), bf.view(o_otm + s * D, D),
                        f32.view(o_rms + 4 + s, 1), ALU.mult, waits=[trst])
        t_tr_last = transpose_tile(pb, ar, o_otm, o_onfm, [ton], t_tr_last)
        tnw_ap = None
        for ck in range(NCK):
            tnw_ap = pb.op("gpsimd", "tensor_single_scalar",
                           bf.view(o_onfm + ck * TT, TT),
                           bf.view(o_onfm + ck * TT, TT),
                           f32.view(o_nwf + ck, 1), ALU.mult,
                           waits=[t_tr_last, t_nw])
        tout = None
        for s in range(NS):
            for hh in range(2):
                def ev_out(bank, mmtok, s=s, hh=hh):
                    return pb.op("vector", "tensor_copy",
                                 f32.view(o_out + s * D + hh * 512, 512),
                                 bank[:, :512], waits=[mmtok])
                tout = group(lambda bank, s=s, hh=hh: [
                    (bank[:, :512], bf.view(o_onfm + ck * TT + s * 128, 128),
                     bf.view(o_wo + ck * D + hh * 512, 512), ck == 0, ck == NCK - 1)
                    for ck in range(NCK)],
                    512, ev_out, extra=[tnw_ap, tw[3]])
        if i > 0:
            pb.wait_only("vector", [t_store[i - 1]])
        lnk.ln(lambda s: f32.view(o_out + s * D, D),
               lambda s: f32.view(o_xo + s * D, D), extra_waits=[tout])
        tres = None
        for s in range(NS):
            tres = pb.op("vector", "tensor_add", f32.view(o_xo + s * D, D),
                         f32.view(o_xo + s * D, D), f32.view(o_xt + s * D, D),
                         waits=[tl])
        t_store[i] = pb.dma("sync", tile_in(io["x_out"], i),
                            f32.view(o_xo, NS * D), waits=[tres], slot="l1a_xo")
    return t_store[-1]



PHASES = ("l0a", "l0f", "l1a", "l1f")


def build_program(phases=PHASES):
    nc = bass.Bass("TRN2", target_bir_lowering=False)
    dt = nc.dram_tensor
    io = {"ident": dt("ident", [128, 128], BF16, kind="ExternalInput").ap()}
    io["x_in"] = dt("x_in", [S, D], F32, kind="ExternalInput").ap()
    io["x_out"] = dt("x_out", [S, D], F32, kind="ExternalOutput").ap()
    n_mid = len(phases) - 1
    mids = [dt(f"mid{k}", [S, D], F32, kind="Internal").ap() for k in range(n_mid)]

    if "l0a" in phases:
        io["l0a"] = {
            "wq": dt("l0a_wq", [D, D], F32, kind="ExternalInput").ap(),
            "wk": dt("l0a_wk", [D, D], F32, kind="ExternalInput").ap(),
            "wvT": dt("l0a_wvT", [D, D], F32, kind="ExternalInput").ap(),
            "proj": dt("l0a_proj", [D, 256], F32, kind="ExternalInput").ap(),
        }
    if "l1a" in phases:
        io["l1a"] = {
            "wq": dt("l1a_wq", [D, D], F32, kind="ExternalInput").ap(),
            "wk": dt("l1a_wk", [D, D], F32, kind="ExternalInput").ap(),
            "wv": dt("l1a_wv", [D, D], F32, kind="ExternalInput").ap(),
            "wo": dt("l1a_wo", [D, D], F32, kind="ExternalInput").ap(),
            "wg1": dt("l1a_wg1", [D, 16], F32, kind="ExternalInput").ap(),
            "wg2": dt("l1a_wg2", [16, D], F32, kind="ExternalInput").ap(),
            "nw8": dt("l1a_nw8", [8, 128], F32, kind="ExternalInput").ap(),
            "costab": dt("l1a_costab", [128, S], F32, kind="ExternalInput").ap(),
            "tri4": dt("l1a_tri4", [128, 512], BF16, kind="ExternalInput").ap(),
            "dup": dt("l1a_dup", [128, 128], BF16, kind="ExternalInput").ap(),
        }
    if "l1f" in phases:
        io["l1f"] = {
            "wrT": dt("l1f_wrT", [D, D], F32, kind="ExternalInput").ap(),
            "wkT": dt("l1f_wkT", [D, D], F32, kind="ExternalInput").ap(),
            "wvT": dt("l1f_wvT", [D, D], F32, kind="ExternalInput").ap(),
            "mu_r8": dt("l1f_mur", [8, 128], F32, kind="ExternalInput").ap(),
            "mu_k8": dt("l1f_muk", [8, 128], F32, kind="ExternalInput").ap(),
        }
    if "l0f" in phases:
        io["l0f"] = {
            "gate_all": dt("l0f_gate", [9, D, 2048], F32, kind="ExternalInput").ap(),
            "up_all": dt("l0f_up", [9, D, 2048], F32, kind="ExternalInput").ap(),
            "down_all": dt("l0f_down", [9, 2048, D], F32, kind="ExternalInput").ap(),
            "gwT": dt("l0f_gwT", [D, 8], F32, kind="ExternalInput").ap(),
            "xnfm_h": dt("l0f_xnfm", [NCK, 128, S], BF16, kind="Internal").ap(),
            "xsfm_h": dt("l0f_xsfm", [NCK, 128, S], BF16, kind="Internal").ap(),
            "acc_h": dt("l0f_acc", [S, D], F32, kind="Internal").ap(),
        }

    pb = PB(nc)
    ar = alloc_common(nc, pb, io)
    emits = {"l0a": emit_l0a, "l0f": emit_l0f, "l1a": emit_l1a,
             "l1f": emit_l1f}
    t_last = None
    for k, ph in enumerate(phases):
        pio = dict(io[ph])
        pio["x_in"] = io["x_in"] if k == 0 else mids[k - 1]
        pio["x_out"] = io["x_out"] if k == len(phases) - 1 else mids[k]
        t_last = emits[ph](pb, pio, ar)
        pb.set_barrier()
    pb.wait_only("sync", [t_last])
    pb.replay()
    return nc


def prep_inputs(inputs, c, phases=PHASES):
    C = np.ascontiguousarray
    m = {"ident": np.eye(128, dtype=BF),
         "x_in": C(np.asarray(inputs["L0_attn_x"][c]))}
    if "l0a" in phases:
        m.update({
            "l0a_wq": C(np.asarray(inputs["L0_attn_Wq"])),
            "l0a_wk": C(np.asarray(inputs["L0_attn_Wk"])),
            "l0a_wvT": C(np.asarray(inputs["L0_attn_Wv"]).T),
            "l0a_proj": C(np.asarray(inputs["L0_attn_proj"])),
        })
    if "l1a" in phases:
        HD = 64
        theta = (np.arange(S, dtype=np.float64)[:, None]
                 * (10000.0 ** (-2.0 * np.arange(HD, dtype=np.float64) / HD))[None])
        ctab = np.empty((128, S), np.float32)
        ctab[0:64] = np.cos(theta).T
        ctab[64:128] = np.sin(theta).T
        tri = np.triu(np.ones((128, 128), np.float32))
        tri4 = np.tile(tri, (1, 4)).astype(BF)
        m.update({
            "l1a_wq": C(np.asarray(inputs["L1_attn_Wq"])),
            "l1a_wk": C(np.asarray(inputs["L1_attn_Wk"])),
            "l1a_wv": C(np.asarray(inputs["L1_attn_Wv"])),
            "l1a_wo": C(np.asarray(inputs["L1_attn_Wo"])),
            "l1a_wg1": C(np.asarray(inputs["L1_attn_Wg1"])),
            "l1a_wg2": C(np.asarray(inputs["L1_attn_Wg2"])),
            "l1a_nw8": C(np.asarray(inputs["L1_attn_norm_w"]).reshape(8, 128)),
            "l1a_costab": ctab,
            "l1a_tri4": C(tri4),
            "l1a_dup": C(np.equal(np.arange(128)[:, None] % 64,
                                  np.arange(128)[None, :] % 64).astype(BF)),
        })
    if "l1f" in phases:
        m.update({
            "l1f_wrT": C(np.asarray(inputs["L1_ffn_W_r"]).T),
            "l1f_wkT": C(np.asarray(inputs["L1_ffn_W_k"]).T),
            "l1f_wvT": C(np.asarray(inputs["L1_ffn_W_v"]).T),
            "l1f_mur": C(np.asarray(inputs["L1_ffn_mu_r"]).reshape(8, 128)),
            "l1f_muk": C(np.asarray(inputs["L1_ffn_mu_k"]).reshape(8, 128)),
        })
    if "l0f" in phases:
        sg = np.asarray(inputs["L0_ffn_shared_gate"])
        su = np.asarray(inputs["L0_ffn_shared_up"])
        sd = np.asarray(inputs["L0_ffn_shared_down"])
        rg = np.asarray(inputs["L0_ffn_r_gate"])
        ru = np.asarray(inputs["L0_ffn_r_up"])
        rd = np.asarray(inputs["L0_ffn_r_down"])
        m.update({
            "l0f_gate": C(np.concatenate([sg.T[None], rg.transpose(0, 2, 1)])),
            "l0f_up": C(np.concatenate([su.T[None], ru.transpose(0, 2, 1)])),
            "l0f_down": C(np.concatenate([sd.T[None], rd.transpose(0, 2, 1)])),
            "l0f_gwT": C(np.asarray(inputs["L0_ffn_gate_w"]).T),
        })
    return m


def kernel(**inputs):
    phases = PHASES
    nc = build_program(phases)
    in_maps = [prep_inputs(inputs, c, phases) for c in range(8)]
    res = run_bass_kernel_spmd(nc, in_maps, core_ids=list(range(8)))
    return np.stack([np.asarray(r["x_out"], dtype=np.float32) for r in res.results])


# version 03 (final)
# speedup vs baseline: 1.1410x; 1.1215x over previous
import numpy as np
import ml_dtypes
import concourse.bass as bass
import concourse.mybir as mybir
from concourse.bass_utils import run_bass_kernel_spmd

F32 = mybir.dt.float32
BF16 = mybir.dt.bfloat16
AF = mybir.ActivationFunctionType
ALU = mybir.AluOpType
AX = mybir.AxisListType

S, D, TT = 8192, 1024, 512
NTL = S // TT
NS = TT // 128
NCK = D // 128
LN_EPS = 1e-5
BF = ml_dtypes.bfloat16

ENGS = ("sync", "gpsimd", "tensor", "vector", "scalar")


class PB:

    def __init__(self, nc, tag="pb"):
        self.nc = nc
        self.tag = tag
        self.items = {e: [] for e in ENGS}
        self.esem = {e: nc.alloc_semaphore(name=f"{tag}_{e}")
                     for e in ("tensor", "vector", "scalar", "gpsimd")}
        self.ecnt = {e: 0 for e in self.esem}
        self.dsem = {}
        self.dcnt = {}
        self.water = {e: {} for e in ENGS}
        self.mem = {e: [] for e in ENGS}
        self.barrier = []

    def _wait(self, eng, tok):
        if tok is None:
            return
        kind, name, val = tok
        if kind == "d":
            val = self.dcnt[name]
        key = (kind, name)
        w = self.water[eng]
        if w.get(key, 0) >= val:
            return
        w[key] = val
        self.items[eng].append(("wait", key, val))

    def _pre(self, eng, waits):
        for t in self.barrier:
            if not (t[0] == "e" and t[1] == eng):
                self._wait(eng, t)
        for t in waits:
            self._wait(eng, t)

    @staticmethod
    def _box(ap):
        try:
            t = ap.tensor.name
            dims = list(ap.ap)
            off = ap.offset
            span = 1
            for (step, count) in dims[1:]:
                span += (count - 1) * step
            return (t, off, off + span)
        except Exception:
            return None

    _WR0 = {"tensor_copy", "tensor_tensor", "tensor_add", "tensor_sub",
            "tensor_mul", "tensor_max", "tensor_scalar", "tensor_single_scalar",
            "tensor_scalar_add", "tensor_scalar_sub", "tensor_scalar_mul",
            "tensor_scalar_max", "tensor_scalar_min", "tensor_relu",
            "tensor_reduce", "reduce_max", "reduce_sum", "scalar_tensor_tensor",
            "tensor_tensor_scan", "memset", "memzero", "activation", "square",
            "sqrt", "mul", "copy", "add", "reciprocal", "select", "max_index",
            "sign"}

    def _self_dep_wait(self, eng, method, args, kw):
        wset, rset = [], []
        known = method in self._WR0
        if known:
            for j, a in enumerate(args):
                b = self._box(a) if hasattr(a, "tensor") else None
                if b is None:
                    continue
                (wset if j == 0 else rset).append(b)
            for k, a in kw.items():
                if hasattr(a, "tensor"):
                    b = self._box(a)
                    if b is not None:
                        (wset if k == "accum_out" else rset).append(b)
        mem = self.mem[eng]
        floor = self.water[eng].get(("e", eng), 0)
        need = 0
        if not known:
            need = self.ecnt[eng]
        else:
            def conflict(entry, mine_w, mine_r):
                t, lo, hi, cnt, is_w = entry
                for (t2, lo2, hi2) in mine_w:
                    if t2 == t and lo < hi2 and lo2 < hi:
                        return True
                if is_w:
                    for (t2, lo2, hi2) in mine_r:
                        if t2 == t and lo < hi2 and lo2 < hi:
                            return True
                return False
            keep = []
            for entry in mem:
                if entry[3] <= floor:
                    continue
                if conflict(entry, wset, rset):
                    need = max(need, entry[3])
                keep.append(entry)
            self.mem[eng] = mem = keep
        if need > 0:
            self._wait(eng, ("e", eng, need))
        return wset, rset, known

    def op(self, eng, method, *args, waits=(), inc=True, **kw):
        assert eng in ("tensor", "vector", "scalar", "gpsimd")
        if eng != "tensor":
            wset, rset, known = self._self_dep_wait(eng, method, args, kw)
        self._pre(eng, waits)
        tok = None
        if inc or eng != "tensor":
            self.ecnt[eng] += 1
            tok = ("e", eng, self.ecnt[eng])
            if eng != "tensor":
                n = self.ecnt[eng]
                if known:
                    for (t, lo, hi) in wset:
                        self.mem[eng].append((t, lo, hi, n, True))
                    for (t, lo, hi) in rset:
                        self.mem[eng].append((t, lo, hi, n, False))
                else:
                    self.mem[eng].append(("*", 0, 1 << 60, n, True))
        self.items[eng].append(("op", method, args, kw, tok))
        return tok if tok is not None else ("e", eng, self.ecnt[eng])

    def dma(self, q, out, in_, waits=(), slot="main"):
        assert q in ("sync", "gpsimd")
        if slot not in self.dsem:
            self.dsem[slot] = self.nc.alloc_semaphore(name=f"{self.tag}_s_{slot}")
            self.dcnt[slot] = 0
        self._pre(q, waits)
        self.dcnt[slot] += 16
        tok = ("d", slot, self.dcnt[slot])
        self.items[q].append(("dma", out, in_, tok))
        return tok

    def wait_only(self, eng, toks):
        for t in toks:
            self._wait(eng, t)

    def tok(self, eng):
        return ("e", eng, self.ecnt[eng])

    def set_barrier(self):
        self.barrier = ([("e", e, c) for e, c in self.ecnt.items() if c > 0]
                        + [("d", s, c) for s, c in self.dcnt.items() if c > 0])

    def _sem(self, key):
        kind, name = key
        return self.esem[name] if kind == "e" else self.dsem[name]

    def replay(self):
        with self.nc.Block() as blk:
            for eng in ENGS:
                items = self.items[eng]

                def make(items):
                    def run(e):
                        for it in items:
                            if it[0] == "wait":
                                e.wait_ge(self._sem(it[1]), it[2])
                            elif it[0] == "op":
                                inst = getattr(e, it[1])(*it[2], **it[3])
                                if it[4] is not None:
                                    inst.then_inc(self.esem[it[4][1]], 1)
                            else:
                                _, out, in_, tok = it
                                e.dma_start(out=out, in_=in_).then_inc(
                                    self.dsem[tok[1]], 16)
                    return run
                getattr(blk, eng)(make(items))


def mm_group(pb, parts, waits=()):
    n = len(parts)
    tok = None
    for k, (out, lh, rh, st_, sp_) in enumerate(parts):
        tok = pb.op("tensor", "matmul", out, lh, rh, start=st_, stop=sp_,
                    waits=(waits if k == 0 else ()), inc=(k == n - 1))
    return tok


class Rot:

    def __init__(self, banks):
        self.banks = banks
        self.evac = [None] * len(banks)
        self.i = 0

    def next(self):
        b = self.i % len(self.banks)
        self.i += 1
        return self.banks[b], self.evac[b], b

    def done(self, b, tok):
        self.evac[b] = tok


class Arena:
    def __init__(self, handle, ncols):
        self.h, self.ncols, self.off = handle, ncols, 0

    def take(self, n):
        assert self.off + n <= self.ncols, f"arena overflow {self.off}+{n}>{self.ncols}"
        o = self.off
        self.off += n
        return o

    def view(self, off, n):
        return self.h[:, off:off + n]


ACT_CONSTS = [1e-5, -0.5, 1e-6, 1.0]


def alloc_common(nc, pb, io, bf_cols=74000, f32_cols=16000):
    ar = {}
    ar["bf"] = Arena(nc.alloc_sbuf_tensor("arena_bf", [128, bf_cols], BF16), bf_cols)
    ar["f32"] = Arena(nc.alloc_sbuf_tensor("arena_f32", [128, f32_cols], F32), f32_cols)
    ar["P"] = [nc.alloc_psum_tensor(f"pr{i}", [128, 512], F32) for i in range(4)]
    ar["PTR"] = nc.alloc_psum_tensor("ptr", [128, 1024], BF16)
    ar["PZ"] = nc.alloc_psum_tensor("pz", [128, 512], F32)
    ar["P67"] = nc.alloc_psum_tensor("p67", [128, 1024], F32)
    f32, bf = ar["f32"], ar["bf"]
    o_c = f32.take(len(ACT_CONSTS))
    for j, v in enumerate(ACT_CONSTS):
        nc.const_aps.register(F32, v, f32.view(o_c + j, 1))
        pb.op("gpsimd", "memset", f32.view(o_c + j, 1), v)
    o_ident = bf.take(128)
    ar["o_ident"] = o_ident
    ar["t_ident"] = pb.dma("gpsimd", bf.view(o_ident, 128), io["ident"][:, :],
                           slot="ident")
    ar["mark"] = (bf.off, f32.off)
    return ar


def arena_reset(ar):
    ar["bf"].off, ar["f32"].off = ar["mark"]


class LNKit:

    def __init__(self, pb, ar):
        self.pb, self.f32 = pb, ar["f32"]
        self.o_st = self.f32.take(32)
        self.o_sq = self.f32.take(D)

    def st1(self, c):
        return self.f32.view(self.o_st + c, 1)

    def st(self, c):
        return self.f32.view(self.o_st + c, 4)

    def m1(self, s):
        return self.f32.view(self.o_st + 8 + s, 1)

    def rstd1(self, s):
        return self.f32.view(self.o_st + 12 + s, 1)

    def ln(self, src_s, dst_s, extra_waits=(), dst2_s=None):
        pb = self.pb
        c_ssum, c_s2, c_m, c_rstd, c_var, c_tmp = 0, 4, 8, 12, 16, 20
        for s in range(NS):
            pb.op("scalar", "activation", self.f32.view(self.o_sq, D), src_s(s),
                  AF.Square, accum_out=self.st1(c_s2 + s),
                  waits=list(extra_waits))
        tsq = pb.tok("scalar")
        for s in range(NS):
            pb.op("vector", "reduce_sum", self.st1(c_ssum + s), src_s(s),
                  axis=AX.X, waits=list(extra_waits))
        pb.op("vector", "tensor_scalar_mul", self.st(c_m), self.st(c_ssum), 1.0 / D)
        pb.op("vector", "tensor_scalar_mul", self.st(c_var), self.st(c_s2), 1.0 / D,
              waits=[tsq])
        pb.op("vector", "tensor_mul", self.st(c_tmp), self.st(c_m), self.st(c_m))
        tvar = pb.op("vector", "tensor_sub", self.st(c_var), self.st(c_var),
                     self.st(c_tmp))
        pb.op("scalar", "activation", self.st(c_rstd), self.st(c_var), AF.Ln,
              bias=LN_EPS, waits=[tvar])
        trstd = pb.op("scalar", "activation", self.st(c_rstd), self.st(c_rstd),
                      AF.Exp, scale=-0.5)
        tok = None
        for s in range(NS):
            tok = pb.op("vector", "tensor_scalar", dst_s(s), src_s(s),
                        self.st1(c_m + s), self.st1(c_rstd + s),
                        ALU.subtract, ALU.mult, waits=[trstd])
        return tok


def transpose_tile(pb, ar, src_tm_off, dst_fm_off, waits, t_tr_last):
    bf, PTR = ar["bf"], ar["PTR"]
    fm3d = bf.view(dst_fm_off, NCK * TT).rearrange("p (c t) -> p c t", c=NCK)
    ptr3d = PTR[:, 0:1024].rearrange("p (c t) -> p c t", c=NCK)
    for s in range(NS):
        w0 = list(waits) + ([t_tr_last] if t_tr_last else []) + [ar["t_ident"]]
        ttr = None
        for ck in range(NCK):
            ttr = pb.op("tensor", "transpose", PTR[:, ck * 128:(ck + 1) * 128],
                        bf.view(src_tm_off + s * D + ck * 128, 128),
                        bf.view(ar["o_ident"], 128),
                        waits=(w0 if ck == 0 else ()), inc=(ck == NCK - 1))
        t_tr_last = pb.op("vector", "tensor_copy",
                          fm3d[:, :, s * 128:(s + 1) * 128], ptr3d, waits=[ttr])
    return t_tr_last


def tile_in(ap_hbm, i):
    return ap_hbm[i * TT:(i + 1) * TT, :].rearrange("(s p) f -> p s f", p=128)



def emit_l0a(pb, io, ar):
    P, PTR, PZ = ar["P"], ar["PTR"], ar["PZ"]
    bf, f32 = ar["bf"], ar["f32"]
    arena_reset(ar)

    o_wq = bf.take(NCK * D)
    o_wk = bf.take(NCK * D)
    o_wvT = bf.take(NCK * D)
    o_proj = bf.take(NCK * 256)
    o_w2q = bf.take(NCK * 256)
    o_w2k = bf.take(NCK * 256)
    o_ones = bf.take(8)
    o_xnfm = bf.take(NCK * TT)
    o_xntm = bf.take(NS * D)
    o_qp = bf.take(2 * S)
    o_kp = bf.take(NS * 256)
    o_vtm = bf.take(NS * D)
    o_kvb = bf.take(2 * D)
    o_eluB = bf.take(512)

    o_xt = f32.take(NS * D)
    o_attn = o_xt
    o_xt2 = f32.take(NS * D)
    o_eluA = f32.take(512)
    o_kvacc = f32.take(2 * D)
    o_xo = f32.take(NS * D)
    o_zi = f32.take(8)
    lnk = LNKit(pb, ar)

    tw = []
    for (off, key, pat) in ((o_wq, "wq", "(c p) d -> p c d"),
                            (o_wk, "wk", "(c p) d -> p c d"),
                            (o_wvT, "wvT", "(c p) d -> p c d"),
                            (o_proj, "proj", "(c p) q -> p c q")):
        n = NCK * (256 if key == "proj" else D)
        tw.append(pb.dma("gpsimd", bf.view(off, n),
                         io[key].rearrange(pat, p=128), slot="l0a_w"))
    t_ones = pb.op("gpsimd", "memset", bf.view(o_ones, 1), 1.0)
    tw.append(t_ones)

    rot = Rot(P)

    def group(parts_fn, width, evac_fn, extra=()):
        bank, banktok, bi = rot.next()
        w = list(extra) + ([banktok] if banktok else [])
        mmtok = mm_group(pb, parts_fn(bank), waits=w)
        evtok = evac_fn(bank, mmtok)
        rot.done(bi, evtok)
        return evtok

    def ev_copy_bf(dst_off, width):
        def f(bank, mmtok):
            return pb.op("vector", "tensor_copy", bf.view(dst_off, width),
                         bank[:, :width], waits=[mmtok])
        return f

    def ev_kv(dst_off, width, first):
        def f(bank, mmtok):
            if first:
                return pb.op("vector", "tensor_copy", f32.view(dst_off, width),
                             bank[:, :width], waits=[mmtok])
            return pb.op("vector", "tensor_add", f32.view(dst_off, width),
                         f32.view(dst_off, width), bank[:, :width], waits=[mmtok])
        return f

    def ev_elu(dst_off, width):
        def f(bank, mmtok):
            tmin = pb.op("vector", "tensor_scalar_min", f32.view(o_eluA, width),
                         bank[:, :width], 0.0, waits=[mmtok])
            texp = pb.op("scalar", "activation", bf.view(o_eluB, width),
                         f32.view(o_eluA, width), AF.Exp, waits=[tmin])
            pb.op("vector", "tensor_scalar_max", bf.view(dst_off, width),
                  bank[:, :width], 0.0)
            return pb.op("vector", "tensor_add", bf.view(dst_off, width),
                         bf.view(dst_off, width), bf.view(o_eluB, width),
                         waits=[texp])
        return f

    for (o_w, o_dst) in ((o_wq, o_w2q), (o_wk, o_w2k)):
        for db in range(NCK):
            group(lambda bank, o_w=o_w, db=db: [
                (bank[:, :256], bf.view(o_w + ck * D + db * 128, 128),
                 bf.view(o_proj + ck * 256, 256), ck == 0, ck == NCK - 1)
                for ck in range(NCK)],
                256, ev_copy_bf(o_dst + db * 256, 256), extra=tw)

    t_groups_done = None
    t_apply_prev = None
    t_tr_last = None
    for i in range(NTL):
        lw = [t_apply_prev] if t_apply_prev else []
        tl = pb.dma("sync", f32.view(o_xt, NS * D), tile_in(io["x_in"], i),
                    waits=lw, slot="l0a_xt")
        t_apply = lnk.ln(lambda s: f32.view(o_xt + s * D, D),
                         lambda s: bf.view(o_xntm + s * D, D),
                         extra_waits=[tl])
        t_apply_prev = t_apply
        w0 = [t_apply] + ([t_groups_done] if t_groups_done else [])
        t_tr_last = transpose_tile(pb, ar, o_xntm, o_xnfm, w0, t_tr_last)
        t_xnfm = t_tr_last
        for half in range(2):
            group(lambda bank, half=half: [
                (bank[:, :TT], bf.view(o_w2q + ck * 256 + half * 128, 128),
                 bf.view(o_xnfm + ck * TT, TT), ck == 0, ck == NCK - 1)
                for ck in range(NCK)],
                TT, ev_elu(o_qp + half * S + i * TT, TT), extra=[t_xnfm])
        for s in range(NS):
            group(lambda bank, s=s: [
                (bank[:, :256], bf.view(o_xnfm + ck * TT + s * 128, 128),
                 bf.view(o_w2k + ck * 256, 256), ck == 0, ck == NCK - 1)
                for ck in range(NCK)],
                256, ev_elu(o_kp + s * 256, 256), extra=[t_xnfm])
        for s in range(NS):
            for h in range(2):
                group(lambda bank, s=s, h=h: [
                    (bank[:, :512], bf.view(o_xnfm + ck * TT + s * 128, 128),
                     bf.view(o_wvT + ck * D + h * 512, 512), ck == 0, ck == NCK - 1)
                    for ck in range(NCK)],
                    512, ev_copy_bf(o_vtm + s * D + h * 512, 512), extra=[t_xnfm])
        t_ready = pb.tok("vector")
        for ph in range(2):
            for dh in range(2):
                group(lambda bank, ph=ph, dh=dh: [
                    (bank[:, :512], bf.view(o_kp + s * 256 + ph * 128, 128),
                     bf.view(o_vtm + s * D + dh * 512, 512), s == 0, s == NS - 1)
                    for s in range(NS)],
                    512, ev_kv(o_kvacc + ph * D + dh * 512, 512, i == 0),
                    extra=[t_ready])
        t_groups_done = pb.tok("tensor")

    t_kvb = None
    for q in range(4):
        t_kvb = pb.op("vector", "tensor_copy", bf.view(o_kvb + q * 512, 512),
                      f32.view(o_kvacc + q * 512, 512))

    t_zinv_prev = None
    t_res_prev = None
    t_store = [None] * NTL
    for i in range(NTL):
        lw = [t_res_prev] if t_res_prev else []
        tl2 = pb.dma("sync", f32.view(o_xt2, NS * D), tile_in(io["x_in"], i),
                     waits=lw, slot="l0a_xt2")
        zparts = []
        for s in range(NS):
            for pc in range(2):
                zparts.append((PZ[:, s:s + 1],
                               bf.view(o_qp + pc * S + i * TT + s * 128, 128),
                               bf.view(o_ones, 1), pc == 0, pc == 1))
        zw = [t_kvb] + ([t_zinv_prev] if t_zinv_prev else [])
        tzmm = mm_group(pb, zparts, waits=zw)
        pb.op("vector", "tensor_scalar_add", f32.view(o_zi, 4),
              PZ[:, 0:4], 1e-8, waits=[tzmm])
        t_zinv_prev = pb.op("vector", "reciprocal", f32.view(o_zi, 4),
                            f32.view(o_zi, 4))
        for s in range(NS):
            for h in range(2):
                def ev(bank, mmtok, s=s, h=h):
                    return pb.op("vector", "tensor_single_scalar",
                                 f32.view(o_attn + s * D + h * 512, 512),
                                 bank[:, :512], f32.view(o_zi + s, 1), ALU.mult,
                                 waits=[mmtok])
                group(lambda bank, s=s, h=h: [
                    (bank[:, :512], bf.view(o_qp + pc * S + i * TT + s * 128, 128),
                     bf.view(o_kvb + pc * D + h * 512, 512), pc == 0, pc == 1)
                    for pc in range(2)],
                    512, ev)
        t_attn = pb.tok("vector")
        if i > 0:
            pb.wait_only("vector", [t_store[i - 1]])
        lnk.ln(lambda s: f32.view(o_attn + s * D, D),
               lambda s: f32.view(o_xo + s * D, D), extra_waits=[t_attn])
        tres = None
        for s in range(NS):
            tres = pb.op("vector", "tensor_add", f32.view(o_xo + s * D, D),
                         f32.view(o_xo + s * D, D), f32.view(o_xt2 + s * D, D),
                         waits=[tl2])
        t_res_prev = tres
        t_store[i] = pb.dma("sync", tile_in(io["x_out"], i),
                            f32.view(o_xo, NS * D), waits=[tres], slot="l0a_xo")
    return t_store[-1]



def emit_l0f(pb, io, ar):
    P, PZ = ar["P"], ar["PZ"]
    bf, f32 = ar["bf"], ar["f32"]
    arena_reset(ar)
    F = 2048
    NFC = F // 128

    o_mask = f32.take(NTL * NS * 8)
    o_iota = f32.take(8)
    lnk = LNKit(pb, ar)
    o_logit = f32.take(NS * 8)
    o_mx = f32.take(4)
    o_mx8 = f32.take(8)
    o_zero8 = f32.take(8)
    o_idx = f32.take(4)
    o_sc = f32.take(4)
    idx_u32 = pb.nc.alloc_sbuf_tensor("l0f_idx", [128, 8], mybir.dt.uint32)
    o_acc = f32.take(NS * D)
    o_xt = f32.take(NS * D)
    o_xo = f32.take(NS * D)

    o_gwf = f32.take(NCK * 8)
    o_xcfm = f32.take(NCK * 128)
    o_identf = f32.take(128)
    mark2 = (bf.off, f32.off)

    o_xntm = bf.take(NS * D)
    o_xstm = bf.take(NS * D)
    o_xnfm = bf.take(NCK * TT)
    o_xsfm = bf.take(NCK * TT)

    t_gw = pb.dma("sync", f32.view(o_gwf, NCK * 8),
                  io["gwT"].rearrange("(c p) e -> p c e", p=128), slot="l0f_gw")
    t_idf = pb.op("vector", "tensor_copy", f32.view(o_identf, 128),
                  bf.view(ar["o_ident"], 128), waits=[ar["t_ident"]])
    t_iota = None
    for e in range(8):
        t_iota = pb.op("gpsimd", "memset", f32.view(o_iota + e, 1), float(e))
    t_iota = pb.op("gpsimd", "memset", f32.view(o_zero8, 8), 0.0)

    rot = Rot(P)

    def group(parts_fn, width, evac_fn, extra=()):
        bank, banktok, bi = rot.next()
        w = list(extra) + ([banktok] if banktok else [])
        mmtok = mm_group(pb, parts_fn(bank), waits=w)
        evtok = evac_fn(bank, mmtok)
        rot.done(bi, evtok)
        return evtok

    xnfm_hbm = io["xnfm_h"].rearrange("c p t -> p c t")
    xsfm_hbm = io["xsfm_h"].rearrange("c p t -> p c t")

    t_apply_prev = None
    t_tr_last = None
    t_pz_last = None
    t_xfst = [None] * NTL
    t_xsst = [None] * NTL
    t_route_pe = None
    for i in range(NTL):
        lw = ([t_apply_prev] if t_apply_prev else []) \
            + ([t_route_pe] if t_route_pe else [])
        tl = pb.dma("sync", f32.view(o_xt, NS * D), tile_in(io["x_in"], i),
                    waits=lw, slot="l0f_xt")
        t_apply = lnk.ln(lambda s: f32.view(o_xt + s * D, D),
                         lambda s: bf.view(o_xntm + s * D, D), extra_waits=[tl])
        w0 = [t_apply] + ([t_xfst[i - 1]] if i > 0 else [])
        t_tr_last = transpose_tile(pb, ar, o_xntm, o_xnfm, w0, t_tr_last)
        t_xnfm = t_tr_last
        for s in range(NS):
            pb.op("vector", "tensor_single_scalar", f32.view(o_xt + s * D, D),
                  f32.view(o_xt + s * D, D), lnk.m1(s), ALU.subtract,
                  waits=[t_apply])
        t_rt = None
        for s in range(NS):
            for half in range(2):
                w0 = [pb.tok("vector")] + ([t_pz_last] if t_pz_last else [])
                ttr = None
                for k in range(4):
                    ck = half * 4 + k
                    ttr = pb.op("tensor", "transpose",
                                PZ[:, k * 128:(k + 1) * 128],
                                f32.view(o_xt + s * D + ck * 128, 128),
                                f32.view(o_identf, 128),
                                waits=(w0 + [t_idf] if k == 0 else ()),
                                inc=(k == 3))
                t_pz_last = pb.op("vector", "tensor_copy",
                                  f32.view(o_xcfm + half * 512, 512),
                                  PZ[:, 0:512], waits=[ttr])
            def ev_logit(bank, mmtok, s=s):
                return pb.op("vector", "tensor_single_scalar",
                             f32.view(o_logit + s * 8, 8), bank[:, :8],
                             lnk.rstd1(s), ALU.mult, waits=[mmtok])
            group(lambda bank, s=s: [
                (bank[:, :8], f32.view(o_xcfm + ck * 128, 128),
                 f32.view(o_gwf + ck * 8, 8), ck == 0, ck == NCK - 1)
                for ck in range(NCK)],
                8, ev_logit, extra=[t_pz_last, t_gw])
            pb.op("vector", "reduce_max", f32.view(o_mx + s, 1),
                  f32.view(o_logit + s * 8, 8), axis=AX.X)
            pb.op("vector", "tensor_single_scalar", f32.view(o_mx8, 8),
                  f32.view(o_zero8, 8), f32.view(o_mx + s, 1), ALU.add,
                  waits=[t_iota])
            pb.op("vector", "max_index", idx_u32[:, 0:8],
                  f32.view(o_mx8, 8), f32.view(o_logit + s * 8, 8))
            pb.op("vector", "tensor_copy", f32.view(o_idx + s, 1),
                  idx_u32[:, 0:1])
            tsg = pb.op("scalar", "activation", f32.view(o_sc + s, 1),
                        f32.view(o_mx + s, 1), AF.Sigmoid,
                        waits=[pb.tok("vector")])
            t_rt = pb.op("vector", "tensor_single_scalar",
                         f32.view(o_mask + (i * NS + s) * 8, 8),
                         f32.view(o_iota, 8),
                         f32.view(o_idx + s, 1), ALU.is_equal)
            pb.op("vector", "tensor_single_scalar", bf.view(o_xstm + s * D, D),
                  bf.view(o_xntm + s * D, D), f32.view(o_sc + s, 1), ALU.mult,
                  waits=[tsg])
        t_xs = pb.tok("vector")
        t_apply_prev = t_xs
        t_route_pe = pb.tok("tensor")
        w1 = [t_xs] + ([t_xsst[i - 1]] if i > 0 else [])
        t_tr_last = transpose_tile(pb, ar, o_xstm, o_xsfm, w1, t_tr_last)
        t_xfst[i] = pb.dma("sync", xnfm_hbm[:, :, i * TT:(i + 1) * TT],
                           bf.view(o_xnfm, NCK * TT).rearrange("p (c t) -> p c t", c=NCK),
                           waits=[t_xnfm], slot="l0f_xnst")
        t_xsst[i] = pb.dma("sync", xsfm_hbm[:, :, i * TT:(i + 1) * TT],
                           bf.view(o_xsfm, NCK * TT).rearrange("p (c t) -> p c t", c=NCK),
                           waits=[t_tr_last], slot="l0f_xsst")
    t_stageA_pe = pb.tok("tensor")
    t_stageA_ve = pb.tok("vector")

    bf.off, f32.off = mark2
    o_wg = bf.take(NCK * F)
    o_wu = bf.take(NCK * F)
    o_wd = bf.take(NFC * D)
    o_xf = bf.take(NCK * TT)
    o_gs = bf.take(4 * TT)
    o_h = bf.take(NFC * TT)

    t_accst = [None] * NTL
    t_pe_pass_end = t_stageA_pe
    o_xf2 = bf.take(NCK * TT)
    o_acc2 = o_xt
    xf_off = {0: o_xf, 1: o_xf2}
    acc_off = {0: o_acc, 1: o_acc2}
    t_pe_par = {0: None, 1: None}
    t_ve_par = {0: None, 1: None}
    t_st_par = {0: None, 1: None}

    def issue_loads(e, i):
        p = i % 2
        src_h = xnfm_hbm if e == 0 else xsfm_hbm
        lw = [t_xfst[i] if e == 0 else t_xsst[i]]
        if t_pe_par[p]:
            lw.append(t_pe_par[p])
        txf = pb.dma("sync", bf.view(xf_off[p], NCK * TT)
                     .rearrange("p (c t) -> p c t", c=NCK),
                     src_h[:, :, i * TT:(i + 1) * TT], waits=lw, slot="l0f_xf")
        tacc = None
        if e > 0:
            aw = [t_accst[i]]
            for t in (t_ve_par[p], t_st_par[p]):
                if t:
                    aw.append(t)
            tacc = pb.dma("sync", f32.view(acc_off[p], NS * D),
                          tile_in(io["acc_h"], i), waits=aw, slot="l0f_acc")
        return txf, tacc

    for e in range(9):
        wld = [t_pe_pass_end, t_stageA_ve, t_xfst[-1], t_xsst[-1]]
        twg = pb.dma("gpsimd", bf.view(o_wg, NCK * F),
                     io["gate_all"][e].rearrange("(c p) f -> p c f", p=128),
                     waits=wld, slot="l0f_w")
        twu = pb.dma("gpsimd", bf.view(o_wu, NCK * F),
                     io["up_all"][e].rearrange("(c p) f -> p c f", p=128),
                     slot="l0f_w")
        twd = pb.dma("gpsimd", bf.view(o_wd, NFC * D),
                     io["down_all"][e].rearrange("(c p) h -> p c h", p=128),
                     slot="l0f_w")
        pend = issue_loads(e, 0)
        for i in range(NTL):
            p = i % 2
            t_xf, t_acc = pend
            if i + 1 < NTL:
                pend = issue_loads(e, i + 1)
            o_xfc = xf_off[p]
            o_accc = acc_off[p]
            t_sil = [None] * NFC
            t_uev = [None] * NFC
            for cm in range(NFC):
                def ev_g(bank, mmtok, cm=cm):
                    w = [mmtok] + ([t_uev[cm - 4]] if cm >= 4 else [])
                    t_sil[cm] = pb.op("scalar", "activation",
                                      bf.view(o_gs + (cm % 4) * TT, TT), bank[:, :TT],
                                      AF.Silu, waits=w)
                    return t_sil[cm]
                group(lambda bank, cm=cm: [
                    (bank[:, :TT], bf.view(o_wg + ck * F + cm * 128, 128),
                     bf.view(o_xfc + ck * TT, TT), ck == 0, ck == NCK - 1)
                    for ck in range(NCK)],
                    TT, ev_g, extra=[t_xf, twg])
                def ev_u(bank, mmtok, cm=cm):
                    t_uev[cm] = pb.op("vector", "tensor_mul",
                                      bf.view(o_h + cm * TT, TT),
                                      bf.view(o_gs + (cm % 4) * TT, TT),
                                      bank[:, :TT], waits=[mmtok, t_sil[cm]])
                    return t_uev[cm]
                group(lambda bank, cm=cm: [
                    (bank[:, :TT], bf.view(o_wu + ck * F + cm * 128, 128),
                     bf.view(o_xfc + ck * TT, TT), ck == 0, ck == NCK - 1)
                    for ck in range(NCK)],
                    TT, ev_u, extra=[t_xf, twu])
            t_h = pb.tok("vector")
            for s in range(NS):
                for hh in range(2):
                    def ev_d(bank, mmtok, s=s, hh=hh):
                        dst = f32.view(o_accc + s * D + hh * 512, 512)
                        if e == 0:
                            w0 = [mmtok] + ([t_st_par[p]] if t_st_par[p] else [])
                            return pb.op("vector", "tensor_copy", dst,
                                         bank[:, :512], waits=w0)
                        mcol = f32.view(o_mask + (i * NS + s) * 8 + (e - 1), 1)
                        return pb.op("vector", "scalar_tensor_tensor", dst,
                                     bank[:, :512], mcol, dst,
                                     ALU.mult, ALU.add, waits=[mmtok, t_acc])
                    group(lambda bank, s=s, hh=hh: [
                        (bank[:, :512], bf.view(o_h + fc * TT + s * 128, 128),
                         bf.view(o_wd + fc * D + hh * 512, 512), fc == 0, fc == NFC - 1)
                        for fc in range(NFC)],
                        512, ev_d, extra=[t_h, twd])
            t_dn = pb.tok("vector")
            t_accst[i] = pb.dma("gpsimd", tile_in(io["acc_h"], i),
                                f32.view(o_accc, NS * D), waits=[t_dn],
                                slot="l0f_accst")
            t_st_par[p] = t_accst[i]
            t_ve_par[p] = t_dn
            t_pe_par[p] = pb.tok("tensor")
        t_pe_pass_end = pb.tok("tensor")

    t_store = [None] * NTL
    t_ep_prev = None
    for i in range(NTL):
        lw = [t_accst[i]] + ([t_ep_prev] if t_ep_prev else [])
        ta = pb.dma("sync", f32.view(o_acc, NS * D), tile_in(io["acc_h"], i),
                    waits=lw, slot="l0f_acc")
        tl = pb.dma("sync", f32.view(o_xt, NS * D), tile_in(io["x_in"], i),
                    waits=lw, slot="l0f_xt")
        if i > 0:
            pb.wait_only("vector", [t_store[i - 1]])
        lnk.ln(lambda s: f32.view(o_acc + s * D, D),
               lambda s: f32.view(o_xo + s * D, D), extra_waits=[ta])
        tres = None
        for s in range(NS):
            tres = pb.op("vector", "tensor_add", f32.view(o_xo + s * D, D),
                         f32.view(o_xo + s * D, D), f32.view(o_xt + s * D, D),
                         waits=[tl])
        t_ep_prev = tres
        t_store[i] = pb.dma("sync", tile_in(io["x_out"], i),
                            f32.view(o_xo, NS * D), waits=[tres], slot="l0f_xo")
    return t_store[-1]



def emit_l1f(pb, io, ar):
    P, PTR, PZ = ar["P"], ar["PTR"], ar["PZ"]
    bf, f32 = ar["bf"], ar["f32"]
    arena_reset(ar)
    SH = TT + 1

    o_wrT = bf.take(NCK * D)
    o_wkT = bf.take(NCK * D)
    o_wvT = bf.take(NCK * D)
    o_xntm = bf.take(NS * D)
    o_xsh = bf.take(NCK * SH)
    o_xr = bf.take(NCK * TT)
    o_xk = bf.take(NCK * TT)
    o_rtm = bf.take(NS * D)
    o_kact = bf.take(NCK * TT)
    o_tmpb = bf.take(512)
    o_tmpc = bf.take(512)

    o_xt = f32.take(NS * D)
    o_xo = f32.take(NS * D)
    o_out = f32.take(NS * D)
    o_mu = f32.take(32)
    o_identf = f32.take(128)
    o_mstg = f32.take(128)
    lnk = LNKit(pb, ar)

    tw = []
    for (off, key) in ((o_wrT, "wrT"), (o_wkT, "wkT"), (o_wvT, "wvT")):
        tw.append(pb.dma("gpsimd", bf.view(off, NCK * D),
                         io[key].rearrange("(c p) d -> p c d", p=128),
                         slot="l1f_w"))
    t_idf = pb.op("vector", "tensor_copy", f32.view(o_identf, 128),
                  bf.view(ar["o_ident"], 128), waits=[ar["t_ident"]])

    t_mu = None
    for j, key in enumerate(("mu_r8", "mu_k8")):
        tm = pb.dma("sync", f32.view(o_mstg, 128)[0:8, :],
                    io[key][:, :], slot="l1f_mu", waits=[t_mu] if t_mu else ())
        ttr = pb.op("tensor", "transpose", PZ[:, 0:8],
                    f32.view(o_mstg, 128)[0:8, :],
                    f32.view(o_identf, 128)[0:8, 0:8], waits=[tm, t_idf], inc=True)
        t_mu = pb.op("vector", "tensor_copy", f32.view(o_mu + j * 8, 8),
                     PZ[:, 0:8], waits=[ttr])
        t_mu = pb.op("vector", "tensor_scalar", f32.view(o_mu + 16 + j * 8, 8),
                     f32.view(o_mu + j * 8, 8), -1.0, 1.0, ALU.mult, ALU.add)

    t_z = None
    for ck in range(NCK):
        t_z = pb.op("vector", "memset", bf.view(o_xsh + ck * SH, 1), 0.0)

    rot = Rot(P)

    def group(parts_fn, width, evac_fn, extra=()):
        bank, banktok, bi = rot.next()
        w = list(extra) + ([banktok] if banktok else [])
        mmtok = mm_group(pb, parts_fn(bank), waits=w)
        evtok = evac_fn(bank, mmtok)
        rot.done(bi, evtok)
        return evtok

    xsh3d = bf.view(o_xsh, NCK * SH).rearrange("p (c t) -> p c t", c=NCK)
    ptr3d = ar["PTR"][:, 0:1024].rearrange("p (c t) -> p c t", c=NCK)

    t_apply_prev = None
    t_tr_last = t_z
    t_pe_prev = None
    t_store = [None] * NTL
    for i in range(NTL):
        lw = ([t_apply_prev] if t_apply_prev else [])
        tl = pb.dma("sync", f32.view(o_xt, NS * D), tile_in(io["x_in"], i),
                    waits=lw, slot="l1f_xt")
        t_apply = lnk.ln(lambda s: f32.view(o_xt + s * D, D),
                         lambda s: bf.view(o_xntm + s * D, D), extra_waits=[tl])
        t_apply_prev = t_apply
        if i > 0:
            tcar = None
            for ck in range(NCK):
                tcar = pb.op("vector", "tensor_copy", bf.view(o_xsh + ck * SH, 1),
                             bf.view(o_xsh + ck * SH + TT, 1),
                             waits=[t_pe_prev] if (ck == 0 and t_pe_prev) else ())
            t_tr_last = tcar
        for s in range(NS):
            w0 = [t_apply] + ([t_tr_last] if t_tr_last else []) + [ar["t_ident"]]
            ttr = None
            for ck in range(NCK):
                ttr = pb.op("tensor", "transpose", PTR[:, ck * 128:(ck + 1) * 128],
                            bf.view(o_xntm + s * D + ck * 128, 128),
                            bf.view(ar["o_ident"], 128),
                            waits=(w0 if ck == 0 else ()), inc=(ck == NCK - 1))
            t_tr_last = pb.op("vector", "tensor_copy",
                              xsh3d[:, :, 1 + s * 128:1 + (s + 1) * 128], ptr3d,
                              waits=[ttr])
        t_mix = None
        for ck in range(NCK):
            for (o_dst, cmu) in ((o_xr, 0), (o_xk, 8)):
                pb.op("vector", "tensor_single_scalar", bf.view(o_tmpc, 512),
                      bf.view(o_xsh + ck * SH, TT), f32.view(o_mu + 16 + cmu + ck, 1),
                      ALU.mult, waits=[t_tr_last, t_mu])
                t_mix = pb.op("vector", "scalar_tensor_tensor",
                              bf.view(o_dst + ck * TT, TT),
                              bf.view(o_xsh + ck * SH + 1, TT),
                              f32.view(o_mu + cmu + ck, 1),
                              bf.view(o_tmpc, 512), ALU.mult, ALU.add)
        for s in range(NS):
            for hh in range(2):
                def ev_r(bank, mmtok, s=s, hh=hh):
                    return pb.op("scalar", "activation",
                                 bf.view(o_rtm + s * D + hh * 512, 512),
                                 bank[:, :512], AF.Sigmoid, waits=[mmtok])
                group(lambda bank, s=s, hh=hh: [
                    (bank[:, :512], bf.view(o_xr + ck * TT + s * 128, 128),
                     bf.view(o_wrT + ck * D + hh * 512, 512), ck == 0, ck == NCK - 1)
                    for ck in range(NCK)],
                    512, ev_r, extra=[t_mix, tw[0]])
        for cm in range(NCK):
            def ev_k(bank, mmtok, cm=cm):
                pb.op("vector", "tensor_scalar_max", bf.view(o_tmpb, 512),
                      bank[:, :512], 0.0, waits=[mmtok])
                return pb.op("vector", "tensor_mul", bf.view(o_kact + cm * TT, TT),
                             bf.view(o_tmpb, 512), bf.view(o_tmpb, 512))
            group(lambda bank, cm=cm: [
                (bank[:, :512], bf.view(o_wkT + ck * D + cm * 128, 128),
                 bf.view(o_xk + ck * TT, TT), ck == 0, ck == NCK - 1)
                for ck in range(NCK)],
                512, ev_k, extra=[t_mix, tw[1]])
        t_kact = pb.tok("vector")
        for s in range(NS):
            for hh in range(2):
                def ev_o(bank, mmtok, s=s, hh=hh):
                    return pb.op("vector", "tensor_mul",
                                 f32.view(o_out + s * D + hh * 512, 512),
                                 bf.view(o_rtm + s * D + hh * 512, 512),
                                 bank[:, :512], waits=[mmtok])
                group(lambda bank, s=s, hh=hh: [
                    (bank[:, :512], bf.view(o_kact + ck * TT + s * 128, 128),
                     bf.view(o_wvT + ck * D + hh * 512, 512), ck == 0, ck == NCK - 1)
                    for ck in range(NCK)],
                    512, ev_o, extra=[t_kact, tw[2]])
        t_out = pb.tok("vector")
        t_pe_prev = pb.tok("tensor")
        if i > 0:
            pb.wait_only("vector", [t_store[i - 1]])
        lnk.ln(lambda s: f32.view(o_out + s * D, D),
               lambda s: f32.view(o_xo + s * D, D), extra_waits=[t_out])
        tres = None
        for s in range(NS):
            tres = pb.op("vector", "tensor_add", f32.view(o_xo + s * D, D),
                         f32.view(o_xo + s * D, D), f32.view(o_xt + s * D, D),
                         waits=[tl])
        t_store[i] = pb.dma("sync", tile_in(io["x_out"], i),
                            f32.view(o_xo, NS * D), waits=[tres], slot="l1f_xo")
    return t_store[-1]



def emit_l1a(pb, io, ar):
    P, PTR, PZ = ar["P"], ar["PTR"], ar["PZ"]
    bf, f32 = ar["bf"], ar["f32"]
    arena_reset(ar)
    NH, HTT = 16, 256

    o_wq = bf.take(NCK * D)
    o_wk = bf.take(NCK * D)
    o_wv = bf.take(NCK * D)
    o_wo = bf.take(NCK * D)
    o_wg1 = bf.take(NCK * 16)
    o_wg2 = bf.take(D)
    o_ctab = bf.take(TT)
    o_tri = bf.take(512)
    o_xnfm = bf.take(NCK * TT)
    o_qR = o_xnfm
    o_xntm = bf.take(NS * D)
    o_qsil = bf.take(NCK * TT)
    o_gate = o_qsil
    o_E = bf.take(NCK * TT)
    o_onfm = o_E
    o_t1 = bf.take(HTT)
    o_ELr = bf.take(NH * HTT)
    o_ELrtm = bf.take(2 * 2048)
    o_vtm = bf.take(NS * D)
    o_Ub = bf.take(NH * 64)
    o_Asb = bf.take(NH * 128)
    o_g1 = bf.take(512)
    o_otm = bf.take(NS * D)

    o_xt = f32.take(NS * D)
    o_xo = f32.take(NS * D)
    o_C = f32.take(NCK * TT)
    o_out = o_C
    o_carry = f32.take(8)
    o_U = f32.take(NH * 64)
    o_nwf = f32.take(8)
    o_rms = f32.take(8)
    o_identf = f32.take(128)
    o_stg = f32.take(128)
    lnk = LNKit(pb, ar)

    tw = []
    for (off, key, wdt) in ((o_wq, "wq", D), (o_wk, "wk", D), (o_wv, "wv", D),
                            (o_wo, "wo", D), (o_wg1, "wg1", 16)):
        tw.append(pb.dma("gpsimd", bf.view(off, NCK * wdt),
                         io[key].rearrange("(c p) d -> p c d", p=128),
                         slot="l1a_w"))
    tw.append(pb.dma("gpsimd", bf.h[0:16, o_wg2:o_wg2 + D], io["wg2"][:, :],
                     slot="l1a_w"))
    o_dup = bf.take(128)
    tw.append(pb.dma("gpsimd", bf.view(o_dup, 128), io["dup"][:, :],
                     slot="l1a_w"))
    t_tri = pb.dma("gpsimd", bf.view(o_tri, 512), io["tri4"][:, :],
                   slot="l1a_w")
    t_idf = pb.op("vector", "tensor_copy", f32.view(o_identf, 128),
                  bf.view(ar["o_ident"], 128), waits=[ar["t_ident"]])
    tnw0 = pb.dma("sync", f32.view(o_stg, 128)[0:8, :], io["nw8"][:, :],
                  slot="l1a_nw")
    tnw1 = pb.op("tensor", "transpose", PZ[:, 0:8], f32.view(o_stg, 128)[0:8, :],
                 f32.view(o_identf, 128)[0:8, 0:8], waits=[tnw0, t_idf], inc=True)
    t_nw = pb.op("vector", "tensor_copy", f32.view(o_nwf, 8), PZ[:, 0:8],
                 waits=[tnw1])
    t_u = None
    for g in range(2):
        pb.op("vector", "memset", f32.view(o_U + g * 512, 512), 0.0)
        t_u = pb.op("vector", "memset", bf.view(o_Ub + g * 512, 512), 0.0)

    rot = Rot(P)

    def group(parts_fn, width, evac_fn, extra=()):
        bank, banktok, bi = rot.next()
        w = list(extra) + ([banktok] if banktok else [])
        mmtok = mm_group(pb, parts_fn(bank), waits=w)
        evtok = evac_fn(bank, mmtok)
        rot.done(bi, evtok)
        return evtok

    t_apply_prev = None
    t_tr_last = None
    t_store = [None] * NTL
    t_lrpe_prev = []
    t_Ubf = t_u
    for i in range(NTL):
        lw = ([t_apply_prev] if t_apply_prev else [])
        tl = pb.dma("sync", f32.view(o_xt, NS * D), tile_in(io["x_in"], i),
                    waits=lw, slot="l1a_xt")
        tct = pb.dma("gpsimd", bf.view(o_ctab, TT),
                     io["costab"][:, i * TT:(i + 1) * TT],
                     waits=list(t_lrpe_prev), slot="l1a_ct")
        t_apply = lnk.ln(lambda s: f32.view(o_xt + s * D, D),
                         lambda s: bf.view(o_xntm + s * D, D), extra_waits=[tl])
        t_apply_prev = t_apply
        t_tr_last = transpose_tile(pb, ar, o_xntm, o_xnfm, [t_apply], t_tr_last)
        t_xnfm = t_tr_last
        tE = [None] * NCK
        for cm in range(NCK):
            def ev_q(bank, mmtok, cm=cm):
                return pb.op("scalar", "activation", bf.view(o_qsil + cm * TT, TT),
                             bank[:, :TT], AF.Silu, waits=[mmtok])
            group(lambda bank, cm=cm: [
                (bank[:, :TT], bf.view(o_wq + ck * D + cm * 128, 128),
                 bf.view(o_xnfm + ck * TT, TT), ck == 0, ck == NCK - 1)
                for ck in range(NCK)],
                TT, ev_q, extra=[t_xnfm, tw[0]])
            def ev_e(bank, mmtok, cm=cm):
                tE[cm] = pb.op("scalar", "activation", bf.view(o_E + cm * TT, TT),
                               bank[:, :TT], AF.Exp, waits=[mmtok])
                return tE[cm]
            group(lambda bank, cm=cm: [
                (bank[:, :TT], bf.view(o_wk + ck * D + cm * 128, 128),
                 bf.view(o_xnfm + ck * TT, TT), ck == 0, ck == NCK - 1)
                for ck in range(NCK)],
                TT, ev_e, extra=[t_xnfm, tw[1]])
        for s in range(NS):
            for hh in range(2):
                def ev_v(bank, mmtok, s=s, hh=hh):
                    return pb.op("vector", "tensor_copy",
                                 bf.view(o_vtm + s * D + hh * 512, 512),
                                 bank[:, :512], waits=[mmtok])
                group(lambda bank, s=s, hh=hh: [
                    (bank[:, :512], bf.view(o_xnfm + ck * TT + s * 128, 128),
                     bf.view(o_wv + ck * D + hh * 512, 512), ck == 0, ck == NCK - 1)
                    for ck in range(NCK)],
                    512, ev_v, extra=[t_xnfm, tw[2]])
        def ev_g1(bank, mmtok):
            return pb.op("vector", "tensor_copy", bf.h[0:16, o_g1:o_g1 + 512],
                         bank[0:16, :512], waits=[mmtok])
        group(lambda bank: [
            (bank[0:16, :512], bf.view(o_wg1 + ck * 16, 16),
             bf.view(o_xnfm + ck * TT, TT), ck == 0, ck == NCK - 1)
            for ck in range(NCK)],
            512, ev_g1, extra=[t_xnfm, tw[4]])
        t_qkv_pe = pb.tok("tensor")
        t_scan = None
        for ck in range(NCK):
            ini = 1.0 if i == 0 else f32.view(o_carry + ck, 1)
            pb.op("vector", "tensor_tensor_scan", f32.view(o_C + ck * TT, TT),
                  bf.view(o_E + ck * TT, TT), bf.view(o_E + ck * TT, TT), ini,
                  ALU.add, ALU.bypass, waits=[tE[ck]])
            t_scan = pb.op("vector", "tensor_copy", f32.view(o_carry + ck, 1),
                           f32.view(o_C + ck * TT + TT - 1, 1))
        tR = [None] * NCK
        for ck in range(NCK):
            pb.op("scalar", "activation", f32.view(o_C + ck * TT, TT),
                  f32.view(o_C + ck * TT, TT), AF.Ln, waits=[t_scan])
            tR[ck] = pb.op("scalar", "activation", f32.view(o_C + ck * TT, TT),
                           f32.view(o_C + ck * TT, TT), AF.Exp, scale=-1.0)

        for hf in range(2):
            t_q = None
            t_el = None
            ctab_v = bf.view(o_ctab + hf * HTT, HTT)
            for ck in range(NCK):
                tt1 = pb.op("vector", "tensor_mul", bf.view(o_t1, HTT),
                            bf.view(o_qsil + ck * TT + hf * HTT, HTT),
                            f32.view(o_C + ck * TT + hf * HTT, HTT),
                            waits=[tR[ck], t_qkv_pe, tct]
                            + ([t_q] if t_q else []))
                for hh in range(2):
                    h = 2 * ck + hh
                    b0 = hh * 64
                    def ev_qd(bank, mmtok, h=h):
                        return pb.op("vector", "tensor_mul",
                                     bf.view(o_qR + h * HTT, HTT),
                                     bank[:, :HTT], ctab_v,
                                     waits=[mmtok, tct])
                    t_q = group(lambda bank, b0=b0: [
                        (bank[:, :HTT], bf.h[b0:b0 + 64, o_dup:o_dup + 128],
                         bf.h[b0:b0 + 64, o_t1:o_t1 + HTT], True, True)],
                        HTT, ev_qd, extra=[tt1, tw[6]])
                    def ev_ed(bank, mmtok, h=h):
                        return pb.op("vector", "tensor_mul",
                                     bf.view(o_ELr + h * HTT, HTT),
                                     bank[:, :HTT], ctab_v,
                                     waits=[mmtok, tct])
                    t_el = group(lambda bank, b0=b0, ck=ck: [
                        (bank[:, :HTT], bf.h[b0:b0 + 64, o_dup:o_dup + 128],
                         bf.h[b0:b0 + 64, o_E + ck * TT + hf * HTT:o_E + ck * TT + hf * HTT + HTT],
                         True, True)],
                        HTT, ev_ed, extra=[tE[ck], tw[6]])
            t_lrpe_prev = [t_el, t_q]
            for sl in range(2):
                for bb in range(2):
                    w0 = [t_el, t_q] + ([t_tr_last] if t_tr_last else [])
                    ttr = None
                    for k in range(8):
                        h = bb * 8 + k
                        ttr = pb.op("tensor", "transpose",
                                    PTR[:, k * 128:(k + 1) * 128],
                                    bf.view(o_ELr + h * HTT + sl * 128, 128),
                                    bf.view(ar["o_ident"], 128),
                                    waits=(w0 if k == 0 else ()), inc=(k == 7))
                    t_tr_last = pb.op("vector", "tensor_copy",
                                      bf.view(o_ELrtm + sl * 2048 + bb * 1024, 1024),
                                      PTR[:, 0:1024], waits=[ttr])
            t_eltm = t_tr_last
            for c in range(2):
                sg = hf * 2 + c
                tA = [None] * 4
                for g4 in range(4):
                    def ev_a(bank, mmtok, g4=g4):
                        tA[g4] = pb.op("vector", "tensor_mul",
                                       bf.view(o_Asb + g4 * 512, 512),
                                       bf.view(o_tri, 512),
                                       bank[:, :512], waits=[mmtok, t_tri])
                        return tA[g4]
                    group(lambda bank, g4=g4: [
                        (bank[:, q * 128:(q + 1) * 128],
                         bf.view(o_ELr + (g4 * 4 + q) * HTT + c * 128, 128),
                         bf.view(o_qR + (g4 * 4 + q) * HTT + c * 128, 128),
                         True, True)
                        for q in range(4)],
                        512, ev_a, extra=[t_el, t_q])
                for hg in range(2):
                    def ev_o(bank, mmtok, hg=hg):
                        return pb.op("vector", "tensor_copy",
                                     bf.view(o_otm + sg * D + hg * 512, 512),
                                     bank[:, :512], waits=[mmtok])
                    def mk_parts(bank, hg=hg):
                        out = []
                        for h8 in range(8):
                            h = hg * 8 + h8
                            out.append((bank[:, h8 * 64:(h8 + 1) * 64],
                                        bf.view(o_Asb + h * 128, 128),
                                        bf.view(o_vtm + sg * D + h * 64, 64),
                                        True, False))
                            out.append((bank[:, h8 * 64:(h8 + 1) * 64],
                                        bf.view(o_qR + h * HTT + c * 128, 128),
                                        bf.view(o_Ub + h * 64, 64),
                                        False, True))
                        return out
                    group(mk_parts, 512, ev_o,
                          extra=[tA[hg * 2], tA[hg * 2 + 1], t_Ubf])
                t_o_pe = pb.tok("tensor")
                for hg in range(2):
                    def ev_ds(bank, mmtok, hg=hg):
                        pb.op("vector", "tensor_add",
                              f32.view(o_U + hg * 512, 512),
                              f32.view(o_U + hg * 512, 512), bank[:, :512],
                              waits=[mmtok])
                        return pb.op("vector", "tensor_copy",
                                     bf.view(o_Ub + hg * 512, 512),
                                     f32.view(o_U + hg * 512, 512),
                                     waits=[t_o_pe])
                    t_Ubf = group(lambda bank, hg=hg: [
                        (bank[:, h8 * 64:(h8 + 1) * 64],
                         bf.view(o_ELrtm + c * 2048 + (hg * 8 + h8) * 128, 128),
                         bf.view(o_vtm + sg * D + (hg * 8 + h8) * 64, 64),
                         True, True)
                        for h8 in range(8)],
                        512, ev_ds, extra=[t_eltm])

        t_half2 = pb.tok("vector")
        tgate = None
        for s in range(NS):
            for hh in range(2):
                def ev_gt(bank, mmtok, s=s, hh=hh):
                    return pb.op("scalar", "activation",
                                 bf.view(o_gate + s * D + hh * 512, 512),
                                 bank[:, :512], AF.Sigmoid, waits=[mmtok])
                tgate = group(lambda bank, s=s, hh=hh: [
                    (bank[:, :512], bf.h[0:16, o_g1 + s * 128:o_g1 + s * 128 + 128],
                     bf.h[0:16, o_wg2 + hh * 512:o_wg2 + hh * 512 + 512],
                     True, True)],
                    512, ev_gt, extra=[t_half2, tw[5]])
        tog = None
        for s in range(NS):
            for hh in range(2):
                tog = pb.op("vector", "tensor_mul",
                            bf.view(o_otm + s * D + hh * 512, 512),
                            bf.view(o_otm + s * D + hh * 512, 512),
                            bf.view(o_gate + s * D + hh * 512, 512),
                            waits=[tgate])
        tsq = None
        for s in range(NS):
            tsq = pb.op("scalar", "activation", f32.view(lnk.o_sq, D),
                        bf.view(o_otm + s * D, D), AF.Square,
                        accum_out=f32.view(o_rms + s, 1), waits=[tog])
        pb.op("vector", "tensor_scalar_mul", f32.view(o_rms, 4),
              f32.view(o_rms, 4), 1.0 / D, waits=[tsq])
        tms = pb.tok("vector")
        pb.op("scalar", "activation", f32.view(o_rms + 4, 4),
              f32.view(o_rms, 4), AF.Ln, bias=1e-6, waits=[tms])
        trst = pb.op("scalar", "activation", f32.view(o_rms + 4, 4),
                     f32.view(o_rms + 4, 4), AF.Exp, scale=-0.5)
        ton = None
        for s in range(NS):
            ton = pb.op("vector", "tensor_single_scalar",
                        bf.view(o_otm + s * D, D), bf.view(o_otm + s * D, D),
                        f32.view(o_rms + 4 + s, 1), ALU.mult, waits=[trst])
        t_tr_last = transpose_tile(pb, ar, o_otm, o_onfm, [ton], t_tr_last)
        tnw_ap = None
        for ck in range(NCK):
            tnw_ap = pb.op("vector", "tensor_single_scalar",
                           bf.view(o_onfm + ck * TT, TT),
                           bf.view(o_onfm + ck * TT, TT),
                           f32.view(o_nwf + ck, 1), ALU.mult,
                           waits=[t_tr_last, t_nw])
        tout = None
        for s in range(NS):
            for hh in range(2):
                def ev_out(bank, mmtok, s=s, hh=hh):
                    return pb.op("vector", "tensor_copy",
                                 f32.view(o_out + s * D + hh * 512, 512),
                                 bank[:, :512], waits=[mmtok])
                tout = group(lambda bank, s=s, hh=hh: [
                    (bank[:, :512], bf.view(o_onfm + ck * TT + s * 128, 128),
                     bf.view(o_wo + ck * D + hh * 512, 512), ck == 0, ck == NCK - 1)
                    for ck in range(NCK)],
                    512, ev_out, extra=[tnw_ap, tw[3]])
        if i > 0:
            pb.wait_only("vector", [t_store[i - 1]])
        lnk.ln(lambda s: f32.view(o_out + s * D, D),
               lambda s: f32.view(o_xo + s * D, D), extra_waits=[tout])
        tres = None
        for s in range(NS):
            tres = pb.op("vector", "tensor_add", f32.view(o_xo + s * D, D),
                         f32.view(o_xo + s * D, D), f32.view(o_xt + s * D, D),
                         waits=[tl])
        t_store[i] = pb.dma("sync", tile_in(io["x_out"], i),
                            f32.view(o_xo, NS * D), waits=[tres], slot="l1a_xo")
    return t_store[-1]



PHASES = ("l0a", "l0f", "l1a", "l1f")


def build_program(phases=PHASES):
    nc = bass.Bass("TRN2", target_bir_lowering=False)
    dt = nc.dram_tensor
    io = {"ident": dt("ident", [128, 128], BF16, kind="ExternalInput").ap()}
    io["x_in"] = dt("x_in", [S, D], F32, kind="ExternalInput").ap()
    io["x_out"] = dt("x_out", [S, D], F32, kind="ExternalOutput").ap()
    n_mid = len(phases) - 1
    mids = [dt(f"mid{k}", [S, D], F32, kind="Internal").ap() for k in range(n_mid)]

    if "l0a" in phases:
        io["l0a"] = {
            "wq": dt("l0a_wq", [D, D], F32, kind="ExternalInput").ap(),
            "wk": dt("l0a_wk", [D, D], F32, kind="ExternalInput").ap(),
            "wvT": dt("l0a_wvT", [D, D], F32, kind="ExternalInput").ap(),
            "proj": dt("l0a_proj", [D, 256], F32, kind="ExternalInput").ap(),
        }
    if "l1a" in phases:
        io["l1a"] = {
            "wq": dt("l1a_wq", [D, D], F32, kind="ExternalInput").ap(),
            "wk": dt("l1a_wk", [D, D], F32, kind="ExternalInput").ap(),
            "wv": dt("l1a_wv", [D, D], F32, kind="ExternalInput").ap(),
            "wo": dt("l1a_wo", [D, D], F32, kind="ExternalInput").ap(),
            "wg1": dt("l1a_wg1", [D, 16], F32, kind="ExternalInput").ap(),
            "wg2": dt("l1a_wg2", [16, D], F32, kind="ExternalInput").ap(),
            "nw8": dt("l1a_nw8", [8, 128], F32, kind="ExternalInput").ap(),
            "costab": dt("l1a_costab", [128, S], F32, kind="ExternalInput").ap(),
            "tri4": dt("l1a_tri4", [128, 512], BF16, kind="ExternalInput").ap(),
            "dup": dt("l1a_dup", [128, 128], BF16, kind="ExternalInput").ap(),
        }
    if "l1f" in phases:
        io["l1f"] = {
            "wrT": dt("l1f_wrT", [D, D], F32, kind="ExternalInput").ap(),
            "wkT": dt("l1f_wkT", [D, D], F32, kind="ExternalInput").ap(),
            "wvT": dt("l1f_wvT", [D, D], F32, kind="ExternalInput").ap(),
            "mu_r8": dt("l1f_mur", [8, 128], F32, kind="ExternalInput").ap(),
            "mu_k8": dt("l1f_muk", [8, 128], F32, kind="ExternalInput").ap(),
        }
    if "l0f" in phases:
        io["l0f"] = {
            "gate_all": dt("l0f_gate", [9, D, 2048], F32, kind="ExternalInput").ap(),
            "up_all": dt("l0f_up", [9, D, 2048], F32, kind="ExternalInput").ap(),
            "down_all": dt("l0f_down", [9, 2048, D], F32, kind="ExternalInput").ap(),
            "gwT": dt("l0f_gwT", [D, 8], F32, kind="ExternalInput").ap(),
            "xnfm_h": dt("l0f_xnfm", [NCK, 128, S], BF16, kind="Internal").ap(),
            "xsfm_h": dt("l0f_xsfm", [NCK, 128, S], BF16, kind="Internal").ap(),
            "acc_h": dt("l0f_acc", [S, D], F32, kind="Internal").ap(),
        }

    pb = PB(nc)
    ar = alloc_common(nc, pb, io)
    emits = {"l0a": emit_l0a, "l0f": emit_l0f, "l1a": emit_l1a,
             "l1f": emit_l1f}
    t_last = None
    for k, ph in enumerate(phases):
        pio = dict(io[ph])
        pio["x_in"] = io["x_in"] if k == 0 else mids[k - 1]
        pio["x_out"] = io["x_out"] if k == len(phases) - 1 else mids[k]
        t_last = emits[ph](pb, pio, ar)
        pb.set_barrier()
    pb.wait_only("sync", [t_last])
    pb.replay()
    return nc


def prep_inputs(inputs, c, phases=PHASES):
    C = np.ascontiguousarray
    m = {"ident": np.eye(128, dtype=BF),
         "x_in": C(np.asarray(inputs["L0_attn_x"][c]))}
    if "l0a" in phases:
        m.update({
            "l0a_wq": C(np.asarray(inputs["L0_attn_Wq"])),
            "l0a_wk": C(np.asarray(inputs["L0_attn_Wk"])),
            "l0a_wvT": C(np.asarray(inputs["L0_attn_Wv"]).T),
            "l0a_proj": C(np.asarray(inputs["L0_attn_proj"])),
        })
    if "l1a" in phases:
        HD = 64
        theta = (np.arange(S, dtype=np.float64)[:, None]
                 * (10000.0 ** (-2.0 * np.arange(HD, dtype=np.float64) / HD))[None])
        ctab = np.empty((128, S), np.float32)
        ctab[0:64] = np.cos(theta).T
        ctab[64:128] = np.sin(theta).T
        tri = np.triu(np.ones((128, 128), np.float32))
        tri4 = np.tile(tri, (1, 4)).astype(BF)
        m.update({
            "l1a_wq": C(np.asarray(inputs["L1_attn_Wq"])),
            "l1a_wk": C(np.asarray(inputs["L1_attn_Wk"])),
            "l1a_wv": C(np.asarray(inputs["L1_attn_Wv"])),
            "l1a_wo": C(np.asarray(inputs["L1_attn_Wo"])),
            "l1a_wg1": C(np.asarray(inputs["L1_attn_Wg1"])),
            "l1a_wg2": C(np.asarray(inputs["L1_attn_Wg2"])),
            "l1a_nw8": C(np.asarray(inputs["L1_attn_norm_w"]).reshape(8, 128)),
            "l1a_costab": ctab,
            "l1a_tri4": C(tri4),
            "l1a_dup": C(np.equal(np.arange(128)[:, None] % 64,
                                  np.arange(128)[None, :] % 64).astype(BF)),
        })
    if "l1f" in phases:
        m.update({
            "l1f_wrT": C(np.asarray(inputs["L1_ffn_W_r"]).T),
            "l1f_wkT": C(np.asarray(inputs["L1_ffn_W_k"]).T),
            "l1f_wvT": C(np.asarray(inputs["L1_ffn_W_v"]).T),
            "l1f_mur": C(np.asarray(inputs["L1_ffn_mu_r"]).reshape(8, 128)),
            "l1f_muk": C(np.asarray(inputs["L1_ffn_mu_k"]).reshape(8, 128)),
        })
    if "l0f" in phases:
        sg = np.asarray(inputs["L0_ffn_shared_gate"])
        su = np.asarray(inputs["L0_ffn_shared_up"])
        sd = np.asarray(inputs["L0_ffn_shared_down"])
        rg = np.asarray(inputs["L0_ffn_r_gate"])
        ru = np.asarray(inputs["L0_ffn_r_up"])
        rd = np.asarray(inputs["L0_ffn_r_down"])
        m.update({
            "l0f_gate": C(np.concatenate([sg.T[None], rg.transpose(0, 2, 1)])),
            "l0f_up": C(np.concatenate([su.T[None], ru.transpose(0, 2, 1)])),
            "l0f_down": C(np.concatenate([sd.T[None], rd.transpose(0, 2, 1)])),
            "l0f_gwT": C(np.asarray(inputs["L0_ffn_gate_w"]).T),
        })
    return m


def kernel(**inputs):
    phases = PHASES
    nc = build_program(phases)
    in_maps = [prep_inputs(inputs, c, phases) for c in range(8)]
    res = run_bass_kernel_spmd(nc, in_maps, core_ids=list(range(8)))
    return np.stack([np.asarray(r["x_out"], dtype=np.float32) for r in res.results])


# version 04
# speedup vs baseline: 1.2035x; 1.0548x over previous
import numpy as np
import ml_dtypes
import concourse.bass as bass
import concourse.mybir as mybir
from concourse.bass_utils import run_bass_kernel_spmd

F32 = mybir.dt.float32
BF16 = mybir.dt.bfloat16
AF = mybir.ActivationFunctionType
ALU = mybir.AluOpType
AX = mybir.AxisListType

S, D, TT = 8192, 1024, 512
NTL = S // TT
NS = TT // 128
NCK = D // 128
LN_EPS = 1e-5
BF = ml_dtypes.bfloat16

ENGS = ("sync", "gpsimd", "tensor", "vector", "scalar")


class PB:

    def __init__(self, nc, tag="pb"):
        self.nc = nc
        self.tag = tag
        self.items = {e: [] for e in ENGS}
        self.esem = {e: nc.alloc_semaphore(name=f"{tag}_{e}")
                     for e in ("tensor", "vector", "scalar", "gpsimd")}
        self.ecnt = {e: 0 for e in self.esem}
        self.dsem = {}
        self.dcnt = {}
        self.water = {e: {} for e in ENGS}
        self.mem = {e: [] for e in ENGS}
        self.barrier = []

    def _wait(self, eng, tok):
        if tok is None:
            return
        kind, name, val = tok
        if kind == "d":
            val = self.dcnt[name]
        key = (kind, name)
        w = self.water[eng]
        if w.get(key, 0) >= val:
            return
        w[key] = val
        self.items[eng].append(("wait", key, val))

    def _pre(self, eng, waits):
        for t in self.barrier:
            if not (t[0] == "e" and t[1] == eng):
                self._wait(eng, t)
        for t in waits:
            self._wait(eng, t)

    @staticmethod
    def _box(ap):
        try:
            t = ap.tensor.name
            dims = list(ap.ap)
            off = ap.offset
            span = 1
            for (step, count) in dims[1:]:
                span += (count - 1) * step
            return (t, off, off + span)
        except Exception:
            return None

    _WR0 = {"tensor_copy", "tensor_tensor", "tensor_add", "tensor_sub",
            "tensor_mul", "tensor_max", "tensor_scalar", "tensor_single_scalar",
            "tensor_scalar_add", "tensor_scalar_sub", "tensor_scalar_mul",
            "tensor_scalar_max", "tensor_scalar_min", "tensor_relu",
            "tensor_reduce", "reduce_max", "reduce_sum", "scalar_tensor_tensor",
            "tensor_tensor_scan", "memset", "memzero", "activation", "square",
            "sqrt", "mul", "copy", "add", "reciprocal", "select", "max_index",
            "sign"}

    def _self_dep_wait(self, eng, method, args, kw):
        wset, rset = [], []
        known = method in self._WR0
        if known:
            for j, a in enumerate(args):
                b = self._box(a) if hasattr(a, "tensor") else None
                if b is None:
                    continue
                (wset if j == 0 else rset).append(b)
            for k, a in kw.items():
                if hasattr(a, "tensor"):
                    b = self._box(a)
                    if b is not None:
                        (wset if k == "accum_out" else rset).append(b)
        mem = self.mem[eng]
        floor = self.water[eng].get(("e", eng), 0)
        need = 0
        if not known:
            need = self.ecnt[eng]
        else:
            def conflict(entry, mine_w, mine_r):
                t, lo, hi, cnt, is_w = entry
                for (t2, lo2, hi2) in mine_w:
                    if t2 == t and lo < hi2 and lo2 < hi:
                        return True
                if is_w:
                    for (t2, lo2, hi2) in mine_r:
                        if t2 == t and lo < hi2 and lo2 < hi:
                            return True
                return False
            keep = []
            for entry in mem:
                if entry[3] <= floor:
                    continue
                if conflict(entry, wset, rset):
                    need = max(need, entry[3])
                keep.append(entry)
            self.mem[eng] = mem = keep
        if need > 0:
            self._wait(eng, ("e", eng, need))
        return wset, rset, known

    def op(self, eng, method, *args, waits=(), inc=True, **kw):
        assert eng in ("tensor", "vector", "scalar", "gpsimd")
        if eng != "tensor":
            wset, rset, known = self._self_dep_wait(eng, method, args, kw)
        self._pre(eng, waits)
        tok = None
        if inc or eng != "tensor":
            self.ecnt[eng] += 1
            tok = ("e", eng, self.ecnt[eng])
            if eng != "tensor":
                n = self.ecnt[eng]
                if known:
                    for (t, lo, hi) in wset:
                        self.mem[eng].append((t, lo, hi, n, True))
                    for (t, lo, hi) in rset:
                        self.mem[eng].append((t, lo, hi, n, False))
                else:
                    self.mem[eng].append(("*", 0, 1 << 60, n, True))
        self.items[eng].append(("op", method, args, kw, tok))
        return tok if tok is not None else ("e", eng, self.ecnt[eng])

    def dma(self, q, out, in_, waits=(), slot="main"):
        assert q in ("sync", "gpsimd")
        if slot not in self.dsem:
            self.dsem[slot] = self.nc.alloc_semaphore(name=f"{self.tag}_s_{slot}")
            self.dcnt[slot] = 0
        self._pre(q, waits)
        self.dcnt[slot] += 16
        tok = ("d", slot, self.dcnt[slot])
        self.items[q].append(("dma", out, in_, tok))
        return tok

    def wait_only(self, eng, toks):
        for t in toks:
            self._wait(eng, t)

    def tok(self, eng):
        return ("e", eng, self.ecnt[eng])

    def set_barrier(self):
        self.barrier = ([("e", e, c) for e, c in self.ecnt.items() if c > 0]
                        + [("d", s, c) for s, c in self.dcnt.items() if c > 0])

    def _sem(self, key):
        kind, name = key
        return self.esem[name] if kind == "e" else self.dsem[name]

    def replay(self):
        with self.nc.Block() as blk:
            for eng in ENGS:
                items = self.items[eng]

                def make(items):
                    def run(e):
                        for it in items:
                            if it[0] == "wait":
                                e.wait_ge(self._sem(it[1]), it[2])
                            elif it[0] == "op":
                                inst = getattr(e, it[1])(*it[2], **it[3])
                                if it[4] is not None:
                                    inst.then_inc(self.esem[it[4][1]], 1)
                            else:
                                _, out, in_, tok = it
                                e.dma_start(out=out, in_=in_).then_inc(
                                    self.dsem[tok[1]], 16)
                    return run
                getattr(blk, eng)(make(items))


def mm_group(pb, parts, waits=()):
    n = len(parts)
    tok = None
    for k, (out, lh, rh, st_, sp_) in enumerate(parts):
        tok = pb.op("tensor", "matmul", out, lh, rh, start=st_, stop=sp_,
                    waits=(waits if k == 0 else ()), inc=(k == n - 1))
    return tok


class Rot:

    def __init__(self, banks):
        self.banks = banks
        self.evac = [None] * len(banks)
        self.i = 0

    def next(self):
        b = self.i % len(self.banks)
        self.i += 1
        return self.banks[b], self.evac[b], b

    def done(self, b, tok):
        self.evac[b] = tok


class Arena:
    def __init__(self, handle, ncols):
        self.h, self.ncols, self.off = handle, ncols, 0

    def take(self, n):
        assert self.off + n <= self.ncols, f"arena overflow {self.off}+{n}>{self.ncols}"
        o = self.off
        self.off += n
        return o

    def view(self, off, n):
        return self.h[:, off:off + n]


ACT_CONSTS = [1e-5, -0.5, 1e-6, 1.0]


def alloc_common(nc, pb, io, bf_cols=74000, f32_cols=16000):
    ar = {}
    ar["bf"] = Arena(nc.alloc_sbuf_tensor("arena_bf", [128, bf_cols], BF16), bf_cols)
    ar["f32"] = Arena(nc.alloc_sbuf_tensor("arena_f32", [128, f32_cols], F32), f32_cols)
    ar["P"] = [nc.alloc_psum_tensor(f"pr{i}", [128, 512], F32) for i in range(4)]
    ar["PTR"] = nc.alloc_psum_tensor("ptr", [128, 1024], BF16)
    ar["PZ"] = nc.alloc_psum_tensor("pz", [128, 512], F32)
    ar["P67"] = nc.alloc_psum_tensor("p67", [128, 1024], F32)
    f32, bf = ar["f32"], ar["bf"]
    o_c = f32.take(len(ACT_CONSTS))
    for j, v in enumerate(ACT_CONSTS):
        nc.const_aps.register(F32, v, f32.view(o_c + j, 1))
        pb.op("gpsimd", "memset", f32.view(o_c + j, 1), v)
    o_ident = bf.take(128)
    ar["o_ident"] = o_ident
    ar["t_ident"] = pb.dma("gpsimd", bf.view(o_ident, 128), io["ident"][:, :],
                           slot="ident")
    ar["mark"] = (bf.off, f32.off)
    return ar


def arena_reset(ar):
    ar["bf"].off, ar["f32"].off = ar["mark"]


class LNKit:

    def __init__(self, pb, ar):
        self.pb, self.f32 = pb, ar["f32"]
        self.o_st = self.f32.take(32)
        self.o_sq = self.f32.take(D)

    def st1(self, c):
        return self.f32.view(self.o_st + c, 1)

    def st(self, c):
        return self.f32.view(self.o_st + c, 4)

    def m1(self, s):
        return self.f32.view(self.o_st + 8 + s, 1)

    def rstd1(self, s):
        return self.f32.view(self.o_st + 12 + s, 1)

    def ln(self, src_s, dst_s, extra_waits=(), dst2_s=None):
        pb = self.pb
        c_ssum, c_s2, c_m, c_rstd, c_var, c_tmp = 0, 4, 8, 12, 16, 20
        for s in range(NS):
            pb.op("scalar", "activation", self.f32.view(self.o_sq, D), src_s(s),
                  AF.Square, accum_out=self.st1(c_s2 + s),
                  waits=list(extra_waits))
        tsq = pb.tok("scalar")
        for s in range(NS):
            pb.op("vector", "reduce_sum", self.st1(c_ssum + s), src_s(s),
                  axis=AX.X, waits=list(extra_waits))
        pb.op("vector", "tensor_scalar_mul", self.st(c_m), self.st(c_ssum), 1.0 / D)
        pb.op("vector", "tensor_scalar_mul", self.st(c_var), self.st(c_s2), 1.0 / D,
              waits=[tsq])
        pb.op("vector", "tensor_mul", self.st(c_tmp), self.st(c_m), self.st(c_m))
        tvar = pb.op("vector", "tensor_sub", self.st(c_var), self.st(c_var),
                     self.st(c_tmp))
        pb.op("scalar", "activation", self.st(c_rstd), self.st(c_var), AF.Ln,
              bias=LN_EPS, waits=[tvar])
        trstd = pb.op("scalar", "activation", self.st(c_rstd), self.st(c_rstd),
                      AF.Exp, scale=-0.5)
        tok = None
        for s in range(NS):
            tok = pb.op("vector", "tensor_scalar", dst_s(s), src_s(s),
                        self.st1(c_m + s), self.st1(c_rstd + s),
                        ALU.subtract, ALU.mult, waits=[trstd])
        return tok


def transpose_tile(pb, ar, src_tm_off, dst_fm_off, waits, t_tr_last):
    bf, PTR = ar["bf"], ar["PTR"]
    fm3d = bf.view(dst_fm_off, NCK * TT).rearrange("p (c t) -> p c t", c=NCK)
    ptr3d = PTR[:, 0:1024].rearrange("p (c t) -> p c t", c=NCK)
    for s in range(NS):
        w0 = list(waits) + ([t_tr_last] if t_tr_last else []) + [ar["t_ident"]]
        ttr = None
        for ck in range(NCK):
            ttr = pb.op("tensor", "transpose", PTR[:, ck * 128:(ck + 1) * 128],
                        bf.view(src_tm_off + s * D + ck * 128, 128),
                        bf.view(ar["o_ident"], 128),
                        waits=(w0 if ck == 0 else ()), inc=(ck == NCK - 1))
        t_tr_last = pb.op("vector", "tensor_copy",
                          fm3d[:, :, s * 128:(s + 1) * 128], ptr3d, waits=[ttr])
    return t_tr_last


def tile_in(ap_hbm, i):
    return ap_hbm[i * TT:(i + 1) * TT, :].rearrange("(s p) f -> p s f", p=128)



def emit_l0a(pb, io, ar):
    P, PTR, PZ = ar["P"], ar["PTR"], ar["PZ"]
    bf, f32 = ar["bf"], ar["f32"]
    arena_reset(ar)

    o_wq = bf.take(NCK * D)
    o_wk = bf.take(NCK * D)
    o_wvT = bf.take(NCK * D)
    o_proj = bf.take(NCK * 256)
    o_w2q = bf.take(NCK * 256)
    o_w2k = bf.take(NCK * 256)
    o_ones = bf.take(8)
    o_xnfm = bf.take(NCK * TT)
    o_xntm = bf.take(NS * D)
    o_qp = bf.take(2 * S)
    o_kp = bf.take(NS * 256)
    o_vtm = bf.take(NS * D)
    o_kvb = bf.take(2 * D)
    o_eluB = bf.take(512)

    o_xt = f32.take(NS * D)
    o_attn = o_xt
    o_xt2 = f32.take(NS * D)
    o_eluA = f32.take(512)
    o_kvacc = f32.take(2 * D)
    o_xo = f32.take(NS * D)
    o_zi = f32.take(8)
    lnk = LNKit(pb, ar)

    tw = []
    for (off, key, pat) in ((o_wq, "wq", "(c p) d -> p c d"),
                            (o_wk, "wk", "(c p) d -> p c d"),
                            (o_wvT, "wvT", "(c p) d -> p c d"),
                            (o_proj, "proj", "(c p) q -> p c q")):
        n = NCK * (256 if key == "proj" else D)
        tw.append(pb.dma("gpsimd", bf.view(off, n),
                         io[key].rearrange(pat, p=128), slot="l0a_w"))
    t_ones = pb.op("gpsimd", "memset", bf.view(o_ones, 1), 1.0)
    tw.append(t_ones)

    rot = Rot(P)

    def group(parts_fn, width, evac_fn, extra=()):
        bank, banktok, bi = rot.next()
        w = list(extra) + ([banktok] if banktok else [])
        mmtok = mm_group(pb, parts_fn(bank), waits=w)
        evtok = evac_fn(bank, mmtok)
        rot.done(bi, evtok)
        return evtok

    def ev_copy_bf(dst_off, width):
        def f(bank, mmtok):
            return pb.op("vector", "tensor_copy", bf.view(dst_off, width),
                         bank[:, :width], waits=[mmtok])
        return f

    def ev_kv(dst_off, width, first):
        def f(bank, mmtok):
            if first:
                return pb.op("vector", "tensor_copy", f32.view(dst_off, width),
                             bank[:, :width], waits=[mmtok])
            return pb.op("vector", "tensor_add", f32.view(dst_off, width),
                         f32.view(dst_off, width), bank[:, :width], waits=[mmtok])
        return f

    def ev_elu(dst_off, width):
        def f(bank, mmtok):
            tmin = pb.op("vector", "tensor_scalar_min", f32.view(o_eluA, width),
                         bank[:, :width], 0.0, waits=[mmtok])
            texp = pb.op("scalar", "activation", bf.view(o_eluB, width),
                         f32.view(o_eluA, width), AF.Exp, waits=[tmin])
            pb.op("vector", "tensor_scalar_max", bf.view(dst_off, width),
                  bank[:, :width], 0.0)
            return pb.op("vector", "tensor_add", bf.view(dst_off, width),
                         bf.view(dst_off, width), bf.view(o_eluB, width),
                         waits=[texp])
        return f

    for (o_w, o_dst) in ((o_wq, o_w2q), (o_wk, o_w2k)):
        for db in range(NCK):
            group(lambda bank, o_w=o_w, db=db: [
                (bank[:, :256], bf.view(o_w + ck * D + db * 128, 128),
                 bf.view(o_proj + ck * 256, 256), ck == 0, ck == NCK - 1)
                for ck in range(NCK)],
                256, ev_copy_bf(o_dst + db * 256, 256), extra=tw)

    t_groups_done = None
    t_apply_prev = None
    t_tr_last = None
    for i in range(NTL):
        lw = [t_apply_prev] if t_apply_prev else []
        tl = pb.dma("sync", f32.view(o_xt, NS * D), tile_in(io["x_in"], i),
                    waits=lw, slot="l0a_xt")
        t_apply = lnk.ln(lambda s: f32.view(o_xt + s * D, D),
                         lambda s: bf.view(o_xntm + s * D, D),
                         extra_waits=[tl])
        t_apply_prev = t_apply
        w0 = [t_apply] + ([t_groups_done] if t_groups_done else [])
        t_tr_last = transpose_tile(pb, ar, o_xntm, o_xnfm, w0, t_tr_last)
        t_xnfm = t_tr_last
        for half in range(2):
            group(lambda bank, half=half: [
                (bank[:, :TT], bf.view(o_w2q + ck * 256 + half * 128, 128),
                 bf.view(o_xnfm + ck * TT, TT), ck == 0, ck == NCK - 1)
                for ck in range(NCK)],
                TT, ev_elu(o_qp + half * S + i * TT, TT), extra=[t_xnfm])
        for s in range(NS):
            group(lambda bank, s=s: [
                (bank[:, :256], bf.view(o_xnfm + ck * TT + s * 128, 128),
                 bf.view(o_w2k + ck * 256, 256), ck == 0, ck == NCK - 1)
                for ck in range(NCK)],
                256, ev_elu(o_kp + s * 256, 256), extra=[t_xnfm])
        for s in range(NS):
            for h in range(2):
                group(lambda bank, s=s, h=h: [
                    (bank[:, :512], bf.view(o_xnfm + ck * TT + s * 128, 128),
                     bf.view(o_wvT + ck * D + h * 512, 512), ck == 0, ck == NCK - 1)
                    for ck in range(NCK)],
                    512, ev_copy_bf(o_vtm + s * D + h * 512, 512), extra=[t_xnfm])
        t_ready = pb.tok("vector")
        for ph in range(2):
            for dh in range(2):
                group(lambda bank, ph=ph, dh=dh: [
                    (bank[:, :512], bf.view(o_kp + s * 256 + ph * 128, 128),
                     bf.view(o_vtm + s * D + dh * 512, 512), s == 0, s == NS - 1)
                    for s in range(NS)],
                    512, ev_kv(o_kvacc + ph * D + dh * 512, 512, i == 0),
                    extra=[t_ready])
        t_groups_done = pb.tok("tensor")

    t_kvb = None
    for q in range(4):
        t_kvb = pb.op("vector", "tensor_copy", bf.view(o_kvb + q * 512, 512),
                      f32.view(o_kvacc + q * 512, 512))

    t_zinv_prev = None
    t_res_prev = None
    t_store = [None] * NTL
    for i in range(NTL):
        lw = [t_res_prev] if t_res_prev else []
        tl2 = pb.dma("sync", f32.view(o_xt2, NS * D), tile_in(io["x_in"], i),
                     waits=lw, slot="l0a_xt2")
        zparts = []
        for s in range(NS):
            for pc in range(2):
                zparts.append((PZ[:, s:s + 1],
                               bf.view(o_qp + pc * S + i * TT + s * 128, 128),
                               bf.view(o_ones, 1), pc == 0, pc == 1))
        zw = [t_kvb] + ([t_zinv_prev] if t_zinv_prev else [])
        tzmm = mm_group(pb, zparts, waits=zw)
        pb.op("vector", "tensor_scalar_add", f32.view(o_zi, 4),
              PZ[:, 0:4], 1e-8, waits=[tzmm])
        t_zinv_prev = pb.op("vector", "reciprocal", f32.view(o_zi, 4),
                            f32.view(o_zi, 4))
        for s in range(NS):
            for h in range(2):
                def ev(bank, mmtok, s=s, h=h):
                    return pb.op("vector", "tensor_single_scalar",
                                 f32.view(o_attn + s * D + h * 512, 512),
                                 bank[:, :512], f32.view(o_zi + s, 1), ALU.mult,
                                 waits=[mmtok])
                group(lambda bank, s=s, h=h: [
                    (bank[:, :512], bf.view(o_qp + pc * S + i * TT + s * 128, 128),
                     bf.view(o_kvb + pc * D + h * 512, 512), pc == 0, pc == 1)
                    for pc in range(2)],
                    512, ev)
        t_attn = pb.tok("vector")
        if i > 0:
            pb.wait_only("vector", [t_store[i - 1]])
        lnk.ln(lambda s: f32.view(o_attn + s * D, D),
               lambda s: f32.view(o_xo + s * D, D), extra_waits=[t_attn])
        tres = None
        for s in range(NS):
            tres = pb.op("vector", "tensor_add", f32.view(o_xo + s * D, D),
                         f32.view(o_xo + s * D, D), f32.view(o_xt2 + s * D, D),
                         waits=[tl2])
        t_res_prev = tres
        t_store[i] = pb.dma("sync", tile_in(io["x_out"], i),
                            f32.view(o_xo, NS * D), waits=[tres], slot="l0a_xo")
    return t_store[-1]



def emit_l0f(pb, io, ar):
    P, PZ = ar["P"], ar["PZ"]
    bf, f32 = ar["bf"], ar["f32"]
    arena_reset(ar)
    F = 2048
    NFC = F // 128

    o_mask = f32.take(NTL * NS * 8)
    o_iota = f32.take(8)
    lnk = LNKit(pb, ar)
    o_logit = f32.take(NS * 8)
    o_mx = f32.take(4)
    o_mx8 = f32.take(8)
    o_zero8 = f32.take(8)
    o_idx = f32.take(4)
    o_sc = f32.take(4)
    idx_u32 = pb.nc.alloc_sbuf_tensor("l0f_idx", [128, 8], mybir.dt.uint32)
    o_acc = f32.take(NS * D)
    o_xt = f32.take(NS * D)
    o_xo = f32.take(NS * D)

    o_gwf = f32.take(NCK * 8)
    o_xcfm = f32.take(NCK * 128)
    o_identf = f32.take(128)
    mark2 = (bf.off, f32.off)

    o_xntm = bf.take(NS * D)
    o_xstm = bf.take(NS * D)
    o_xnfm = bf.take(NCK * TT)
    o_xsfm = bf.take(NCK * TT)

    t_gw = pb.dma("sync", f32.view(o_gwf, NCK * 8),
                  io["gwT"].rearrange("(c p) e -> p c e", p=128), slot="l0f_gw")
    t_idf = pb.op("vector", "tensor_copy", f32.view(o_identf, 128),
                  bf.view(ar["o_ident"], 128), waits=[ar["t_ident"]])
    t_iota = None
    for e in range(8):
        t_iota = pb.op("gpsimd", "memset", f32.view(o_iota + e, 1), float(e))
    t_iota = pb.op("gpsimd", "memset", f32.view(o_zero8, 8), 0.0)

    rot = Rot(P)

    def group(parts_fn, width, evac_fn, extra=()):
        bank, banktok, bi = rot.next()
        w = list(extra) + ([banktok] if banktok else [])
        mmtok = mm_group(pb, parts_fn(bank), waits=w)
        evtok = evac_fn(bank, mmtok)
        rot.done(bi, evtok)
        return evtok

    xnfm_hbm = io["xnfm_h"].rearrange("c p t -> p c t")
    xsfm_hbm = io["xsfm_h"].rearrange("c p t -> p c t")

    t_apply_prev = None
    t_tr_last = None
    t_pz_last = None
    t_xfst = [None] * NTL
    t_xsst = [None] * NTL
    t_route_pe = None
    for i in range(NTL):
        lw = ([t_apply_prev] if t_apply_prev else []) \
            + ([t_route_pe] if t_route_pe else [])
        tl = pb.dma("sync", f32.view(o_xt, NS * D), tile_in(io["x_in"], i),
                    waits=lw, slot="l0f_xt")
        t_apply = lnk.ln(lambda s: f32.view(o_xt + s * D, D),
                         lambda s: bf.view(o_xntm + s * D, D), extra_waits=[tl])
        w0 = [t_apply] + ([t_xfst[i - 1]] if i > 0 else [])
        t_tr_last = transpose_tile(pb, ar, o_xntm, o_xnfm, w0, t_tr_last)
        t_xnfm = t_tr_last
        for s in range(NS):
            pb.op("vector", "tensor_single_scalar", f32.view(o_xt + s * D, D),
                  f32.view(o_xt + s * D, D), lnk.m1(s), ALU.subtract,
                  waits=[t_apply])
        t_rt = None
        for s in range(NS):
            for half in range(2):
                w0 = [pb.tok("vector")] + ([t_pz_last] if t_pz_last else [])
                ttr = None
                for k in range(4):
                    ck = half * 4 + k
                    ttr = pb.op("tensor", "transpose",
                                PZ[:, k * 128:(k + 1) * 128],
                                f32.view(o_xt + s * D + ck * 128, 128),
                                f32.view(o_identf, 128),
                                waits=(w0 + [t_idf] if k == 0 else ()),
                                inc=(k == 3))
                t_pz_last = pb.op("vector", "tensor_copy",
                                  f32.view(o_xcfm + half * 512, 512),
                                  PZ[:, 0:512], waits=[ttr])
            def ev_logit(bank, mmtok, s=s):
                return pb.op("vector", "tensor_single_scalar",
                             f32.view(o_logit + s * 8, 8), bank[:, :8],
                             lnk.rstd1(s), ALU.mult, waits=[mmtok])
            group(lambda bank, s=s: [
                (bank[:, :8], f32.view(o_xcfm + ck * 128, 128),
                 f32.view(o_gwf + ck * 8, 8), ck == 0, ck == NCK - 1)
                for ck in range(NCK)],
                8, ev_logit, extra=[t_pz_last, t_gw])
            pb.op("vector", "reduce_max", f32.view(o_mx + s, 1),
                  f32.view(o_logit + s * 8, 8), axis=AX.X)
            pb.op("vector", "tensor_single_scalar", f32.view(o_mx8, 8),
                  f32.view(o_zero8, 8), f32.view(o_mx + s, 1), ALU.add,
                  waits=[t_iota])
            pb.op("vector", "max_index", idx_u32[:, 0:8],
                  f32.view(o_mx8, 8), f32.view(o_logit + s * 8, 8))
            pb.op("vector", "tensor_copy", f32.view(o_idx + s, 1),
                  idx_u32[:, 0:1])
            tsg = pb.op("scalar", "activation", f32.view(o_sc + s, 1),
                        f32.view(o_mx + s, 1), AF.Sigmoid,
                        waits=[pb.tok("vector")])
            t_rt = pb.op("vector", "tensor_single_scalar",
                         f32.view(o_mask + (i * NS + s) * 8, 8),
                         f32.view(o_iota, 8),
                         f32.view(o_idx + s, 1), ALU.is_equal)
            pb.op("vector", "tensor_single_scalar", bf.view(o_xstm + s * D, D),
                  bf.view(o_xntm + s * D, D), f32.view(o_sc + s, 1), ALU.mult,
                  waits=[tsg])
        t_xs = pb.tok("vector")
        t_apply_prev = t_xs
        t_route_pe = pb.tok("tensor")
        w1 = [t_xs] + ([t_xsst[i - 1]] if i > 0 else [])
        t_tr_last = transpose_tile(pb, ar, o_xstm, o_xsfm, w1, t_tr_last)
        t_xfst[i] = pb.dma("sync", xnfm_hbm[:, :, i * TT:(i + 1) * TT],
                           bf.view(o_xnfm, NCK * TT).rearrange("p (c t) -> p c t", c=NCK),
                           waits=[t_xnfm], slot="l0f_xnst")
        t_xsst[i] = pb.dma("sync", xsfm_hbm[:, :, i * TT:(i + 1) * TT],
                           bf.view(o_xsfm, NCK * TT).rearrange("p (c t) -> p c t", c=NCK),
                           waits=[t_tr_last], slot="l0f_xsst")
    t_stageA_pe = pb.tok("tensor")
    t_stageA_ve = pb.tok("vector")

    bf.off, f32.off = mark2
    o_wg = bf.take(NCK * F)
    o_wu = bf.take(NCK * F)
    o_wd = bf.take(NFC * D)
    o_xf = bf.take(NCK * TT)
    o_gs = bf.take(4 * TT)
    o_h = bf.take(NFC * TT)

    t_accst = [None] * NTL
    t_pe_pass_end = t_stageA_pe
    o_xf2 = bf.take(NCK * TT)
    o_acc2 = o_xt
    xf_off = {0: o_xf, 1: o_xf2}
    acc_off = {0: o_acc, 1: o_acc2}
    t_pe_par = {0: None, 1: None}
    t_ve_par = {0: None, 1: None}
    t_st_par = {0: None, 1: None}

    def issue_loads(e, i):
        p = i % 2
        src_h = xnfm_hbm if e == 0 else xsfm_hbm
        lw = [t_xfst[i] if e == 0 else t_xsst[i]]
        if t_pe_par[p]:
            lw.append(t_pe_par[p])
        txf = pb.dma("sync", bf.view(xf_off[p], NCK * TT)
                     .rearrange("p (c t) -> p c t", c=NCK),
                     src_h[:, :, i * TT:(i + 1) * TT], waits=lw, slot="l0f_xf")
        tacc = None
        if e > 0:
            aw = [t_accst[i]]
            for t in (t_ve_par[p], t_st_par[p]):
                if t:
                    aw.append(t)
            tacc = pb.dma("sync", f32.view(acc_off[p], NS * D),
                          tile_in(io["acc_h"], i), waits=aw, slot="l0f_acc")
        return txf, tacc

    t_gu_done = None
    for e in range(9):
        wld = [t_stageA_ve, t_xfst[-1], t_xsst[-1]]
        wld.append(t_gu_done if t_gu_done else t_pe_pass_end)
        twg = pb.dma("gpsimd", bf.view(o_wg, NCK * F),
                     io["gate_all"][e].rearrange("(c p) f -> p c f", p=128),
                     waits=wld, slot="l0f_w")
        twu = pb.dma("gpsimd", bf.view(o_wu, NCK * F),
                     io["up_all"][e].rearrange("(c p) f -> p c f", p=128),
                     slot="l0f_w")
        twd = pb.dma("gpsimd", bf.view(o_wd, NFC * D),
                     io["down_all"][e].rearrange("(c p) h -> p c h", p=128),
                     waits=[t_pe_pass_end], slot="l0f_w")
        pend = issue_loads(e, 0)
        for i in range(NTL):
            p = i % 2
            t_xf, t_acc = pend
            if i + 1 < NTL:
                pend = issue_loads(e, i + 1)
            o_xfc = xf_off[p]
            o_accc = acc_off[p]
            t_sil = [None] * NFC
            t_uev = [None] * NFC
            for cm in range(NFC):
                def ev_g(bank, mmtok, cm=cm):
                    w = [mmtok] + ([t_uev[cm - 4]] if cm >= 4 else [])
                    t_sil[cm] = pb.op("scalar", "activation",
                                      bf.view(o_gs + (cm % 4) * TT, TT), bank[:, :TT],
                                      AF.Silu, waits=w)
                    return t_sil[cm]
                group(lambda bank, cm=cm: [
                    (bank[:, :TT], bf.view(o_wg + ck * F + cm * 128, 128),
                     bf.view(o_xfc + ck * TT, TT), ck == 0, ck == NCK - 1)
                    for ck in range(NCK)],
                    TT, ev_g, extra=[t_xf, twg])
                def ev_u(bank, mmtok, cm=cm):
                    t_uev[cm] = pb.op("vector", "tensor_mul",
                                      bf.view(o_h + cm * TT, TT),
                                      bf.view(o_gs + (cm % 4) * TT, TT),
                                      bank[:, :TT], waits=[mmtok, t_sil[cm]])
                    return t_uev[cm]
                group(lambda bank, cm=cm: [
                    (bank[:, :TT], bf.view(o_wu + ck * F + cm * 128, 128),
                     bf.view(o_xfc + ck * TT, TT), ck == 0, ck == NCK - 1)
                    for ck in range(NCK)],
                    TT, ev_u, extra=[t_xf, twu])
            t_h = pb.tok("vector")
            t_gu_mark = pb.tok("tensor")
            for s in range(NS):
                for hh in range(2):
                    def ev_d(bank, mmtok, s=s, hh=hh):
                        dst = f32.view(o_accc + s * D + hh * 512, 512)
                        if e == 0:
                            w0 = [mmtok] + ([t_st_par[p]] if t_st_par[p] else [])
                            return pb.op("vector", "tensor_copy", dst,
                                         bank[:, :512], waits=w0)
                        mcol = f32.view(o_mask + (i * NS + s) * 8 + (e - 1), 1)
                        return pb.op("vector", "scalar_tensor_tensor", dst,
                                     bank[:, :512], mcol, dst,
                                     ALU.mult, ALU.add, waits=[mmtok, t_acc])
                    group(lambda bank, s=s, hh=hh: [
                        (bank[:, :512], bf.view(o_h + fc * TT + s * 128, 128),
                         bf.view(o_wd + fc * D + hh * 512, 512), fc == 0, fc == NFC - 1)
                        for fc in range(NFC)],
                        512, ev_d, extra=[t_h, twd])
            if i == NTL - 1:
                t_gu_done = t_gu_mark
            t_dn = pb.tok("vector")
            q = "sync" if i == NTL - 1 else "gpsimd"
            t_accst[i] = pb.dma(q, tile_in(io["acc_h"], i),
                                f32.view(o_accc, NS * D), waits=[t_dn],
                                slot="l0f_accst" + ("2" if q == "sync" else ""))
            t_st_par[p] = t_accst[i]
            t_ve_par[p] = t_dn
            t_pe_par[p] = pb.tok("tensor")
        t_pe_pass_end = pb.tok("tensor")

    t_store = [None] * NTL
    t_ep_prev = None
    for i in range(NTL):
        lw = [t_accst[i], t_accst[NTL - 1]] + ([t_ep_prev] if t_ep_prev else [])
        ta = pb.dma("sync", f32.view(o_acc, NS * D), tile_in(io["acc_h"], i),
                    waits=lw, slot="l0f_acc")
        tl = pb.dma("sync", f32.view(o_xt, NS * D), tile_in(io["x_in"], i),
                    waits=lw, slot="l0f_xt")
        if i > 0:
            pb.wait_only("vector", [t_store[i - 1]])
        lnk.ln(lambda s: f32.view(o_acc + s * D, D),
               lambda s: f32.view(o_xo + s * D, D), extra_waits=[ta])
        tres = None
        for s in range(NS):
            tres = pb.op("vector", "tensor_add", f32.view(o_xo + s * D, D),
                         f32.view(o_xo + s * D, D), f32.view(o_xt + s * D, D),
                         waits=[tl])
        t_ep_prev = tres
        t_store[i] = pb.dma("sync", tile_in(io["x_out"], i),
                            f32.view(o_xo, NS * D), waits=[tres], slot="l0f_xo")
    return t_store[-1]



def emit_l1f(pb, io, ar):
    P, PTR, PZ = ar["P"], ar["PTR"], ar["PZ"]
    bf, f32 = ar["bf"], ar["f32"]
    arena_reset(ar)
    SH = TT + 1

    o_wrT = bf.take(NCK * D)
    o_wkT = bf.take(NCK * D)
    o_wvT = bf.take(NCK * D)
    o_xntm = bf.take(NS * D)
    o_xsh = bf.take(NCK * SH)
    o_xr = bf.take(NCK * TT)
    o_xk = bf.take(NCK * TT)
    o_rtm = bf.take(NS * D)
    o_kact = bf.take(NCK * TT)
    o_tmpb = bf.take(512)
    o_tmpc = bf.take(512)

    o_xt = f32.take(NS * D)
    o_xo = f32.take(NS * D)
    o_out = f32.take(NS * D)
    o_mu = f32.take(32)
    o_identf = f32.take(128)
    o_mstg = f32.take(128)
    lnk = LNKit(pb, ar)

    tw = []
    for (off, key) in ((o_wrT, "wrT"), (o_wkT, "wkT"), (o_wvT, "wvT")):
        tw.append(pb.dma("gpsimd", bf.view(off, NCK * D),
                         io[key].rearrange("(c p) d -> p c d", p=128),
                         slot="l1f_w"))
    t_idf = pb.op("vector", "tensor_copy", f32.view(o_identf, 128),
                  bf.view(ar["o_ident"], 128), waits=[ar["t_ident"]])

    t_mu = None
    for j, key in enumerate(("mu_r8", "mu_k8")):
        tm = pb.dma("sync", f32.view(o_mstg, 128)[0:8, :],
                    io[key][:, :], slot="l1f_mu", waits=[t_mu] if t_mu else ())
        ttr = pb.op("tensor", "transpose", PZ[:, 0:8],
                    f32.view(o_mstg, 128)[0:8, :],
                    f32.view(o_identf, 128)[0:8, 0:8], waits=[tm, t_idf], inc=True)
        t_mu = pb.op("vector", "tensor_copy", f32.view(o_mu + j * 8, 8),
                     PZ[:, 0:8], waits=[ttr])
        t_mu = pb.op("vector", "tensor_scalar", f32.view(o_mu + 16 + j * 8, 8),
                     f32.view(o_mu + j * 8, 8), -1.0, 1.0, ALU.mult, ALU.add)

    t_z = None
    for ck in range(NCK):
        t_z = pb.op("vector", "memset", bf.view(o_xsh + ck * SH, 1), 0.0)

    rot = Rot(P)

    def group(parts_fn, width, evac_fn, extra=()):
        bank, banktok, bi = rot.next()
        w = list(extra) + ([banktok] if banktok else [])
        mmtok = mm_group(pb, parts_fn(bank), waits=w)
        evtok = evac_fn(bank, mmtok)
        rot.done(bi, evtok)
        return evtok

    xsh3d = bf.view(o_xsh, NCK * SH).rearrange("p (c t) -> p c t", c=NCK)
    ptr3d = ar["PTR"][:, 0:1024].rearrange("p (c t) -> p c t", c=NCK)

    t_apply_prev = None
    t_tr_last = t_z
    t_pe_prev = None
    t_store = [None] * NTL
    for i in range(NTL):
        lw = ([t_apply_prev] if t_apply_prev else [])
        tl = pb.dma("sync", f32.view(o_xt, NS * D), tile_in(io["x_in"], i),
                    waits=lw, slot="l1f_xt")
        t_apply = lnk.ln(lambda s: f32.view(o_xt + s * D, D),
                         lambda s: bf.view(o_xntm + s * D, D), extra_waits=[tl])
        t_apply_prev = t_apply
        if i > 0:
            tcar = None
            for ck in range(NCK):
                tcar = pb.op("vector", "tensor_copy", bf.view(o_xsh + ck * SH, 1),
                             bf.view(o_xsh + ck * SH + TT, 1),
                             waits=[t_pe_prev] if (ck == 0 and t_pe_prev) else ())
            t_tr_last = tcar
        for s in range(NS):
            w0 = [t_apply] + ([t_tr_last] if t_tr_last else []) + [ar["t_ident"]]
            ttr = None
            for ck in range(NCK):
                ttr = pb.op("tensor", "transpose", PTR[:, ck * 128:(ck + 1) * 128],
                            bf.view(o_xntm + s * D + ck * 128, 128),
                            bf.view(ar["o_ident"], 128),
                            waits=(w0 if ck == 0 else ()), inc=(ck == NCK - 1))
            t_tr_last = pb.op("vector", "tensor_copy",
                              xsh3d[:, :, 1 + s * 128:1 + (s + 1) * 128], ptr3d,
                              waits=[ttr])
        t_mix = None
        for ck in range(NCK):
            for (o_dst, cmu) in ((o_xr, 0), (o_xk, 8)):
                pb.op("vector", "tensor_single_scalar", bf.view(o_tmpc, 512),
                      bf.view(o_xsh + ck * SH, TT), f32.view(o_mu + 16 + cmu + ck, 1),
                      ALU.mult, waits=[t_tr_last, t_mu])
                t_mix = pb.op("vector", "scalar_tensor_tensor",
                              bf.view(o_dst + ck * TT, TT),
                              bf.view(o_xsh + ck * SH + 1, TT),
                              f32.view(o_mu + cmu + ck, 1),
                              bf.view(o_tmpc, 512), ALU.mult, ALU.add)
        for s in range(NS):
            for hh in range(2):
                def ev_r(bank, mmtok, s=s, hh=hh):
                    return pb.op("scalar", "activation",
                                 bf.view(o_rtm + s * D + hh * 512, 512),
                                 bank[:, :512], AF.Sigmoid, waits=[mmtok])
                group(lambda bank, s=s, hh=hh: [
                    (bank[:, :512], bf.view(o_xr + ck * TT + s * 128, 128),
                     bf.view(o_wrT + ck * D + hh * 512, 512), ck == 0, ck == NCK - 1)
                    for ck in range(NCK)],
                    512, ev_r, extra=[t_mix, tw[0]])
        for cm in range(NCK):
            def ev_k(bank, mmtok, cm=cm):
                pb.op("vector", "tensor_scalar_max", bf.view(o_tmpb, 512),
                      bank[:, :512], 0.0, waits=[mmtok])
                return pb.op("vector", "tensor_mul", bf.view(o_kact + cm * TT, TT),
                             bf.view(o_tmpb, 512), bf.view(o_tmpb, 512))
            group(lambda bank, cm=cm: [
                (bank[:, :512], bf.view(o_wkT + ck * D + cm * 128, 128),
                 bf.view(o_xk + ck * TT, TT), ck == 0, ck == NCK - 1)
                for ck in range(NCK)],
                512, ev_k, extra=[t_mix, tw[1]])
        t_kact = pb.tok("vector")
        for s in range(NS):
            for hh in range(2):
                def ev_o(bank, mmtok, s=s, hh=hh):
                    return pb.op("vector", "tensor_mul",
                                 f32.view(o_out + s * D + hh * 512, 512),
                                 bf.view(o_rtm + s * D + hh * 512, 512),
                                 bank[:, :512], waits=[mmtok])
                group(lambda bank, s=s, hh=hh: [
                    (bank[:, :512], bf.view(o_kact + ck * TT + s * 128, 128),
                     bf.view(o_wvT + ck * D + hh * 512, 512), ck == 0, ck == NCK - 1)
                    for ck in range(NCK)],
                    512, ev_o, extra=[t_kact, tw[2]])
        t_out = pb.tok("vector")
        t_pe_prev = pb.tok("tensor")
        if i > 0:
            pb.wait_only("vector", [t_store[i - 1]])
        lnk.ln(lambda s: f32.view(o_out + s * D, D),
               lambda s: f32.view(o_xo + s * D, D), extra_waits=[t_out])
        tres = None
        for s in range(NS):
            tres = pb.op("vector", "tensor_add", f32.view(o_xo + s * D, D),
                         f32.view(o_xo + s * D, D), f32.view(o_xt + s * D, D),
                         waits=[tl])
        t_store[i] = pb.dma("sync", tile_in(io["x_out"], i),
                            f32.view(o_xo, NS * D), waits=[tres], slot="l1f_xo")
    return t_store[-1]



def emit_l1a(pb, io, ar):
    P, PTR, PZ, P67 = ar["P"], ar["PTR"], ar["PZ"], ar["P67"]
    bf, f32 = ar["bf"], ar["f32"]
    arena_reset(ar)
    NH, HTT = 16, 256

    o_wq = bf.take(NCK * D)
    o_wk = bf.take(NCK * D)
    o_wv = bf.take(NCK * D)
    o_wo = bf.take(NCK * D)
    o_wg1 = bf.take(NCK * 16)
    o_wg2 = bf.take(D)
    o_ctab = bf.take(TT)
    o_tri = bf.take(512)
    o_xnfm = bf.take(NCK * TT)
    o_qR = o_xnfm
    o_xntm = bf.take(NS * D)
    o_qsil = bf.take(NCK * TT)
    o_gate = o_qsil
    o_E = bf.take(NCK * TT)
    o_onfm = o_E
    o_t1 = bf.take(HTT)
    o_ELr = bf.take(NH * HTT)
    o_ELrtm = bf.take(2 * 2048)
    o_vtm = bf.take(NS * D)
    o_Ub = bf.take(NH * 64)
    o_Asb = bf.take(2 * NH * 128)
    o_g1 = bf.take(512)
    o_otm = bf.take(NS * D)

    o_xt = f32.take(NS * D)
    o_xo = f32.take(NS * D)
    o_C = f32.take(NCK * TT)
    o_out = o_C
    o_carry = f32.take(8)
    o_U = f32.take(NH * 64)
    o_nwf = f32.take(8)
    o_rms = f32.take(8)
    o_identf = f32.take(128)
    o_stg = f32.take(128)
    lnk = LNKit(pb, ar)

    tw = []
    for (off, key, wdt) in ((o_wq, "wq", D), (o_wk, "wk", D), (o_wv, "wv", D),
                            (o_wo, "wo", D), (o_wg1, "wg1", 16)):
        tw.append(pb.dma("gpsimd", bf.view(off, NCK * wdt),
                         io[key].rearrange("(c p) d -> p c d", p=128),
                         slot="l1a_w"))
    tw.append(pb.dma("gpsimd", bf.h[0:16, o_wg2:o_wg2 + D], io["wg2"][:, :],
                     slot="l1a_w"))
    o_dup = bf.take(128)
    tw.append(pb.dma("gpsimd", bf.view(o_dup, 128), io["dup"][:, :],
                     slot="l1a_w"))
    t_tri = pb.dma("gpsimd", bf.view(o_tri, 512), io["tri4"][:, :],
                   slot="l1a_w")
    t_idf = pb.op("vector", "tensor_copy", f32.view(o_identf, 128),
                  bf.view(ar["o_ident"], 128), waits=[ar["t_ident"]])
    tnw0 = pb.dma("sync", f32.view(o_stg, 128)[0:8, :], io["nw8"][:, :],
                  slot="l1a_nw")
    tnw1 = pb.op("tensor", "transpose", PZ[:, 0:8], f32.view(o_stg, 128)[0:8, :],
                 f32.view(o_identf, 128)[0:8, 0:8], waits=[tnw0, t_idf], inc=True)
    t_nw = pb.op("vector", "tensor_copy", f32.view(o_nwf, 8), PZ[:, 0:8],
                 waits=[tnw1])
    t_u = None
    for g in range(2):
        pb.op("vector", "memset", f32.view(o_U + g * 512, 512), 0.0)
        t_u = pb.op("vector", "memset", bf.view(o_Ub + g * 512, 512), 0.0)

    rot = Rot(P)

    def group(parts_fn, width, evac_fn, extra=()):
        bank, banktok, bi = rot.next()
        w = list(extra) + ([banktok] if banktok else [])
        mmtok = mm_group(pb, parts_fn(bank), waits=w)
        evtok = evac_fn(bank, mmtok)
        rot.done(bi, evtok)
        return evtok

    t_apply_prev = None
    t_tr_last = None
    t_store = [None] * NTL
    t_lrpe_prev = []
    t_Ubf = t_u
    for i in range(NTL):
        lw = ([t_apply_prev] if t_apply_prev else [])
        tl = pb.dma("sync", f32.view(o_xt, NS * D), tile_in(io["x_in"], i),
                    waits=lw, slot="l1a_xt")
        tct = pb.dma("gpsimd", bf.view(o_ctab, TT),
                     io["costab"][:, i * TT:(i + 1) * TT],
                     waits=list(t_lrpe_prev), slot="l1a_ct")
        t_apply = lnk.ln(lambda s: f32.view(o_xt + s * D, D),
                         lambda s: bf.view(o_xntm + s * D, D), extra_waits=[tl])
        t_apply_prev = t_apply
        t_tr_last = transpose_tile(pb, ar, o_xntm, o_xnfm, [t_apply], t_tr_last)
        t_xnfm = t_tr_last
        tE = [None] * NCK
        for cm in range(NCK):
            def ev_q(bank, mmtok, cm=cm):
                return pb.op("scalar", "activation", bf.view(o_qsil + cm * TT, TT),
                             bank[:, :TT], AF.Silu, waits=[mmtok])
            group(lambda bank, cm=cm: [
                (bank[:, :TT], bf.view(o_wq + ck * D + cm * 128, 128),
                 bf.view(o_xnfm + ck * TT, TT), ck == 0, ck == NCK - 1)
                for ck in range(NCK)],
                TT, ev_q, extra=[t_xnfm, tw[0]])
            def ev_e(bank, mmtok, cm=cm):
                tE[cm] = pb.op("scalar", "activation", bf.view(o_E + cm * TT, TT),
                               bank[:, :TT], AF.Exp, waits=[mmtok])
                return tE[cm]
            group(lambda bank, cm=cm: [
                (bank[:, :TT], bf.view(o_wk + ck * D + cm * 128, 128),
                 bf.view(o_xnfm + ck * TT, TT), ck == 0, ck == NCK - 1)
                for ck in range(NCK)],
                TT, ev_e, extra=[t_xnfm, tw[1]])
        for s in range(NS):
            for hh in range(2):
                def ev_v(bank, mmtok, s=s, hh=hh):
                    return pb.op("vector", "tensor_copy",
                                 bf.view(o_vtm + s * D + hh * 512, 512),
                                 bank[:, :512], waits=[mmtok])
                group(lambda bank, s=s, hh=hh: [
                    (bank[:, :512], bf.view(o_xnfm + ck * TT + s * 128, 128),
                     bf.view(o_wv + ck * D + hh * 512, 512), ck == 0, ck == NCK - 1)
                    for ck in range(NCK)],
                    512, ev_v, extra=[t_xnfm, tw[2]])
        def ev_g1(bank, mmtok):
            return pb.op("vector", "tensor_copy", bf.h[0:16, o_g1:o_g1 + 512],
                         bank[0:16, :512], waits=[mmtok])
        group(lambda bank: [
            (bank[0:16, :512], bf.view(o_wg1 + ck * 16, 16),
             bf.view(o_xnfm + ck * TT, TT), ck == 0, ck == NCK - 1)
            for ck in range(NCK)],
            512, ev_g1, extra=[t_xnfm, tw[4]])
        t_qkv_pe = pb.tok("tensor")
        t_scan = None
        for ck in range(NCK):
            ini = 1.0 if i == 0 else f32.view(o_carry + ck, 1)
            pb.op("vector", "tensor_tensor_scan", f32.view(o_C + ck * TT, TT),
                  bf.view(o_E + ck * TT, TT), bf.view(o_E + ck * TT, TT), ini,
                  ALU.add, ALU.bypass, waits=[tE[ck]])
            t_scan = pb.op("vector", "tensor_copy", f32.view(o_carry + ck, 1),
                           f32.view(o_C + ck * TT + TT - 1, 1))
        pb.op("scalar", "activation", f32.view(o_C, NCK * TT),
              f32.view(o_C, NCK * TT), AF.Ln, waits=[t_scan])
        tRall = pb.op("scalar", "activation", f32.view(o_C, NCK * TT),
                      f32.view(o_C, NCK * TT), AF.Exp, scale=-1.0)
        tR = [tRall] * NCK

        for hf in range(2):
            t_q = None
            t_el = None
            ctab_v = bf.view(o_ctab + hf * HTT, HTT)
            for ck in range(NCK):
                tt1 = pb.op("vector", "tensor_mul", bf.view(o_t1, HTT),
                            bf.view(o_qsil + ck * TT + hf * HTT, HTT),
                            f32.view(o_C + ck * TT + hf * HTT, HTT),
                            waits=[tR[ck], t_qkv_pe, tct]
                            + ([t_q] if t_q else []))
                for hh in range(2):
                    h = 2 * ck + hh
                    b0 = hh * 64
                    def ev_qd(bank, mmtok, h=h):
                        return pb.op("vector", "tensor_mul",
                                     bf.view(o_qR + h * HTT, HTT),
                                     bank[:, :HTT], ctab_v,
                                     waits=[mmtok, tct])
                    t_q = group(lambda bank, b0=b0: [
                        (bank[:, :HTT], bf.h[b0:b0 + 64, o_dup:o_dup + 128],
                         bf.h[b0:b0 + 64, o_t1:o_t1 + HTT], True, True)],
                        HTT, ev_qd, extra=[tt1, tw[6]])
                    def ev_ed(bank, mmtok, h=h):
                        return pb.op("vector", "tensor_mul",
                                     bf.view(o_ELr + h * HTT, HTT),
                                     bank[:, :HTT], ctab_v,
                                     waits=[mmtok, tct])
                    t_el = group(lambda bank, b0=b0, ck=ck: [
                        (bank[:, :HTT], bf.h[b0:b0 + 64, o_dup:o_dup + 128],
                         bf.h[b0:b0 + 64, o_E + ck * TT + hf * HTT:o_E + ck * TT + hf * HTT + HTT],
                         True, True)],
                        HTT, ev_ed, extra=[tE[ck], tw[6]])
            t_lrpe_prev = [t_el, t_q]
            for sl in range(2):
                for bb in range(2):
                    w0 = [t_el, t_q] + ([t_tr_last] if t_tr_last else [])
                    ttr = None
                    for k in range(8):
                        h = bb * 8 + k
                        ttr = pb.op("tensor", "transpose",
                                    PTR[:, k * 128:(k + 1) * 128],
                                    bf.view(o_ELr + h * HTT + sl * 128, 128),
                                    bf.view(ar["o_ident"], 128),
                                    waits=(w0 if k == 0 else ()), inc=(k == 7))
                    t_tr_last = pb.op("vector", "tensor_copy",
                                      bf.view(o_ELrtm + sl * 2048 + bb * 1024, 1024),
                                      PTR[:, 0:1024], waits=[ttr])
            t_eltm = t_tr_last
            tA2 = {}
            for c in range(2):
                for g4 in range(4):
                    def ev_a(bank, mmtok, c=c, g4=g4):
                        tA2[(c, g4)] = pb.op("vector", "tensor_mul",
                                             bf.view(o_Asb + (c * 4 + g4) * 512, 512),
                                             bf.view(o_tri, 512),
                                             bank[:, :512], waits=[mmtok, t_tri])
                        return tA2[(c, g4)]
                    group(lambda bank, c=c, g4=g4: [
                        (bank[:, q * 128:(q + 1) * 128],
                         bf.view(o_ELr + (g4 * 4 + q) * HTT + c * 128, 128),
                         bf.view(o_qR + (g4 * 4 + q) * HTT + c * 128, 128),
                         True, True)
                        for q in range(4)],
                        512, ev_a, extra=[t_el, t_q])
            for c in range(2):
                sg = hf * 2 + c
                tA = [tA2[(c, g4)] for g4 in range(4)]
                for hg in range(2):
                    def ev_o(bank, mmtok, hg=hg):
                        return pb.op("vector", "tensor_copy",
                                     bf.view(o_otm + sg * D + hg * 512, 512),
                                     bank[:, :512], waits=[mmtok])
                    def mk_parts(bank, hg=hg):
                        out = []
                        for h8 in range(8):
                            h = hg * 8 + h8
                            out.append((bank[:, h8 * 64:(h8 + 1) * 64],
                                        bf.view(o_Asb + c * 2048 + h * 128, 128),
                                        bf.view(o_vtm + sg * D + h * 64, 64),
                                        True, False))
                            out.append((bank[:, h8 * 64:(h8 + 1) * 64],
                                        bf.view(o_qR + h * HTT + c * 128, 128),
                                        bf.view(o_Ub + h * 64, 64),
                                        False, True))
                        return out
                    group(mk_parts, 512, ev_o,
                          extra=[tA[hg * 2], tA[hg * 2 + 1], t_Ubf])
                t_o_pe = pb.tok("tensor")
                for hg in range(2):
                    def ev_ds(bank, mmtok, hg=hg):
                        pb.op("vector", "tensor_add",
                              f32.view(o_U + hg * 512, 512),
                              f32.view(o_U + hg * 512, 512), bank[:, :512],
                              waits=[mmtok])
                        return pb.op("vector", "tensor_copy",
                                     bf.view(o_Ub + hg * 512, 512),
                                     f32.view(o_U + hg * 512, 512),
                                     waits=[t_o_pe])
                    t_Ubf = group(lambda bank, hg=hg: [
                        (bank[:, h8 * 64:(h8 + 1) * 64],
                         bf.view(o_ELrtm + c * 2048 + (hg * 8 + h8) * 128, 128),
                         bf.view(o_vtm + sg * D + (hg * 8 + h8) * 64, 64),
                         True, True)
                        for h8 in range(8)],
                        512, ev_ds, extra=[t_eltm, t_u])

        t_half2 = pb.tok("vector")
        tgate = None
        for s in range(NS):
            for hh in range(2):
                def ev_gt(bank, mmtok, s=s, hh=hh):
                    return pb.op("scalar", "activation",
                                 bf.view(o_gate + s * D + hh * 512, 512),
                                 bank[:, :512], AF.Sigmoid, waits=[mmtok])
                tgate = group(lambda bank, s=s, hh=hh: [
                    (bank[:, :512], bf.h[0:16, o_g1 + s * 128:o_g1 + s * 128 + 128],
                     bf.h[0:16, o_wg2 + hh * 512:o_wg2 + hh * 512 + 512],
                     True, True)],
                    512, ev_gt, extra=[t_half2, tw[5]])
        tog = None
        for s in range(NS):
            for hh in range(2):
                tog = pb.op("vector", "tensor_mul",
                            bf.view(o_otm + s * D + hh * 512, 512),
                            bf.view(o_otm + s * D + hh * 512, 512),
                            bf.view(o_gate + s * D + hh * 512, 512),
                            waits=[tgate])
        tsq = None
        for s in range(NS):
            tsq = pb.op("scalar", "activation", f32.view(lnk.o_sq, D),
                        bf.view(o_otm + s * D, D), AF.Square,
                        accum_out=f32.view(o_rms + s, 1), waits=[tog])
        pb.op("vector", "tensor_scalar_mul", f32.view(o_rms, 4),
              f32.view(o_rms, 4), 1.0 / D, waits=[tsq])
        tms = pb.tok("vector")
        pb.op("scalar", "activation", f32.view(o_rms + 4, 4),
              f32.view(o_rms, 4), AF.Ln, bias=1e-6, waits=[tms])
        trst = pb.op("scalar", "activation", f32.view(o_rms + 4, 4),
                     f32.view(o_rms + 4, 4), AF.Exp, scale=-0.5)
        ton = None
        for s in range(NS):
            ton = pb.op("vector", "tensor_single_scalar",
                        bf.view(o_otm + s * D, D), bf.view(o_otm + s * D, D),
                        f32.view(o_rms + 4 + s, 1), ALU.mult, waits=[trst])
        t_tr_last = transpose_tile(pb, ar, o_otm, o_onfm, [ton], t_tr_last)
        tnw_ap = None
        for ck in range(NCK):
            tnw_ap = pb.op("vector", "tensor_single_scalar",
                           bf.view(o_onfm + ck * TT, TT),
                           bf.view(o_onfm + ck * TT, TT),
                           f32.view(o_nwf + ck, 1), ALU.mult,
                           waits=[t_tr_last, t_nw])
        tout = None
        for s in range(NS):
            for hh in range(2):
                def ev_out(bank, mmtok, s=s, hh=hh):
                    return pb.op("vector", "tensor_copy",
                                 f32.view(o_out + s * D + hh * 512, 512),
                                 bank[:, :512], waits=[mmtok])
                tout = group(lambda bank, s=s, hh=hh: [
                    (bank[:, :512], bf.view(o_onfm + ck * TT + s * 128, 128),
                     bf.view(o_wo + ck * D + hh * 512, 512), ck == 0, ck == NCK - 1)
                    for ck in range(NCK)],
                    512, ev_out, extra=[tnw_ap, tw[3]])
        if i > 0:
            pb.wait_only("vector", [t_store[i - 1]])
        lnk.ln(lambda s: f32.view(o_out + s * D, D),
               lambda s: f32.view(o_xo + s * D, D), extra_waits=[tout])
        tres = None
        for s in range(NS):
            tres = pb.op("vector", "tensor_add", f32.view(o_xo + s * D, D),
                         f32.view(o_xo + s * D, D), f32.view(o_xt + s * D, D),
                         waits=[tl])
        t_store[i] = pb.dma("sync", tile_in(io["x_out"], i),
                            f32.view(o_xo, NS * D), waits=[tres], slot="l1a_xo")
    return t_store[-1]



PHASES = ("l0a", "l0f", "l1a", "l1f")


def build_program(phases=PHASES):
    nc = bass.Bass("TRN2", target_bir_lowering=False)
    dt = nc.dram_tensor
    io = {"ident": dt("ident", [128, 128], BF16, kind="ExternalInput").ap()}
    io["x_in"] = dt("x_in", [S, D], F32, kind="ExternalInput").ap()
    io["x_out"] = dt("x_out", [S, D], F32, kind="ExternalOutput").ap()
    n_mid = len(phases) - 1
    mids = [dt(f"mid{k}", [S, D], F32, kind="Internal").ap() for k in range(n_mid)]

    if "l0a" in phases:
        io["l0a"] = {
            "wq": dt("l0a_wq", [D, D], F32, kind="ExternalInput").ap(),
            "wk": dt("l0a_wk", [D, D], F32, kind="ExternalInput").ap(),
            "wvT": dt("l0a_wvT", [D, D], F32, kind="ExternalInput").ap(),
            "proj": dt("l0a_proj", [D, 256], F32, kind="ExternalInput").ap(),
        }
    if "l1a" in phases:
        io["l1a"] = {
            "wq": dt("l1a_wq", [D, D], F32, kind="ExternalInput").ap(),
            "wk": dt("l1a_wk", [D, D], F32, kind="ExternalInput").ap(),
            "wv": dt("l1a_wv", [D, D], F32, kind="ExternalInput").ap(),
            "wo": dt("l1a_wo", [D, D], F32, kind="ExternalInput").ap(),
            "wg1": dt("l1a_wg1", [D, 16], F32, kind="ExternalInput").ap(),
            "wg2": dt("l1a_wg2", [16, D], F32, kind="ExternalInput").ap(),
            "nw8": dt("l1a_nw8", [8, 128], F32, kind="ExternalInput").ap(),
            "costab": dt("l1a_costab", [128, S], F32, kind="ExternalInput").ap(),
            "tri4": dt("l1a_tri4", [128, 512], BF16, kind="ExternalInput").ap(),
            "dup": dt("l1a_dup", [128, 128], BF16, kind="ExternalInput").ap(),
        }
    if "l1f" in phases:
        io["l1f"] = {
            "wrT": dt("l1f_wrT", [D, D], F32, kind="ExternalInput").ap(),
            "wkT": dt("l1f_wkT", [D, D], F32, kind="ExternalInput").ap(),
            "wvT": dt("l1f_wvT", [D, D], F32, kind="ExternalInput").ap(),
            "mu_r8": dt("l1f_mur", [8, 128], F32, kind="ExternalInput").ap(),
            "mu_k8": dt("l1f_muk", [8, 128], F32, kind="ExternalInput").ap(),
        }
    if "l0f" in phases:
        io["l0f"] = {
            "gate_all": dt("l0f_gate", [9, D, 2048], F32, kind="ExternalInput").ap(),
            "up_all": dt("l0f_up", [9, D, 2048], F32, kind="ExternalInput").ap(),
            "down_all": dt("l0f_down", [9, 2048, D], F32, kind="ExternalInput").ap(),
            "gwT": dt("l0f_gwT", [D, 8], F32, kind="ExternalInput").ap(),
            "xnfm_h": dt("l0f_xnfm", [NCK, 128, S], BF16, kind="Internal").ap(),
            "xsfm_h": dt("l0f_xsfm", [NCK, 128, S], BF16, kind="Internal").ap(),
            "acc_h": dt("l0f_acc", [S, D], F32, kind="Internal").ap(),
        }

    pb = PB(nc)
    ar = alloc_common(nc, pb, io)
    emits = {"l0a": emit_l0a, "l0f": emit_l0f, "l1a": emit_l1a,
             "l1f": emit_l1f}
    t_last = None
    for k, ph in enumerate(phases):
        pio = dict(io[ph])
        pio["x_in"] = io["x_in"] if k == 0 else mids[k - 1]
        pio["x_out"] = io["x_out"] if k == len(phases) - 1 else mids[k]
        t_last = emits[ph](pb, pio, ar)
        pb.set_barrier()
    pb.wait_only("sync", [t_last])
    pb.replay()
    return nc


def prep_inputs(inputs, c, phases=PHASES):
    C = np.ascontiguousarray
    m = {"ident": np.eye(128, dtype=BF),
         "x_in": C(np.asarray(inputs["L0_attn_x"][c]))}
    if "l0a" in phases:
        m.update({
            "l0a_wq": C(np.asarray(inputs["L0_attn_Wq"])),
            "l0a_wk": C(np.asarray(inputs["L0_attn_Wk"])),
            "l0a_wvT": C(np.asarray(inputs["L0_attn_Wv"]).T),
            "l0a_proj": C(np.asarray(inputs["L0_attn_proj"])),
        })
    if "l1a" in phases:
        HD = 64
        theta = (np.arange(S, dtype=np.float64)[:, None]
                 * (10000.0 ** (-2.0 * np.arange(HD, dtype=np.float64) / HD))[None])
        ctab = np.empty((128, S), np.float32)
        ctab[0:64] = np.cos(theta).T
        ctab[64:128] = np.sin(theta).T
        tri = np.triu(np.ones((128, 128), np.float32))
        tri4 = np.tile(tri, (1, 4)).astype(BF)
        m.update({
            "l1a_wq": C(np.asarray(inputs["L1_attn_Wq"])),
            "l1a_wk": C(np.asarray(inputs["L1_attn_Wk"])),
            "l1a_wv": C(np.asarray(inputs["L1_attn_Wv"])),
            "l1a_wo": C(np.asarray(inputs["L1_attn_Wo"])),
            "l1a_wg1": C(np.asarray(inputs["L1_attn_Wg1"])),
            "l1a_wg2": C(np.asarray(inputs["L1_attn_Wg2"])),
            "l1a_nw8": C(np.asarray(inputs["L1_attn_norm_w"]).reshape(8, 128)),
            "l1a_costab": ctab,
            "l1a_tri4": C(tri4),
            "l1a_dup": C(np.equal(np.arange(128)[:, None] % 64,
                                  np.arange(128)[None, :] % 64).astype(BF)),
        })
    if "l1f" in phases:
        m.update({
            "l1f_wrT": C(np.asarray(inputs["L1_ffn_W_r"]).T),
            "l1f_wkT": C(np.asarray(inputs["L1_ffn_W_k"]).T),
            "l1f_wvT": C(np.asarray(inputs["L1_ffn_W_v"]).T),
            "l1f_mur": C(np.asarray(inputs["L1_ffn_mu_r"]).reshape(8, 128)),
            "l1f_muk": C(np.asarray(inputs["L1_ffn_mu_k"]).reshape(8, 128)),
        })
    if "l0f" in phases:
        sg = np.asarray(inputs["L0_ffn_shared_gate"])
        su = np.asarray(inputs["L0_ffn_shared_up"])
        sd = np.asarray(inputs["L0_ffn_shared_down"])
        rg = np.asarray(inputs["L0_ffn_r_gate"])
        ru = np.asarray(inputs["L0_ffn_r_up"])
        rd = np.asarray(inputs["L0_ffn_r_down"])
        m.update({
            "l0f_gate": C(np.concatenate([sg.T[None], rg.transpose(0, 2, 1)])),
            "l0f_up": C(np.concatenate([su.T[None], ru.transpose(0, 2, 1)])),
            "l0f_down": C(np.concatenate([sd.T[None], rd.transpose(0, 2, 1)])),
            "l0f_gwT": C(np.asarray(inputs["L0_ffn_gate_w"]).T),
        })
    return m


def kernel(**inputs):
    phases = PHASES
    nc = build_program(phases)
    in_maps = [prep_inputs(inputs, c, phases) for c in range(8)]
    res = run_bass_kernel_spmd(nc, in_maps, core_ids=list(range(8)))
    return np.stack([np.asarray(r["x_out"], dtype=np.float32) for r in res.results])
